# version 11
# baseline (speedup 1.0000x reference)
import numpy as np
import ml_dtypes
import concourse.bass as bass
import concourse.mybir as mybir
from concourse.bass_utils import run_bass_kernel_spmd
from concourse.bass_core_rust import IndirectOffsetOnAxis

F32 = mybir.dt.float32
BF16 = mybir.dt.bfloat16
FP16 = mybir.dt.float16
FP8 = mybir.dt.float8e4
I32 = mybir.dt.int32
AF = mybir.ActivationFunctionType
ALU = mybir.AluOpType
AX = mybir.AxisListType

NCORE = 8
TOK = 2048
HALO = 128
TOKH = TOK + 2 * HALO
NT = TOKH // 128
NTO = TOK // 128
D = 1024
DCH = D // 128
HFF = 4096
HCH = HFF // 128
NH = 16
HD = 64
E = 8
CAP = 2048
HMOE = 2048
LN_EPS = 1e-5
NTOTAL = NCORE * TOK

ENGINES = ("sync", "scalar", "tensor", "vector", "gpsimd")
AUTO_SERIAL = {"vector", "scalar", "gpsimd"}


class Prog:

    def __init__(self, nc):
        self.nc = nc
        self.items = {e: [] for e in ENGINES}
        self.sems = {}
        self.cnt = {}
        self.hw = {e: {} for e in ENGINES}

    def _sem(self, name):
        if name not in self.sems:
            self.sems[name] = self.nc.alloc_semaphore(name)
            self.cnt[name] = 0
        return self.sems[name]

    def wait(self, eng, tok):
        if tok is None:
            return
        name, val = tok
        if val <= self.hw[eng].get(name, 0):
            return
        self.hw[eng][name] = val
        self.items[eng].append(("w", self._sem(name), val))

    def op(self, eng, fn, waits=(), dma=False, slot=0):
        sem_name = eng + (f"_d{slot}" if dma else "")
        if dma and self.cnt.get(sem_name, 0) > 0:
            self.wait(eng, (sem_name, self.cnt[sem_name]))
        if eng in AUTO_SERIAL:
            if not dma and self.cnt.get(sem_name, 0) > 0:
                self.wait(eng, (sem_name, self.cnt[sem_name]))
            if dma and self.cnt.get(eng, 0) > 0:
                self.wait(eng, (eng, self.cnt[eng]))
        for t in waits:
            self.wait(eng, t)
        h = self._sem(sem_name)
        amt = 16 if dma else 1
        self.cnt[sem_name] = self.cnt.get(sem_name, 0) + amt
        self.items[eng].append(("e", fn, h, amt))
        return (sem_name, self.cnt[sem_name])

    def dma(self, eng, out, in_, waits=(), slot=0, **kw):
        return self.op(eng, lambda g: g.dma_start(out, in_, **kw), waits=waits,
                       dma=True, slot=slot)

    def heads(self):
        return [(n, c) for n, c in self.cnt.items() if c > 0]

    def finalize(self):
        with self.nc.Block() as block:
            for e in ENGINES:
                items = self.items[e]
                if not items:
                    continue

                def runner(eng, items=items):
                    for it in items:
                        if it[0] == "w":
                            eng.wait_ge(it[1], it[2])
                        else:
                            it[1](eng).then_inc(it[2], it[3])

                getattr(block, e)(runner)


class Mem:

    def __init__(self, nc):
        self.nc = nc
        self.base = (nc.sbuf_base + 31) // 32 * 32
        self.top = nc.sbuf_top

    def at(self, name, off, shape, dt=F32):
        end = off + int(np.prod(shape[1:])) * mybir.dt.size(dt)
        assert self.base + end <= self.top, (name, off, end, self.top - self.base)
        return self.nc.alloc_sbuf_tensor_at(name, list(shape), dt,
                                            offset=self.base + off)


def _finish_early(nc, p, outs, srcs):
    x3_out, xn_out, sc_out = outs
    xsb, znb_g, scev = srcs
    x3o2 = x3_out.rearrange("(t p) d -> p t d", p=128)
    hh = p.heads()
    for t in range(1, NT - 1):
        p.dma("sync", x3o2[:, t - 1:t, :], xsb[:, t * D:(t + 1) * D],
              waits=(hh if t == 1 else []))
    ends = p.heads()
    for e in ENGINES:
        for tok in ends:
            p.wait(e, tok)
    p.finalize()
    return nc


def emit_ln(p, sc, tile_in, tile_out, deps=(), eps=LN_EPS):
    p.op("vector", lambda v: v.bn_stats(sc["bn12"][:, 0:6], tile_in[:, 0:512]),
         waits=deps)
    p.op("vector", lambda v: v.bn_stats(sc["bn12"][:, 6:12], tile_in[:, 512:1024]))
    t3 = p.op("vector", lambda v: v.bn_aggr(sc["st2"][:], sc["bn12"][:]))
    p.op("scalar", lambda s: s.activation(sc["rstd"][:], sc["st2"][:, 1:2],
                                          AF.Ln, bias=sc["eps"]), waits=[t3])
    t5 = p.op("scalar", lambda s: s.activation(sc["rstd"][:], sc["rstd"][:],
                                               AF.Exp, scale=-0.5))
    return p.op("vector", lambda v: v.tensor_scalar(tile_out[:], tile_in[:],
                                                    sc["st2"][:, 0:1],
                                                    sc["rstd"][:],
                                                    ALU.subtract, ALU.mult),
                waits=[t5])



def build_l0(phases=99, dbg=False, dbg_g=0, dbg_hp=0, dbg_lq=0):
    nc = bass.Bass("TRN2", target_bir_lowering=False)

    inp = lambda n, sh, dt=F32: nc.dram_tensor(n, sh, dt, kind="ExternalInput").ap()
    out_ = lambda n, sh, dt=F32: nc.dram_tensor(n, sh, dt, kind="ExternalOutput").ap()

    xh = inp("xh", [TOKH, D])
    w1l = inp("w1l", [128, DCH * HFF])
    b1l = inp("b1l", [128, HCH])
    w2l = inp("w2l", [128, HCH * D])
    b2 = inp("b2", [1, D])
    wqkvl = inp("wqkvl", [128, DCH * 3 * D])
    woutl = inp("woutl", [128, DCH * D])
    gaml = inp("gaml", [128, DCH])
    betl = inp("betl", [128, DCH])
    wgl = inp("wgl", [128, DCH * E])
    ident = inp("ident", [128, 128])
    mprev = inp("mprev", [128, 128])
    mnext = inp("mnext", [128, 128])
    mfirst = inp("mfirst", [128, 128])
    mlast = inp("mlast", [128, 128])

    x3_out = out_("x3_out", [TOK, D])
    xn_out = out_("xn_out", [TOK, D], BF16)
    sc_out = out_("sc_out", [E, TOK])
    x3o = x3_out.rearrange("(t p) d -> p t d", p=128)
    xno = xn_out.rearrange("(t p) d -> p t d", p=128)
    m = Mem(nc)
    xsb = m.at("xsb", 0, [128, NT * D])
    wA = m.at("wA", 73728, [128, 32768], BF16)
    R2, R3, R4 = 139264, 176128, 204288
    wB8 = m.at("wB8", R2, [128, 32768], FP8)
    znT6 = m.at("znT6", R2, [128, DCH * 768], BF16)
    Kf = m.at("Kf", R2 + 12288, [128, 8 * 768], BF16)
    Vt = m.at("Vt", R2 + 24576, [128, 6 * 1024], BF16)
    stag = m.at("stag", R3, [128, 3 * D])
    b2row = m.at("b2row", R3 + 12288, [1, D])
    b2q = m.at("b2q", R3 + 16384, [1, D], FP16)
    hq8 = m.at("hq8", R3, [128, HCH * 512], FP8)
    znTf = m.at("znTf", R3 + 16384, [128, DCH * 512], BF16)
    znb_f = m.at("znb_f", R2 + 32768, [128, D], BF16)
    fout = m.at("fout", R2 + 34816, [128, D], FP16)
    Qf = m.at("Qf", R3, [128, 8 * 256], BF16)
    atf = m.at("atf", R3 + 4096, [128, 2048], BF16)
    Pt = m.at("Pt", R3 + 8192, [128, 4 * 384], BF16)
    zn_a = m.at("zn_a", R3 + 11264, [128, D])
    att = m.at("att", R3 + 11264, [128, 2048], BF16)
    znb_a = m.at("znb_a", R3 + 15360, [128, D], BF16)
    vbrow = m.at("vbrow", R3 + 17408, [1, D], BF16)
    qkrow = m.at("qkrow", R3 + 19456, [1, 2048], BF16)
    onesw = m.at("onesw", R3 + 23552, [1, 512], BF16)
    rcl4 = m.at("rcl4", R3 + 24576, [128, 4])
    znbg2 = m.at("znbg2", R3, [128, D], BF16)
    znTg2 = m.at("znTg2", R3 + 2048, [128, DCH * 128], BF16)
    znb_g = m.at("znb_g", R3 + 4096, [128, D], BF16)
    znTg = m.at("znTg", R3 + 6144, [128, DCH * 128], BF16)
    scev = m.at("scev", R3 + 8192, [8, 512])
    b1s = m.at("b1s", R4, [128, HCH])
    b2tile = m.at("b2tile", R4 + 128, [128, D])
    gams = m.at("gams", R4 + 4224, [128, DCH])
    bets = m.at("bets", R4 + 4256, [128, DCH])
    idn = m.at("idn", R4 + 4288, [128, 128], BF16)
    masks = m.at("masks", R4 + 4544, [128, 4 * 128], BF16)
    wgs = m.at("wgs", R4 + 5568, [128, DCH * E], BF16)
    ones512 = m.at("ones512", R4 + 5696, [1, 512])
    ones1 = m.at("ones1", R4 + 7744, [1, 128])
    bn12 = m.at("bn12", R4 + 8256, [128, 12])
    st2 = m.at("st2", R4 + 8320, [128, 2])
    rstd = m.at("rstd", R4 + 8352, [128, 1])
    nmr = m.at("nmr", R4 + 8384, [128, 1])
    rcl = m.at("rcl", R4 + 8416, [128, 1])
    von = m.at("von", R4 + 8448, [128, 1], BF16)
    epsc = m.at("epsc", R4 + 8480, [128, 1])
    sc = {"bn12": bn12, "st2": st2, "rstd": rstd, "nmr": nmr,
          "eps": epsc[:, 0:1]}

    pt = nc.alloc_psum_tensor("pt", [128, 128], BF16)
    pa = nc.alloc_psum_tensor("pa", [128, 512], F32)
    pb = nc.alloc_psum_tensor("pb", [128, 512], F32)
    pc = nc.alloc_psum_tensor("pc", [128, 512], F32)
    pd = nc.alloc_psum_tensor("pd", [128, 512], F32)
    pe_ = nc.alloc_psum_tensor("pe", [128, 512], F32)
    pf = nc.alloc_psum_tensor("pf", [128, 512], F32)
    pg = nc.alloc_psum_tensor("pg", [128, 512], F32)

    p = Prog(nc)

    p.dma("sync", b1s[:], b1l)
    p.dma("sync", gams[:], gaml)
    p.dma("sync", bets[:], betl)
    p.dma("gpsimd", idn[:], ident)
    p.dma("gpsimd", masks[:, 0:128], mprev)
    p.dma("gpsimd", masks[:, 128:256], mnext)
    p.dma("gpsimd", masks[:, 256:384], mfirst)
    p.dma("gpsimd", masks[:, 384:512], mlast)
    p.dma("gpsimd", wgs[:], wgl)
    tb2 = p.dma("sync", b2row[:], b2)
    p.dma("sync", xsb[:].rearrange("p (t d) -> p t d", d=D),
          xh.rearrange("(t p) d -> p t d", p=128))
    p.dma("gpsimd", wA[:], w1l)
    p.dma("gpsimd", wB8[:], w2l)
    p.op("vector", lambda v: v.memset(ones512[:], 1.0))
    p.op("vector", lambda v: v.memset(epsc[:], LN_EPS))
    p.op("vector", lambda v: v.memset(ones1[:], 1.0))
    p.op("vector", lambda v: v.memset(von[:], 1.0))
    tq = p.op("vector", lambda v: v.tensor_copy(b2q[:], b2row[:]), waits=[tb2])
    tq2 = p.op("vector", lambda v: v.tensor_copy(b2row[:], b2q[:]), waits=[tq])
    tbc = None
    for n in range(2):
        ta = p.op("tensor", lambda t, n=n: t.matmul(
            pg[:, 0:512], ones1[:], b2row[:, n * 512:(n + 1) * 512],
            start=True, stop=True), waits=[tq2, tbc])
        tbc = p.op("vector", lambda v, n=n: v.tensor_copy(
            b2tile[:, n * 512:(n + 1) * 512], pg[:, 0:512]), waits=[ta])
    setup_done = p.heads()
    if phases < 1:
        return _finish_early(nc, p, [x3_out, xn_out, sc_out], [xsb, znb_g, scev])

    def emit_ffn(t_lo, t_hi, start_deps):
        prev_mm1_evac = {0: None, 1: None}
        prev_mm2_add = {0: None, 1: None}
        hq_guard = None
        znT_guard = None
        deps0 = list(start_deps)
        tiles = list(range(t_lo, t_hi))
        for si in range(0, len(tiles), 4):
            slab = tiles[si:si + 4]
            nslab = len(slab) * 128
            tz = None
            for k, t in enumerate(slab):
                xt = xsb[:, t * D:(t + 1) * D]
                tln = emit_ln(p, sc, xt, znb_f[:],
                              deps=deps0 + [znT_guard, tz])
                deps0 = []
                for c in range(DCH):
                    ttr = p.op("tensor", lambda tt, c=c: tt.transpose(
                        pt[:], znb_f[:, c * 128:(c + 1) * 128], idn[:]),
                        waits=[tln, tz])
                    tz = p.op("vector", lambda v, c=c, k=k: v.tensor_copy(
                        znTf[:, c * 512 + k * 128: c * 512 + k * 128 + 128],
                        pt[:]), waits=[ttr])
            last_evac = None
            for hc in range(HCH):
                ps = [pa, pb][hc % 2]
                w0 = [tz, prev_mm1_evac[hc % 2], hq_guard]
                tm = None
                for dc in range(DCH):
                    tm = p.op("tensor", lambda t_, hc=hc, dc=dc, ps=ps, nslab=nslab: t_.matmul(
                        ps[:, 0:nslab],
                        wA[:, dc * HFF + hc * 128: dc * HFF + hc * 128 + 128],
                        znTf[:, dc * 512: dc * 512 + nslab],
                        start=(dc == 0), stop=(dc == DCH - 1)),
                        waits=(w0 if dc == 0 else []))
                tev = p.op("scalar", lambda s, hc=hc, ps=ps, nslab=nslab: s.activation(
                    hq8[:, hc * 512: hc * 512 + nslab], ps[:, 0:nslab],
                    AF.Relu, bias=b1s[:, hc:hc + 1]), waits=[tm])
                prev_mm1_evac[hc % 2] = tev
                last_evac = tev
            znT_guard = last_evac
            for k, t in enumerate(slab):
                pspair = [pc, pd] if k % 2 == 0 else [pe_, pf]
                for n in range(2):
                    ps2 = pspair[n]
                    tm = None
                    for hc in range(HCH):
                        tm = p.op("tensor", lambda t_, hc=hc, n=n, k=k, ps2=ps2: t_.matmul(
                            ps2[:, 0:512],
                            hq8[:, hc * 512 + k * 128: hc * 512 + k * 128 + 128],
                            wB8[:, hc * D + n * 512: hc * D + n * 512 + 512],
                            start=(hc == 0), stop=(hc == HCH - 1)),
                            waits=([last_evac, prev_mm2_add[k % 2]]
                                   if hc == 0 else []))
                    tq_ = p.op("vector", lambda v, n=n, ps2=ps2: v.tensor_tensor(
                        fout[:, n * 512:(n + 1) * 512], ps2[:, 0:512],
                        b2tile[:, n * 512:(n + 1) * 512], ALU.add), waits=[tm])
                    tadd = p.op("vector", lambda v, n=n, t=t: v.scalar_tensor_tensor(
                        xsb[:, t * D + n * 512: t * D + n * 512 + 512],
                        fout[:, n * 512:(n + 1) * 512], 0.5,
                        xsb[:, t * D + n * 512: t * D + n * 512 + 512],
                        ALU.mult, ALU.add), waits=[tq_])
                    prev_mm2_add[k % 2] = tadd
                hq_guard = prev_mm2_add[k % 2]
        return p.heads()

    ffn1_done = emit_ffn(0, NT, setup_done)
    if phases < 2:
        return _finish_early(nc, p, [x3_out, xn_out, sc_out], [xsb, znb_g, scev])

    qb_ps = [pa, pb, pc, pd, pe_, pf]
    tqb = [None] * 6
    tfold = None
    for dc in range(DCH):
        tst = p.dma("sync", stag[:], wqkvl[:, dc * 3 * D:(dc + 1) * 3 * D],
                    waits=(ffn1_done if dc == 0 else [tfold]))
        for n in range(6):
            tqb[n] = p.op("tensor", lambda t, n=n, dc=dc: t.matmul(
                qb_ps[n][0:1, 0:512], bets[:, dc:dc + 1],
                stag[:, n * 512:(n + 1) * 512],
                start=(dc == 0), stop=(dc == DCH - 1)), waits=[tst])
        tfold = p.op("vector", lambda v, dc=dc: v.tensor_scalar(
            wA[:, dc * 3 * D:(dc + 1) * 3 * D], stag[:],
            gams[:, dc:dc + 1], None, ALU.mult), waits=[tst, tqb[5]])
    tqk = None
    for n in range(4):
        tqk = p.op("vector", lambda v, n=n: v.tensor_copy(
            qkrow[0:1, n * 512:(n + 1) * 512], qb_ps[n][0:1, 0:512]),
            waits=[tqb[n], tqk])
    tvb = p.op("vector", lambda v: v.tensor_copy(vbrow[0:1, 0:512],
                                                 pe_[0:1, 0:512]),
               waits=[tqb[4], tqk])
    tvb = p.op("vector", lambda v: v.tensor_copy(vbrow[0:1, 512:1024],
                                                 pf[0:1, 0:512]),
               waits=[tqb[5], tvb])
    tqbc = tqk
    twout = p.dma("gpsimd", wA[:, 24576:32768], woutl, waits=ffn1_done)
    tone = p.op("vector", lambda v: v.memset(onesw[:], 1.0), waits=ffn1_done)
    if phases < 3:
        return _finish_early(nc, p, [x3_out, xn_out, sc_out], [xsb, znb_g, scev])

    prev_op_t = {}
    tevac_reg = [None, None]
    texp_reg = [None, None]
    trc_prev = None
    tact_reg = [None, None]
    tVreg = {0: None, 1: None}
    tpn = None
    ptch = [None]
    setup_tok = [tfold, tqbc, tvb, twout, tone]
    tprojV = {}
    attn_first = True
    for q0 in range(1, NTO, 2):
        newtiles = list(range(q0 + 1, q0 + 3)) if q0 > 1 else [0, 1, 2, 3]
        for t in newtiles:
            xt = xsb[:, t * D:(t + 1) * D]
            t1 = emit_ln(p, sc, xt, zn_a[:],
                         deps=(setup_tok if attn_first else []) + [tpn])
            attn_first = False
            t2 = emit_ln(p, sc, zn_a[:], znb_a[:], deps=[t1])
            for c in range(DCH):
                ttr = p.op("tensor", lambda tt, c=c: tt.transpose(
                    pt[:], znb_a[:, c * 128:(c + 1) * 128], idn[:]),
                    waits=[t2, ptch[0]])
                tpn = p.op("vector", lambda v, c=c, t=t: v.tensor_copy(
                    znT6[:, c * 768 + (t % 6) * 128: c * 768 + (t % 6) * 128 + 128],
                    pt[:]), waits=[ttr])
                ptch[0] = tpn
            if 1 <= t <= NT - 2:
                tpn = p.op("vector", lambda v, xt=xt: v.tensor_add(
                    xt, xt, zn_a[:]), waits=[t2, tpn])
        runs = []
        for t in newtiles:
            if runs and runs[-1][0] + len(runs[-1][1]) == t % 6:
                runs[-1][1].append(t)
            else:
                runs.append([t % 6, [t]])
        qruns = []
        for q in (q0, q0 + 1):
            if qruns and qruns[-1][0] + len(qruns[-1][1]) == q % 6:
                qruns[-1][1].append(q)
            else:
                qruns.append([q % 6, [q]])
        pi = 0
        for hp in range(NH // 2):
            for s0, tl in runs:
                W = 128 * len(tl)
                psK = [pc, pd][pi % 2]
                tm = None
                for dc in range(DCH):
                    tm = p.op("tensor", lambda t_, dc=dc, hp=hp, s0=s0, W=W, psK=psK: t_.matmul(
                        psK[:, 0:W],
                        wA[:, dc * 3 * D + D + hp * 128: dc * 3 * D + D + hp * 128 + 128],
                        znT6[:, dc * 768 + s0 * 128: dc * 768 + s0 * 128 + W],
                        start=(dc == 0), stop=False),
                        waits=([tpn, tevac_reg[pi % 2], texp_reg[pi % 2],
                                prev_op_t.get(id(psK))] if dc == 0 else []))
                tm = p.op("tensor", lambda t_, hp=hp, W=W, psK=psK: t_.matmul(
                    psK[:, 0:W], qkrow[0:1, (8 + hp) * 128: (9 + hp) * 128],
                    onesw[0:1, 0:W], start=False, stop=True), waits=[tqbc, tone])
                tevac_reg[pi % 2] = p.op("vector", lambda v, hp=hp, s0=s0, W=W, psK=psK: v.tensor_copy(
                    Kf[:, hp * 768 + s0 * 128: hp * 768 + s0 * 128 + W],
                    psK[:, 0:W]), waits=[tm])
                pi += 1
            for s0, ql in qruns:
                W = 128 * len(ql)
                psQ = [pc, pd][pi % 2]
                tm = None
                for dc in range(DCH):
                    tm = p.op("tensor", lambda t_, dc=dc, hp=hp, s0=s0, W=W, psQ=psQ: t_.matmul(
                        psQ[:, 0:W],
                        wA[:, dc * 3 * D + hp * 128: dc * 3 * D + hp * 128 + 128],
                        znT6[:, dc * 768 + s0 * 128: dc * 768 + s0 * 128 + W],
                        start=(dc == 0), stop=False),
                        waits=([tpn, tevac_reg[pi % 2], texp_reg[pi % 2],
                                prev_op_t.get(id(psQ))] if dc == 0 else []))
                tm = p.op("tensor", lambda t_, hp=hp, W=W, psQ=psQ: t_.matmul(
                    psQ[:, 0:W], qkrow[0:1, hp * 128: (hp + 1) * 128],
                    onesw[0:1, 0:W], start=False, stop=True), waits=[tqbc, tone])
                tevac_reg[pi % 2] = p.op("vector", lambda v, hp=hp, q0=q0, ql=ql, W=W, psQ=psQ: v.tensor_copy(
                    Qf[:, hp * 256 + (ql[0] - q0) * 128: hp * 256 + (ql[0] - q0) * 128 + W],
                    psQ[:, 0:W]), waits=[tm])
                pi += 1
        for t in newtiles:
            for n in range(2):
                psV = [pe_, pf][n]
                tm = None
                for dc in range(DCH):
                    tm = p.op("tensor", lambda t_, dc=dc, n=n, t=t, psV=psV: t_.matmul(
                        psV[:, 0:512],
                        znT6[:, dc * 768 + (t % 6) * 128: dc * 768 + (t % 6) * 128 + 128],
                        wA[:, dc * 3 * D + 2 * D + n * 512: dc * 3 * D + 2 * D + n * 512 + 512],
                        start=(dc == 0), stop=False),
                        waits=([tpn, prev_op_t.get(id(psV)), tVreg[n]]
                               if dc == 0 else []))
                tm = p.op("tensor", lambda t_, n=n, psV=psV: t_.matmul(
                    psV[:, 0:512], onesw[0:1, 0:128],
                    vbrow[0:1, n * 512:(n + 1) * 512],
                    start=False, stop=True), waits=[tvb, tone])
                tVreg[n] = p.op("vector", lambda v, n=n, t=t, psV=psV: v.tensor_copy(
                    Vt[:, (t % 6) * 1024 + n * 512: (t % 6) * 1024 + n * 512 + 512],
                    psV[:, 0:512]), waits=[tm, tprojV.get(t % 6)])
                tprojV[t % 6] = tVreg[n]
        oi = 0
        tact_last = None
        for hp in range(NH // 2):
            for hh in range(2):
                h = 2 * hp + hh
                pbase = hh * 64
                tPt = {}
                for ki, k in enumerate(range(q0 - 1, q0 + 3)):
                    qlo = max(k - 1, q0)
                    qhi = min(k + 1, q0 + 1)
                    W = (qhi - qlo + 1) * 128
                    psS = [pc, pd][ki % 2]
                    tsc = p.op("tensor", lambda t_, hp=hp, k=k, qlo=qlo, W=W, psS=psS, pbase=pbase, q0=q0: t_.matmul(
                        psS[:, 0:W],
                        Kf[pbase:pbase + 64, hp * 768 + (k % 6) * 128: hp * 768 + (k % 6) * 128 + 128],
                        Qf[pbase:pbase + 64, hp * 256 + (qlo - q0) * 128: hp * 256 + (qlo - q0) * 128 + W],
                        start=True, stop=True),
                        waits=[texp_reg[ki % 2], tevac_reg[ki % 2],
                               tevac_reg[1 - ki % 2]])
                    toff = (qlo - k + 1) * 128
                    texp = p.op("scalar", lambda s, ki=ki, toff=toff, W=W, psS=psS: s.activation(
                        Pt[:, ki * 384 + toff: ki * 384 + toff + W],
                        psS[:, 0:W], AF.Exp, scale=0.125), waits=[tsc])
                    texp_reg[ki % 2] = texp
                    tblk = texp
                    for q in range(qlo, qhi + 1):
                        if q == k:
                            continue
                        if q == k + 1:
                            mi = 2 if k == 0 else 0
                        else:
                            mi = 3 if k == NT - 1 else 1
                        tblk = p.op("vector", lambda v, ki=ki, q=q, k=k, mi=mi: v.tensor_tensor(
                            Pt[:, ki * 384 + (q - k + 1) * 128: ki * 384 + (q - k + 1) * 128 + 128],
                            Pt[:, ki * 384 + (q - k + 1) * 128: ki * 384 + (q - k + 1) * 128 + 128],
                            masks[:, mi * 128:(mi + 1) * 128], ALU.mult),
                            waits=[texp, tblk])
                    tPt[ki] = tblk
                for qi, q in enumerate((q0, q0 + 1)):
                    reg = hh * 2 + qi
                    tlm = None
                    for j, kk in enumerate(range(q - 1, q + 2)):
                        ki = kk - (q0 - 1)
                        tlm = p.op("tensor", lambda t_, ki=ki, q=q, kk=kk, j=j: t_.matmul(
                            pa[:, 0:1],
                            Pt[:, ki * 384 + (q - kk + 1) * 128: ki * 384 + (q - kk + 1) * 128 + 128],
                            von[:],
                            start=(j == 0), stop=(j == 2)),
                            waits=([tPt[ki], trc_prev] if j == 0 else [tPt[ki]]))
                    trc_prev = p.op("vector", lambda v, reg=reg: v.reciprocal(
                        rcl4[:, reg:reg + 1], pa[:, 0:1]), waits=[tlm])
                    trc = trc_prev
                    psO = [pb, pg][oi % 2]
                    tav = None
                    for j, kk in enumerate(range(q - 1, q + 2)):
                        ki = kk - (q0 - 1)
                        tav = p.op("tensor", lambda t_, ki=ki, q=q, kk=kk, j=j, h=h, psO=psO: t_.matmul(
                            psO[:, 0:64],
                            Pt[:, ki * 384 + (q - kk + 1) * 128: ki * 384 + (q - kk + 1) * 128 + 128],
                            Vt[:, (kk % 6) * 1024 + h * 64: (kk % 6) * 1024 + h * 64 + 64],
                            start=(j == 0), stop=(j == 2)),
                            waits=([tPt[ki], tprojV[kk % 6], tact_reg[oi % 2]]
                                   if j == 0 else [tPt[ki]]))
                    tact_last = p.op("scalar", lambda s, reg=reg, qi=qi, h=h, psO=psO: s.activation(
                        att[:, qi * 1024 + h * 64: qi * 1024 + h * 64 + 64],
                        psO[:, 0:64], AF.Copy,
                        scale=rcl4[:, reg:reg + 1]), waits=[tav, trc])
                    tact_reg[oi % 2] = tact_last
                    oi += 1
        tfq = {}
        for qi, q in enumerate((q0, q0 + 1)):
            tfq[qi] = []
            for c in range(DCH):
                ttr = p.op("tensor", lambda t_, qi=qi, c=c: t_.transpose(
                    pt[:], att[:, qi * 1024 + c * 128: qi * 1024 + c * 128 + 128],
                    idn[:]), waits=[tact_last, ptch[0]])
                tev = p.op("vector", lambda v, qi=qi, c=c: v.tensor_copy(
                    atf[:, qi * 1024 + c * 128: qi * 1024 + c * 128 + 128],
                    pt[:]), waits=[ttr])
                ptch[0] = tev
                tfq[qi].append(tev)
        for qi, q in enumerate((q0, q0 + 1)):
            pop = [pc, pd] if qi == 0 else [pe_, pf]
            for n in range(2):
                tm = None
                for hp in range(NH // 2):
                    tm = p.op("tensor", lambda t_, hp=hp, n=n, qi=qi, pop=pop: t_.matmul(
                        pop[n][:, 0:512],
                        atf[:, qi * 1024 + hp * 128: qi * 1024 + hp * 128 + 128],
                        wA[:, 24576 + hp * D + n * 512: 24576 + hp * D + n * 512 + 512],
                        start=(hp == 0), stop=(hp == NH // 2 - 1)),
                        waits=[tfq[qi][hp]] + ([prev_op_t.get(id(pop[n])),
                                                texp_reg[0], texp_reg[1]]
                                               if hp == 0 else []))
                prev_op_t[id(pop[n])] = p.op("vector", lambda v, n=n, q=q, pop=pop: v.tensor_tensor(
                    xsb[:, q * D + n * 512: q * D + n * 512 + 512],
                    pop[n][:, 0:512],
                    xsb[:, q * D + n * 512: q * D + n * 512 + 512],
                    ALU.add), waits=[tm])
    attn_done = p.heads()
    if phases < 5:
        return _finish_early(nc, p, [x3_out, xn_out, sc_out], [xsb, znb_g, scev])

    tw1b = p.dma("gpsimd", wA[:], w1l, waits=attn_done)
    tw2b = p.dma("gpsimd", wB8[:], w2l)
    ffn2_done = emit_ffn(1, NT - 1, [tw1b, tw2b])
    if phases < 6:
        return _finish_early(nc, p, [x3_out, xn_out, sc_out], [xsb, znb_g, scev])

    tge = {0: None, 1: None}
    tztg = None
    tscdma = None
    txndma = {0: None, 1: None}
    for t in range(1, NT - 1):
        b = (t - 1) % 2
        znbg = [znb_g, znbg2][b]
        zntg = [znTg, znTg2][b]
        psg = [pa, pg][b]
        p.dma("sync", x3o[:, t - 1:t, :], xt_ := xsb[:, t * D:(t + 1) * D],
              waits=(ffn2_done if t == 1 else []), slot=2 + b)
        tln = emit_ln(p, sc, xt_, znbg[:],
                      deps=(ffn2_done if t == 1 else [txndma[b]]))
        txndma[b] = p.dma("sync", xno[:, t - 1:t, :], znbg[:], waits=[tln],
                          slot=b)
        for c in range(DCH):
            ttr = p.op("tensor", lambda tt, c=c, znbg=znbg: tt.transpose(
                pt[:], znbg[:, c * 128:(c + 1) * 128], idn[:]),
                waits=[tln, tztg])
            tztg = p.op("vector", lambda v, c=c, zntg=zntg: v.tensor_copy(
                zntg[:, c * 128:(c + 1) * 128], pt[:]), waits=[ttr])
        tg = None
        for dc in range(DCH):
            tg = p.op("tensor", lambda t_, dc=dc, psg=psg, zntg=zntg: t_.matmul(
                psg[0:8, 0:128], wgs[:, dc * E:(dc + 1) * E],
                zntg[:, dc * 128:(dc + 1) * 128],
                start=(dc == 0), stop=(dc == DCH - 1)),
                waits=([tztg, tge[b]] if dc == 0 else []))
        tge[b] = p.op("vector", lambda v, t=t, psg=psg: v.tensor_copy(
            scev[0:8, ((t - 1) % 4) * 128: ((t - 1) % 4 + 1) * 128],
            psg[0:8, 0:128]), waits=[tg] + ([tscdma] if (t - 1) % 4 <= 1 else []))
        if (t - 1) % 4 == 3:
            tscdma = p.dma("sync", sc_out[:, (t - 4) * 128: t * 128],
                           scev[0:8, :], waits=[tge[0], tge[1]], slot=2)
    ends = p.heads()
    for e in ENGINES:
        for tok in ends:
            p.wait(e, tok)
    p.finalize()
    return nc


def _fix_unused():
    pass



def _sbuf_chunks(a, nch):
    F = a.shape[1]
    return np.ascontiguousarray(
        a.reshape(nch, 128, F).transpose(1, 0, 2).reshape(128, nch * F))


def _l0_host_inputs(inputs):
    w1T = np.ascontiguousarray(np.asarray(inputs["L0_ffn_w1"], np.float32).T)
    w2T = np.ascontiguousarray(np.asarray(inputs["L0_ffn_w2"], np.float32).T)
    wqkvT = np.ascontiguousarray(np.asarray(inputs["L0_attn_w_qkv"], np.float32).T)
    woutT = np.ascontiguousarray(np.asarray(inputs["L0_attn_w_out"], np.float32).T)
    wgT = np.ascontiguousarray(np.asarray(inputs["L1_ffn_Wg"], np.float32).T)
    com = dict(
        w1l=_sbuf_chunks(w1T, DCH),
        b1l=np.ascontiguousarray(
            np.asarray(inputs["L0_ffn_b1"], np.float32).reshape(HCH, 128).T),
        w2l=_sbuf_chunks(w2T, HCH),
        b2=np.asarray(inputs["L0_ffn_b2"], np.float32).reshape(1, D),
        wqkvl=_sbuf_chunks(wqkvT, DCH),
        woutl=_sbuf_chunks(woutT, DCH),
        gaml=np.ascontiguousarray(
            np.asarray(inputs["L0_attn_gamma"], np.float32).reshape(DCH, 128).T),
        betl=np.ascontiguousarray(
            np.asarray(inputs["L0_attn_beta"], np.float32).reshape(DCH, 128).T),
        wgl=_sbuf_chunks(wgT, DCH),
        ident=np.eye(128, dtype=np.float32),
    )
    kk, qq = np.meshgrid(np.arange(128), np.arange(128), indexing="ij")
    mprev = (qq <= kk).astype(np.float32)
    mnext = (qq >= kk).astype(np.float32)
    zeros = np.zeros((128, 128), np.float32)
    x = np.ascontiguousarray(np.asarray(inputs["L0_attn_x"], np.float32)
                             .reshape(NTOTAL, D))
    maps = []
    for j in range(NCORE):
        lo = j * TOK - HALO
        hi = j * TOK + TOK + HALO
        xhj = np.zeros((TOKH, D), np.float32)
        s0 = max(lo, (j // 2) * 2 * TOK)
        s1 = min(hi, (j // 2 + 1) * 2 * TOK)
        xhj[s0 - lo: s1 - lo] = x[s0:s1]
        mp = dict(com)
        mp["xh"] = xhj
        mp["mprev"] = mprev
        mp["mnext"] = mnext
        mp["mfirst"] = zeros if j % 2 == 0 else mprev
        mp["mlast"] = zeros if j % 2 == 1 else mnext
        maps.append(mp)
    return maps


def run_l0(inputs, phases=99):
    nc = build_l0(phases)
    maps = _l0_host_inputs(inputs)
    res = run_bass_kernel_spmd(nc, maps, core_ids=list(range(NCORE)))
    x3 = np.concatenate([np.asarray(r["x3_out"]) for r in res.results], 0)
    xn = np.concatenate([np.asarray(r["xn_out"]) for r in res.results], 0)
    scs = np.concatenate([np.asarray(r["sc_out"]) for r in res.results], 1)
    return x3, xn, scs


def kernel(**inputs):
    raise NotImplementedError("under construction")


NBIS = 34
BIGR = 60000.0
HCH_M = HMOE // 128


def build_moe():
    nc = bass.Bass("TRN2", target_bir_lowering=False)
    inp = lambda n, sh, dt=F32: nc.dram_tensor(n, sh, dt, kind="ExternalInput").ap()
    out_ = lambda n, sh, dt=F32: nc.dram_tensor(n, sh, dt, kind="ExternalOutput").ap()

    scfl = inp("scfl", [128, 1024])
    xnf = inp("xnf", [NTOTAL, D], BF16)
    w1e = inp("w1e", [128, DCH * HMOE])
    w3e = inp("w3e", [128, DCH * HMOE])
    w2e = inp("w2e", [128, HCH_M * D])
    grpm = inp("grpm", [128, 8])
    gbt = inp("gbt", [8, 128])
    l16 = inp("l16", [128, 128])
    bneg = inp("bneg", [2, 128])
    bpos = inp("bpos", [2, 128])
    rowsel = inp("rowsel", [128, 1], I32)
    ident = inp("ident", [128, 128])

    y_out = out_("y_out", [CAP, D])
    rkw_out = out_("rkw_out", [2, 128, 1024])
    yo = y_out.rearrange("(t p) d -> p t d", p=128)

    rank_dram = nc.dram_tensor("rank_dram", [1024, 128], F32, kind="Internal").ap()
    xg_dram = nc.dram_tensor("xg_dram", [CAP, D], BF16, kind="Internal").ap()

    m = Mem(nc)
    Off = [0]
    def alloc(name, shape, dt=F32):
        Off[0] = (Off[0] + 31) // 32 * 32
        t = m.at(name, Off[0], shape, dt)
        Off[0] += int(np.prod(shape[1:])) * mybir.dt.size(dt)
        return t

    scs = alloc("scs", [128, 1024])
    w1s = alloc("w1s", [128, DCH * HMOE], BF16)
    w3s = alloc("w3s", [128, DCH * HMOE], BF16)
    w2s = alloc("w2s", [128, HCH_M * D], BF16)
    grs = alloc("grs", [128, 8])
    gbs = alloc("gbs", [8, 128])
    l16s = alloc("l16s", [128, 128])
    bns = alloc("bns", [2, 128])
    bps = alloc("bps", [2, 128])
    idn = alloc("idn", [128, 128], BF16)
    idf = alloc("idf", [128, 128])
    rsel = alloc("rsel", [128, 1], I32)
    lo = alloc("lo", [128, 1])
    hi = alloc("hi", [128, 1])
    mid = alloc("mid", [128, 1])
    cnt = alloc("cnt", [128, 1])
    cnt8 = alloc("cnt8", [8, 1])
    selm = alloc("selm", [128, 1], I32)
    selm2 = alloc("selm2", [128, 1], I32)
    mask = alloc("mask", [128, 1024])
    ztile = alloc("ztile", [128, 1024])
    cums = alloc("cums", [128, 1024])
    rank = alloc("rank", [128, 1024])
    wts = alloc("wts", [128, 1024])
    exm = alloc("exm", [128, 1024])
    tot = alloc("tot", [128, 1])
    offs = alloc("offs", [128, 1])
    nth = alloc("nth", [128, 1])
    rs1 = alloc("rs1", [128, 1])
    mnmx = alloc("mnmx", [128, 2])
    rb2 = alloc("rb2", [2, 128])
    gmx = alloc("gmx", [2, 1])
    rrs2 = alloc("rrs2", [128, 128])
    rrs = alloc("rrs", [128, 128])
    rri = alloc("rri", [128, 128], I32)
    idxcs = [alloc(f"idxc{r}", [128, 1], I32) for r in range(8)]
    xts = [alloc(f"xt{r}", [128, D], BF16) for r in range(8)]
    xgr = alloc("xgr", [128, D], BF16)
    xgt = alloc("xgt", [128, DCH * 512], BF16)
    ha = alloc("ha", [128, 512], BF16)
    hb = alloc("hb", [128, 512], BF16)
    hq = alloc("hq", [128, HCH_M * 512], BF16)
    ev = alloc("ev", [128, D])

    ptf = nc.alloc_psum_tensor("ptf", [128, 128], F32)
    ptb = nc.alloc_psum_tensor("ptb", [128, 128], BF16)
    pa = nc.alloc_psum_tensor("pa", [128, 512], F32)
    pb = nc.alloc_psum_tensor("pb", [128, 512], F32)
    pc = nc.alloc_psum_tensor("pc", [128, 512], F32)
    pd = nc.alloc_psum_tensor("pd", [128, 512], F32)
    pe_ = nc.alloc_psum_tensor("pe", [128, 512], F32)
    pf = nc.alloc_psum_tensor("pf", [128, 512], F32)

    p = Prog(nc)

    bcreg = nc.alloc_register(mybir.EngineType.Pool, "bcap")
    tbc0 = p.op("gpsimd", lambda g: g.reg_mov(bcreg, CAP - 1))

    tsc = p.dma("sync", scs[:], scfl)
    p.dma("sync", grs[:], grpm)
    p.dma("sync", gbs[:], gbt)
    p.dma("sync", l16s[:], l16)
    p.dma("sync", bns[:], bneg)
    p.dma("sync", bps[:], bpos)
    p.dma("sync", rsel[:], rowsel)
    tid = p.dma("sync", idf[:], ident)
    p.dma("gpsimd", idn[:], ident)
    p.dma("gpsimd", w1s[:], w1e)
    p.dma("gpsimd", w3s[:], w3e)
    tw = p.dma("gpsimd", w2s[:], w2e)
    tz = p.op("vector", lambda v: v.memset(ztile[:], 0.0))

    p.op("vector", lambda v: v.tensor_reduce(mnmx[:, 0:1], scs[:], axis=AX.X,
                                             op=ALU.min, negate=True), waits=[tsc])
    t2 = p.op("vector", lambda v: v.reduce_max(mnmx[:, 1:2], scs[:], axis=AX.X))
    ttr = p.op("tensor", lambda t: t.transpose(ptf[0:2, 0:128], mnmx[:], idf[:]),
               waits=[t2, tid])
    t3 = p.op("vector", lambda v: v.tensor_copy(rb2[:], ptf[0:2, 0:128]),
              waits=[ttr])
    t4 = p.op("vector", lambda v: v.reduce_max(gmx[:], rb2[:], axis=AX.X))
    tb1 = p.op("tensor", lambda t: t.matmul(pa[:, 0:1], bns[:], gmx[:],
                                            start=True, stop=True), waits=[t4])
    tb2 = p.op("tensor", lambda t: t.matmul(pb[:, 0:1], bps[:], gmx[:],
                                            start=True, stop=True))
    tl = p.op("vector", lambda v: v.tensor_scalar(lo[:], pa[:, 0:1], 1.0, None,
                                                  ALU.subtract), waits=[tb1])
    th = p.op("vector", lambda v: v.tensor_scalar(hi[:], pb[:, 0:1], 1.0, None,
                                                  ALU.add), waits=[tb2])

    tcn8 = None
    tbc = th
    for it in range(NBIS):
        p.op("vector", lambda v: v.tensor_add(mid[:], lo[:], hi[:]),
             waits=[tl, th, tbc])
        p.op("vector", lambda v: v.tensor_scalar(mid[:], mid[:], 0.5, None,
                                                 ALU.mult))
        tcm = p.op("vector", lambda v: v.tensor_scalar(mask[:], scs[:], mid[:],
                                                       None, ALU.is_ge, ALU.add,
                                                       accum_out=cnt[:]))
        tg = p.op("tensor", lambda t: t.matmul(pa[0:8, 0:1], grs[:], cnt[:],
                                               start=True, stop=True),
                  waits=[tcm, tcn8])
        tcn8 = p.op("vector", lambda v: v.tensor_copy(cnt8[:], pa[0:8, 0:1]),
                    waits=[tg])
        tg2 = p.op("tensor", lambda t: t.matmul(pb[:, 0:1], gbs[:], cnt8[:],
                                                start=True, stop=True),
                   waits=[tcn8, tbc])
        tsl = p.op("vector", lambda v: v.tensor_scalar(selm[:], pb[:, 0:1],
                                                       float(CAP), None,
                                                       ALU.is_ge), waits=[tg2])
        tsl2 = p.op("vector", lambda v: v.tensor_scalar(selm2[:], pb[:, 0:1],
                                                        float(CAP), None,
                                                        ALU.is_lt))
        tl = p.op("vector", lambda v: v.copy_predicated(lo[:], selm[:], mid[:]))
        th = p.op("vector", lambda v: v.copy_predicated(hi[:], selm2[:], mid[:]))
        tbc = tsl2

    tmask = p.op("vector", lambda v: v.tensor_scalar(mask[:], scs[:], lo[:],
                                                     None, ALU.is_ge, ALU.add,
                                                     accum_out=tot[:]),
                 waits=[tl])
    tnth = p.op("vector", lambda v: v.tensor_scalar(nth[:], lo[:], -1.0, None,
                                                    ALU.mult))
    tex = p.op("scalar", lambda s: s.activation(exm[:], scs[:], AF.Exp,
                                                bias=nth[:]), waits=[tnth])
    tex2 = p.op("vector", lambda v: v.tensor_mul(exm[:], exm[:], mask[:]),
                waits=[tex, tmask])
    trs = p.op("vector", lambda v: v.reduce_sum(rs1[:], exm[:], axis=AX.X))
    tg = p.op("tensor", lambda t: t.matmul(pa[0:8, 0:1], grs[:], rs1[:],
                                           start=True, stop=True),
              waits=[trs, tcn8])
    tcn8 = p.op("vector", lambda v: v.tensor_copy(cnt8[:], pa[0:8, 0:1]),
                waits=[tg])
    tg2 = p.op("tensor", lambda t: t.matmul(pb[:, 0:1], gbs[:], cnt8[:],
                                            start=True, stop=True),
               waits=[tcn8, tsl])
    trc = p.op("vector", lambda v: v.reciprocal(rs1[:], pb[:, 0:1]), waits=[tg2])
    twt = p.op("vector", lambda v: v.tensor_scalar(wts[:], exm[:], rs1[:], None,
                                                   ALU.mult), waits=[trc])
    tcu = p.op("vector", lambda v: v.tensor_tensor_scan(cums[:], mask[:],
                                                        ztile[:], 0.0,
                                                        ALU.add, ALU.add),
               waits=[tz])
    tof = p.op("tensor", lambda t: t.matmul(pa[:, 0:1], l16s[:], tot[:],
                                            start=True, stop=True),
               waits=[tmask, tg2])
    tofe = p.op("vector", lambda v: v.tensor_copy(offs[:], pa[:, 0:1]),
                waits=[tof])
    tr1 = p.op("vector", lambda v: v.tensor_sub(rank[:], cums[:], mask[:]),
               waits=[tcu])
    tr2 = p.op("vector", lambda v: v.tensor_scalar(rank[:], rank[:], offs[:],
                                                   None, ALU.add), waits=[tofe])
    tr3 = p.op("vector", lambda v: v.tensor_scalar(exm[:], mask[:], -BIGR, BIGR,
                                                   ALU.mult, ALU.add))
    tr4 = p.op("vector", lambda v: v.tensor_add(rank[:], rank[:], exm[:]),
               waits=[tr3])
    p.op("vector", lambda v: v.tensor_scalar(cums[:], rank[:], -1.0,
                                             float(CAP), ALU.mult, ALU.add),
         waits=[tr4])
    p.op("vector", lambda v: v.tensor_scalar_max(cums[:], cums[:], 0.0))
    tq3 = p.op("vector", lambda v: v.tensor_scalar_min(cums[:], cums[:], 1.0))
    twt = p.op("vector", lambda v: v.tensor_tensor(wts[:], wts[:], cums[:],
                                                   ALU.mult), waits=[twt, tq3])
    p.dma("sync", rkw_out[0:1, :, :], rank[:], waits=[tr4])
    p.dma("sync", rkw_out[1:2, :, :], wts[:], waits=[twt])
    trd = p.dma("sync", rank_dram.rearrange("(a b) f -> a (b f)", a=128),
                rank[:], waits=[tr4])
    trg = p.op("gpsimd", lambda g: g.indirect_dma_start(
        rrs2[:], None, rank_dram, IndirectOffsetOnAxis(rsel[:], 0)),
        waits=[trd], dma=True)
    ttr2 = p.op("tensor", lambda t: t.transpose(ptf[:], rrs2[:], idf[:]),
                waits=[trg, ttr])
    trr = p.op("vector", lambda v: v.tensor_copy(rrs[:], ptf[:]), waits=[ttr2])
    tri = p.op("vector", lambda v: v.tensor_copy(rri[:], rrs[:]))

    tprev = [None] * 8
    tscat = None
    for T in range(NTOTAL // 128):
        r = T % 8
        xt = xts[r]
        idxc = idxcs[r]
        tld = p.dma("sync", xt[:], xnf[T * 128:(T + 1) * 128, :],
                    waits=[tprev[r]], slot=r)
        tcx = p.op("vector", lambda v, T=T, idxc=idxc: v.tensor_copy(
            idxc[:], rri[:, T:T + 1]), waits=[tri, tprev[r]])
        tscat = p.op("gpsimd", lambda g, xt=xt, idxc=idxc: g.indirect_dma_start(
            xg_dram, IndirectOffsetOnAxis(idxc[:], 0), xt[:], None,
            bounds_check=bcreg, oob_is_err=False),
            waits=[tld, tcx, tbc0], dma=True, slot=r)
        tprev[r] = tscat

    tldp = None
    tnext = [t for t in tprev if t is not None]
    prev_ha = None
    prev_hb = None
    prev_ev = {0: None, 1: None}
    hq_guard = None
    xgt_guard = None
    for si in range(CAP // 512):
        tz2 = None
        for k in range(4):
            tile = si * 4 + k
            tldp = p.dma("sync", xgr[:], xg_dram[tile * 128:(tile + 1) * 128, :],
                         waits=tnext + [xgt_guard, tz2], slot=1)
            tnext = []
            for c in range(DCH):
                ttp = p.op("tensor", lambda t, c=c: t.transpose(
                    ptb[:], xgr[:, c * 128:(c + 1) * 128], idn[:]),
                    waits=[tldp, tz2])
                tz2 = p.op("vector", lambda v, c=c, k=k: v.tensor_copy(
                    xgt[:, c * 512 + k * 128: c * 512 + k * 128 + 128], ptb[:]),
                    waits=[ttp])
        thq = None
        for hc in range(HCH_M):
            tma = None
            for dc in range(DCH):
                tma = p.op("tensor", lambda t, hc=hc, dc=dc: t.matmul(
                    pa[:, 0:512],
                    w1s[:, dc * HMOE + hc * 128: dc * HMOE + hc * 128 + 128],
                    xgt[:, dc * 512: dc * 512 + 512],
                    start=(dc == 0), stop=(dc == DCH - 1)),
                    waits=([tz2, tw, prev_ha, hq_guard] if dc == 0 else []))
            tha = p.op("scalar", lambda s: s.activation(ha[:], pa[:, 0:512],
                                                        AF.Silu),
                       waits=[tma, thq])
            prev_ha = tha
            tmb = None
            for dc in range(DCH):
                tmb = p.op("tensor", lambda t, hc=hc, dc=dc: t.matmul(
                    pb[:, 0:512],
                    w3s[:, dc * HMOE + hc * 128: dc * HMOE + hc * 128 + 128],
                    xgt[:, dc * 512: dc * 512 + 512],
                    start=(dc == 0), stop=(dc == DCH - 1)),
                    waits=([prev_hb] if dc == 0 else []))
            thb = p.op("vector", lambda v: v.tensor_copy(hb[:], pb[:, 0:512]),
                       waits=[tmb, thq])
            prev_hb = thb
            thq = p.op("vector", lambda v, hc=hc: v.tensor_tensor(
                hq[:, hc * 512:(hc + 1) * 512], ha[:], hb[:], ALU.mult),
                waits=[tha, thb])
        for k in range(4):
            tile = si * 4 + k
            pspair = [pc, pd] if k % 2 == 0 else [pe_, pf]
            tevs = []
            for n in range(2):
                ps2 = pspair[n]
                tm2 = None
                for hc in range(HCH_M):
                    tm2 = p.op("tensor", lambda t, hc=hc, n=n, k=k, ps2=ps2: t.matmul(
                        ps2[:, 0:512],
                        hq[:, hc * 512 + k * 128: hc * 512 + k * 128 + 128],
                        w2s[:, hc * D + n * 512: hc * D + n * 512 + 512],
                        start=(hc == 0), stop=(hc == HCH_M - 1)),
                        waits=([thq, prev_ev[k % 2]] if hc == 0 else []))
                tev = p.op("vector", lambda v, n=n, ps2=ps2: v.tensor_copy(
                    ev[:, n * 512:(n + 1) * 512], ps2[:, 0:512]),
                    waits=[tm2, prev_ev[1 - k % 2]])
                tevs.append(tev)
            tdy = p.dma("sync", yo[:, tile:tile + 1, :], ev[:], waits=tevs,
                        slot=2 + k % 2)
            prev_ev[k % 2] = tdy
            if k == 3:
                hq_guard = tm2
        xgt_guard = thq

    ends = p.heads()
    for e in ENGINES:
        for tok in ends:
            p.wait(e, tok)
    p.finalize()
    return nc


def _moe_host_inputs(inputs, xn_full, scores_full, layer):
    W1 = np.asarray(inputs["L1_ffn_W1"], np.float32)
    W2 = np.asarray(inputs["L1_ffn_W2"], np.float32)
    W3 = np.asarray(inputs["L1_ffn_W3"], np.float32)
    kk = np.arange(128)
    grpm = np.zeros((128, 8), np.float32)
    for e_ in range(8):
        grpm[e_ * 16:(e_ + 1) * 16, e_] = 1.0
    gbt = np.zeros((8, 128), np.float32)
    for e_ in range(8):
        gbt[e_, e_ * 16:(e_ + 1) * 16] = 1.0
    l16 = ((kk[:, None] // 16 == kk[None, :] // 16) &
           (kk[:, None] < kk[None, :])).astype(np.float32)
    bneg = np.zeros((2, 128), np.float32); bneg[0, :] = -1.0
    bpos = np.zeros((2, 128), np.float32); bpos[1, :] = 1.0
    com = dict(
        scfl=np.ascontiguousarray(scores_full.reshape(128, 1024)),
        xnf=xn_full,
        grpm=grpm, gbt=gbt, l16=l16, bneg=bneg, bpos=bpos,
        ident=np.eye(128, dtype=np.float32),
    )
    maps = []
    for j in range(NCORE):
        mp = dict(com)
        mp["w1e"] = _sbuf_chunks(np.ascontiguousarray(W1[j].T), DCH)
        mp["w3e"] = _sbuf_chunks(np.ascontiguousarray(W3[j].T), DCH)
        mp["w2e"] = _sbuf_chunks(np.ascontiguousarray(W2[j].T), HCH_M)
        mp["rowsel"] = (j * 128 + np.arange(128, dtype=np.int32))[:, None]
        maps.append(mp)
    return maps


def emit_combine(p, nc, xsb, y1f, rki, wls, bufs, idxcs, hwc, bcreg, deps,
                 tile_major=False):
    R = len(bufs)
    tz = [p.op("vector", lambda v, r=r: v.memset(bufs[r][:], 0.0), waits=deps)
          for r in range(R)]
    tadd = None
    tadd_hist = [None] * R
    tgat_hist = [None] * R
    tadd_tile = [None] * NTO
    for i in range(E * NTO):
        if tile_major:
            T, e_ = divmod(i, E)
        else:
            e_, T = divmod(i, NTO)
        r = i % R
        tcx = p.op("gpsimd", lambda g, e_=e_, T=T, r=r: g.tensor_copy(
            idxcs[r][:], rki[:, e_ * 16 + T: e_ * 16 + T + 1]),
            waits=deps + [tgat_hist[r]])
        tgat = p.op("gpsimd", lambda g, e_=e_, r=r: g.indirect_dma_start(
            bufs[r][:], None, y1f, IndirectOffsetOnAxis(idxcs[r][:], 0),
            element_offset=e_ * CAP * D, bounds_check=bcreg,
            oob_is_err=False), waits=[tcx, tz[r], tadd_hist[r]],
            dma=True, slot=r)
        tgat_hist[r] = tgat
        thw = p.op("vector", lambda v, e_=e_, T=T: v.tensor_scalar(
            hwc[:], wls[:, e_ * 16 + T: e_ * 16 + T + 1], 0.5, None,
            ALU.mult), waits=[tgat])
        tadd = p.op("vector", lambda v, T=T, r=r: v.scalar_tensor_tensor(
            xsb[:, T * D:(T + 1) * D], bufs[r][:], hwc[:],
            xsb[:, T * D:(T + 1) * D], ALU.mult, ALU.add), waits=[thw])
        tadd_hist[r] = tadd
        tadd_tile[T] = tadd
    return tadd, tadd_tile


def _combine_host(rkw, j):
    rk = rkw[0].reshape(E, NTOTAL)[:, j * TOK:(j + 1) * TOK]
    wv = rkw[1].reshape(E, NTOTAL)[:, j * TOK:(j + 1) * TOK]
    rkl = np.ascontiguousarray(
        rk.reshape(E, NTO, 128).transpose(2, 0, 1).reshape(128, 128))
    wl = np.ascontiguousarray(
        wv.reshape(E, NTO, 128).transpose(2, 0, 1).reshape(128, 128))
    return rkl.astype(np.float32), wl.astype(np.float32)



def build_att1a():
    nc = bass.Bass("TRN2", target_bir_lowering=False)
    inp = lambda n, sh, dt=F32: nc.dram_tensor(n, sh, dt, kind="ExternalInput").ap()
    out_ = lambda n, sh, dt=F32: nc.dram_tensor(n, sh, dt, kind="ExternalOutput").ap()

    x3 = inp("x3", [TOK, D])
    y1f = inp("y1f", [NTOTAL, D])
    rkl = inp("rkl", [128, 128])
    wl = inp("wl", [128, 128])
    wkl = inp("wkl", [128, DCH * D])
    wvl = inp("wvl", [128, DCH * D])
    bk = inp("bk", [1, D])
    bv = inp("bv", [1, D])
    ident = inp("ident", [128, 128])

    x4_out = out_("x4_out", [TOK, D])
    st_out = out_("st_out", [1025, D])
    x4o = x4_out.rearrange("(t p) d -> p t d", p=128)

    m = Mem(nc)
    Off = [0]
    def alloc(name, shape, dt=F32):
        Off[0] = (Off[0] + 31) // 32 * 32
        t = m.at(name, Off[0], shape, dt)
        Off[0] += int(np.prod(shape[1:])) * mybir.dt.size(dt)
        return t

    xsb = alloc("xsb", [128, NTO * D])
    kps = alloc("kps", [128, NTO * D], BF16)
    vps = alloc("vps", [128, NTO * D], BF16)
    wks = alloc("wks", [128, DCH * D], BF16)
    wvs = alloc("wvs", [128, DCH * D], BF16)
    rks = alloc("rks", [128, 128])
    rki = alloc("rki", [128, 128], I32)
    wls = alloc("wls", [128, 128])
    bufs = [alloc(f"buf{r}", [128, D]) for r in range(6)]
    idxcs = [alloc(f"idxc{r}", [128, 1], I32) for r in range(6)]
    hwc = alloc("hwc", [128, 1])
    znb = alloc("znb", [128, D], BF16)
    znt = alloc("znt", [128, DCH * 128], BF16)
    idn = alloc("idn", [128, 128], BF16)
    bks = alloc("bks", [1, D])
    bvs = alloc("bvs", [1, D])
    ones1 = alloc("ones1", [1, 128])
    onesc = alloc("onesc", [128, 1], BF16)
    phi1 = alloc("phi1", [128, 512])
    phi2 = alloc("phi2", [128, 512])
    sev = alloc("sev", [128, 512])
    bn12 = alloc("bn12", [128, 12])
    st2 = alloc("st2", [128, 2])
    rstd = alloc("rstd", [128, 1])
    epsc = alloc("epsc", [128, 1])
    sc = {"bn12": bn12, "st2": st2, "rstd": rstd, "nmr": None, "eps": epsc[:, 0:1]}

    pt = nc.alloc_psum_tensor("pt", [128, 128], BF16)
    pa = nc.alloc_psum_tensor("pa", [128, 512], F32)
    pb = nc.alloc_psum_tensor("pb", [128, 512], F32)
    pc = nc.alloc_psum_tensor("pc", [128, 512], F32)
    pd = nc.alloc_psum_tensor("pd", [128, 512], F32)

    p = Prog(nc)
    bcreg = nc.alloc_register(mybir.EngineType.Pool, "bcap")
    tb0 = p.op("gpsimd", lambda g: g.reg_mov(bcreg, CAP - 1))
    tx = p.dma("sync", xsb[:].rearrange("p (t d) -> p t d", d=D),
               x3.rearrange("(t p) d -> p t d", p=128))
    trk = p.dma("sync", rks[:], rkl)
    twl = p.dma("sync", wls[:], wl)
    tbk = p.dma("sync", bks[:], bk)
    tbv = p.dma("sync", bvs[:], bv)
    p.dma("gpsimd", idn[:], ident)
    p.dma("gpsimd", wks[:], wkl)
    tw = p.dma("gpsimd", wvs[:], wvl)
    p.op("vector", lambda v: v.memset(ones1[:], 1.0))
    p.op("vector", lambda v: v.memset(onesc[:], 1.0))
    p.op("vector", lambda v: v.memset(epsc[:], LN_EPS))
    tri = p.op("vector", lambda v: v.tensor_copy(rki[:], rks[:]), waits=[trk])

    tcomb, ttile = emit_combine(p, nc, xsb, y1f, rki, wls, bufs, idxcs, hwc,
                                bcreg, [tx, tri, twl, tb0], tile_major=True)

    tdx = None
    for T in range(NTO):
        tdx = p.dma("sync", x4o[:, T:T + 1, :], xsb[:, T * D:(T + 1) * D],
                    waits=[ttile[T]], slot=T % 4)

    tzt = None
    tphi = None
    for T in range(NTO):
        tln = emit_ln(p, sc, xsb[:, T * D:(T + 1) * D], znb[:],
                      deps=[ttile[T], tphi])
        for c in range(DCH):
            ttr = p.op("tensor", lambda t, c=c: t.transpose(
                pt[:], znb[:, c * 128:(c + 1) * 128], idn[:]), waits=[tln, tzt])
            tzt = p.op("vector", lambda v, c=c: v.tensor_copy(
                znt[:, c * 128:(c + 1) * 128], pt[:]), waits=[ttr])
        for n in range(2):
            psK = [pa, pb][n]
            tmk = None
            for dc in range(DCH):
                tmk = p.op("tensor", lambda t, dc=dc, n=n, psK=psK: t.matmul(
                    psK[:, 0:512], znt[:, dc * 128:(dc + 1) * 128],
                    wks[:, dc * D + n * 512: dc * D + n * 512 + 512],
                    start=(dc == 0), stop=False),
                    waits=([tzt, tw, tphi] if dc == 0 else []))
            tmk = p.op("tensor", lambda t, n=n, psK=psK: t.matmul(
                psK[:, 0:512], ones1[:], bks[:, n * 512:(n + 1) * 512],
                start=False, stop=True), waits=[tbk])
            t1 = p.op("vector", lambda v, psK=psK: v.tensor_scalar_min(
                phi1[:], psK[:, 0:512], 0.0), waits=[tmk])
            t2 = p.op("scalar", lambda s: s.activation(phi1[:], phi1[:], AF.Exp),
                      waits=[t1])
            t3 = p.op("vector", lambda v, psK=psK: v.tensor_scalar_max(
                phi2[:], psK[:, 0:512], 0.0), waits=[t2])
            tphi = p.op("vector", lambda v, T=T, n=n: v.tensor_tensor(
                kps[:, T * D + n * 512: T * D + n * 512 + 512], phi1[:],
                phi2[:], ALU.add), waits=[t3])
        for n in range(2):
            psV = [pc, pd][n]
            tmv = None
            for dc in range(DCH):
                tmv = p.op("tensor", lambda t, dc=dc, n=n, psV=psV: t.matmul(
                    psV[:, 0:512], znt[:, dc * 128:(dc + 1) * 128],
                    wvs[:, dc * D + n * 512: dc * D + n * 512 + 512],
                    start=(dc == 0), stop=False),
                    waits=([tzt, tbv, tphi] if dc == 0 else []))
            tmv = p.op("tensor", lambda t, n=n, psV=psV: t.matmul(
                psV[:, 0:512], ones1[:], bvs[:, n * 512:(n + 1) * 512],
                start=False, stop=True), waits=[tbv])
            tphi = p.op("vector", lambda v, T=T, n=n, psV=psV: v.tensor_copy(
                vps[:, T * D + n * 512: T * D + n * 512 + 512], psV[:, 0:512]),
                waits=[tmv])

    kv_done = tphi
    tse = None
    for dc in range(DCH):
        for n in range(2):
            ps_ = [pa, pb][(dc * 2 + n) % 2]
            tms = None
            for T in range(NTO):
                tms = p.op("tensor", lambda t, dc=dc, n=n, T=T, ps_=ps_: t.matmul(
                    ps_[:, 0:512],
                    kps[:, T * D + dc * 128: T * D + dc * 128 + 128],
                    vps[:, T * D + n * 512: T * D + n * 512 + 512],
                    start=(T == 0), stop=(T == NTO - 1)),
                    waits=([kv_done, tse] if T == 0 else []))
            tev = p.op("vector", lambda v, ps_=ps_: v.tensor_copy(
                sev[:], ps_[:, 0:512]), waits=[tms])
            tse = p.dma("sync", st_out[dc * 128:(dc + 1) * 128,
                                       n * 512:(n + 1) * 512], sev[:],
                        waits=[tev])
    for n in range(2):
        tmz = None
        for T in range(NTO):
            tmz = p.op("tensor", lambda t, n=n, T=T: t.matmul(
                pc[0:1, 0:512], onesc[:],
                kps[:, T * D + n * 512: T * D + n * 512 + 512],
                start=(T == 0), stop=(T == NTO - 1)),
                waits=([kv_done, tse] if T == 0 else []))
        tev = p.op("vector", lambda v: v.tensor_copy(sev[0:1, 0:512],
                                                     pc[0:1, 0:512]),
                   waits=[tmz])
        tse = p.dma("sync", st_out[1024:1025, n * 512:(n + 1) * 512],
                    sev[0:1, 0:512], waits=[tev])

    ends = p.heads()
    for e in ENGINES:
        for tok in ends:
            p.wait(e, tok)
    p.finalize()
    return nc


def build_att1b():
    nc = bass.Bass("TRN2", target_bir_lowering=False)
    inp = lambda n, sh, dt=F32: nc.dram_tensor(n, sh, dt, kind="ExternalInput").ap()
    out_ = lambda n, sh, dt=F32: nc.dram_tensor(n, sh, dt, kind="ExternalOutput").ap()

    x4 = inp("x4", [TOK, D])
    stp = inp("stp", [1025, D])
    wql = inp("wql", [128, DCH * D])
    wkl = inp("wkl", [128, DCH * D])
    wvl = inp("wvl", [128, DCH * D])
    wol = inp("wol", [128, DCH * D])
    bq = inp("bq", [1, D])
    bk = inp("bk", [1, D])
    bv = inp("bv", [1, D])
    bo = inp("bo", [1, D])
    ue = inp("ue", [128, 128])
    wgl = inp("wgl", [128, DCH * E])
    ident = inp("ident", [128, 128])

    x5_out = out_("x5_out", [TOK, D])
    xn2_out = out_("xn2_out", [TOK, D], BF16)
    sc2_out = out_("sc2_out", [E, TOK])
    x5o = x5_out.rearrange("(t p) d -> p t d", p=128)
    xno = xn2_out.rearrange("(t p) d -> p t d", p=128)

    m = Mem(nc)
    Off = [0]
    def alloc(name, shape, dt=F32):
        Off[0] = (Off[0] + 31) // 32 * 32
        t = m.at(name, Off[0], shape, dt)
        Off[0] += int(np.prod(shape[1:])) * mybir.dt.size(dt)
        return t

    xtl = [alloc(f"xtl{i}", [128, D]) for i in range(2)]
    Sf = alloc("Sf", [128, DCH * D])
    Sb = alloc("Sb", [128, DCH * D], BF16)
    wqs = alloc("wqs", [128, DCH * D], BF16)
    wks = alloc("wks", [128, DCH * D], BF16)
    wvs = alloc("wvs", [128, DCH * D], BF16)
    wos = alloc("wos", [128, DCH * D], BF16)
    wgs = alloc("wgs", [128, DCH * E], BF16)
    Zr = alloc("Zr", [1, D])
    ueb = alloc("ueb", [128, 128], BF16)
    idn = alloc("idn", [128, 128], BF16)
    znt2 = [alloc(f"znt{i}", [128, DCH * 128], BF16) for i in range(2)]
    znb2 = [alloc(f"znb{i}", [128, D], BF16) for i in range(2)]
    qpt2 = [alloc(f"qpt{i}", [128, D], BF16) for i in range(2)]
    qptT2 = [alloc(f"qptT{i}", [128, DCH * 128], BF16) for i in range(2)]
    kptR = [alloc(f"kpt{i}", [128, D], BF16) for i in range(4)]
    kptTR = [alloc(f"kptT{i}", [128, DCH * 128], BF16) for i in range(4)]
    vtR = [alloc(f"vt{i}", [128, D], BF16) for i in range(4)]
    At2b = [alloc(f"At{i}", [128, 128], BF16) for i in range(2)]
    Apr = alloc("Apr", [128, 128], BF16)
    Ot2 = [alloc(f"Ot{i}", [128, D], BF16) for i in range(2)]
    OT2 = [alloc(f"OT{i}", [128, DCH * 128], BF16) for i in range(2)]
    scr = alloc("scr", [128, D])
    phi1 = alloc("phi1", [128, 512])
    phi2 = alloc("phi2", [128, 512])
    den = alloc("den", [128, 1])
    den2 = alloc("den2", [128, 1])
    rden = alloc("rden", [128, 1])
    bqs = alloc("bqs", [1, D])
    bks = alloc("bks", [1, D])
    bvs = alloc("bvs", [1, D])
    bos = alloc("bos", [1, D])
    ones1 = alloc("ones1", [1, 128])
    onesc = alloc("onesc", [128, 1], BF16)
    znbG = [alloc(f"znbG{i}", [128, D], BF16) for i in range(2)]
    znTg = alloc("znTg", [128, DCH * 128], BF16)
    scev = alloc("scev", [8, 512])
    bn12 = alloc("bn12", [128, 12])
    st2 = alloc("st2", [128, 2])
    rstd = alloc("rstd", [128, 1])
    epsc = alloc("epsc", [128, 1])
    sc = {"bn12": bn12, "st2": st2, "rstd": rstd, "eps": epsc[:, 0:1]}

    pt = nc.alloc_psum_tensor("pt", [128, 128], BF16)
    pa = nc.alloc_psum_tensor("pa", [128, 512], F32)
    pb = nc.alloc_psum_tensor("pb", [128, 512], F32)
    pc = nc.alloc_psum_tensor("pc", [128, 512], F32)
    pd = nc.alloc_psum_tensor("pd", [128, 512], F32)
    pe_ = nc.alloc_psum_tensor("pe", [128, 512], F32)
    pf = nc.alloc_psum_tensor("pf", [128, 512], F32)
    pg = nc.alloc_psum_tensor("pg", [128, 512], F32)

    p = Prog(nc)

    x4i = x4.rearrange("(t p) d -> p t d", p=128)
    tS = p.dma("sync", Sf[:].rearrange("p (c e) -> p c e", e=D),
               stp[0:1024, :].rearrange("(c p) e -> p c e", p=128))
    tZ = p.dma("sync", Zr[:], stp[1024:1025, :])
    tbq = p.dma("sync", bqs[:], bq)
    tbk = p.dma("sync", bks[:], bk)
    tbv = p.dma("sync", bvs[:], bv)
    tbo = p.dma("sync", bos[:], bo)
    p.dma("gpsimd", idn[:], ident)
    p.dma("gpsimd", ueb[:], ue)
    p.dma("gpsimd", wqs[:], wql)
    p.dma("gpsimd", wks[:], wkl)
    p.dma("gpsimd", wvs[:], wvl)
    p.dma("gpsimd", wos[:], wol)
    twg = p.dma("gpsimd", wgs[:], wgl)
    p.op("vector", lambda v: v.memset(ones1[:], 1.0))
    p.op("vector", lambda v: v.memset(onesc[:], 1.0))
    p.op("vector", lambda v: v.memset(epsc[:], LN_EPS))
    tSbl = None
    for cset in range(DCH):
        tSbl = p.op("scalar", lambda s, cset=cset: s.activation(
            Sb[:, cset * D:(cset + 1) * D], Sf[:, cset * D:(cset + 1) * D],
            AF.Copy), waits=[tS])
    tloads = p.heads()

    def proj(pspair, ws, brow, tbias, deps, phi, dst, src):
        tout = None
        for n in range(2):
            ps_ = pspair[n]
            tm = None
            for dc in range(DCH):
                tm = p.op("tensor", lambda t, dc=dc, n=n, ps_=ps_, ws=ws, src=src: t.matmul(
                    ps_[:, 0:512], src[:, dc * 128:(dc + 1) * 128],
                    ws[:, dc * D + n * 512: dc * D + n * 512 + 512],
                    start=(dc == 0), stop=False),
                    waits=(list(deps) if dc == 0 else []))
            tm = p.op("tensor", lambda t, n=n, ps_=ps_, brow=brow: t.matmul(
                ps_[:, 0:512], ones1[:], brow[:, n * 512:(n + 1) * 512],
                start=False, stop=True), waits=[tbias])
            if phi:
                t1 = p.op("vector", lambda v, ps_=ps_: v.tensor_scalar_min(
                    phi1[:], ps_[:, 0:512], 0.0), waits=[tm])
                t2 = p.op("scalar", lambda s: s.activation(phi1[:], phi1[:],
                                                           AF.Exp), waits=[t1])
                t3 = p.op("vector", lambda v, ps_=ps_: v.tensor_scalar_max(
                    phi2[:], ps_[:, 0:512], 0.0), waits=[t2])
                tout = p.op("vector", lambda v, n=n, dst=dst: v.tensor_tensor(
                    dst[:, n * 512:(n + 1) * 512], phi1[:], phi2[:], ALU.add),
                    waits=[t3])
            else:
                tout = p.op("vector", lambda v, n=n, ps_=ps_, dst=dst: v.tensor_copy(
                    dst[:, n * 512:(n + 1) * 512], ps_[:, 0:512]), waits=[tm])
        return tout

    pt_chain = [None]
    def transposes(src, dst, deps):
        tzt = pt_chain[0]
        for c in range(DCH):
            ttr = p.op("tensor", lambda t, c=c, src=src: t.transpose(
                pt[:], src[:, c * 128:(c + 1) * 128], idn[:]),
                waits=list(deps) + [tzt])
            tzt = p.op("vector", lambda v, c=c, dst=dst: v.tensor_copy(
                dst[:, c * 128:(c + 1) * 128], pt[:]), waits=[ttr])
        pt_chain[0] = tzt
        return tzt

    G = 4
    guard_num = {0: None, 1: None}
    guard_zr = None
    guard_pe = [None]
    xout = {0: None, 1: None}
    xfree = {0: None, 1: None}
    tSb_tok = [tSbl]
    tfold_mm = None
    tge = None
    tscdma = None
    txndma = None
    for T in range(NTO):
        b = T % 2
        r = T % 4
        g0 = (T // G) * G
        xt = xtl[b]
        txl = p.dma("sync", xt[:], x4i[:, T:T + 1, :],
                    waits=[xout[b], xfree[b]], slot=b)
        tln = emit_ln(p, sc, xt[:], znb2[b][:],
                      deps=[txl] + (tloads if T == 0 else []))
        tzt = transposes(znb2[b], znt2[b], [tln] + (tloads if T == 0 else []))
        tq = proj([pa, pb], wqs, bqs, tbq, [tzt], True, qpt2[b], znt2[b])
        tk = proj([pe_, pf], wks, bks, tbk,
                  [tzt, guard_pe[0]], True, kptR[r], znt2[b])
        tv = proj([pc, pd], wvs, bvs, tbv,
                  [tzt, guard_num[0], guard_num[1]], False, vtR[r], znt2[b])
        tqT = transposes(qpt2[b], qptT2[b], [tq])
        tkT = transposes(kptR[r], kptTR[r], [tk])
        tzc = {}
        for n in range(2):
            ps_ = [pe_, pf][n]
            p.op("tensor", lambda t, n=n, ps_=ps_, r=r: t.matmul(
                ps_[:, 0:512], ueb[:], kptR[r][:, n * 512:(n + 1) * 512],
                start=True, stop=False), waits=[tk, guard_zr])
            tzc[n] = p.op("tensor", lambda t, n=n, ps_=ps_: t.matmul(
                ps_[:, 0:512], ones1[:], Zr[:, n * 512:(n + 1) * 512],
                start=False, stop=True), waits=[tZ])
        td1 = p.op("vector", lambda v, b=b: v.scalar_tensor_tensor(
            scr[:, 0:512], qpt2[b][:, 0:512], 1.0, pe_[:, 0:512], ALU.mult,
            ALU.mult, accum_out=den[:]), waits=[tzc[0], tq])
        td2 = p.op("vector", lambda v, b=b: v.scalar_tensor_tensor(
            scr[:, 512:1024], qpt2[b][:, 512:1024], 1.0, pf[:, 0:512], ALU.mult,
            ALU.mult, accum_out=den2[:]), waits=[tzc[1]])
        p.op("vector", lambda v: v.tensor_add(den[:], den[:], den2[:]),
             waits=[td1, td2])
        p.op("vector", lambda v: v.tensor_scalar(den[:], den[:], 1e-6,
                                                 None, ALU.add))
        trd = p.op("vector", lambda v: v.reciprocal(rden[:], den[:]))
        guard_pe[0] = td2
        for n in range(2):
            tzs = p.op("tensor", lambda t, n=n, r=r: t.matmul(
                pg[0:1, 0:512], onesc[:], kptR[r][:, n * 512:(n + 1) * 512],
                start=True, stop=True), waits=[tk, guard_zr, tge])
            guard_zr = p.op("vector", lambda v, n=n: v.tensor_add(
                Zr[:, n * 512:(n + 1) * 512], Zr[:, n * 512:(n + 1) * 512],
                pg[0:1, 0:512]), waits=[tzs, td1 if n == 0 else td2])
        tA = None
        for dc in range(DCH):
            tA = p.op("tensor", lambda t, dc=dc, b=b, r=r: t.matmul(
                pa[:, 0:128], kptTR[r][:, dc * 128:(dc + 1) * 128],
                qptT2[b][:, dc * 128:(dc + 1) * 128],
                start=(dc == 0), stop=(dc == DCH - 1)),
                waits=([tkT, tqT, tv] if dc == 0 else []))
        tAm = p.op("vector", lambda v, b=b: v.tensor_tensor(
            At2b[b][:], pa[:, 0:128], ueb[:], ALU.mult), waits=[tA])
        tnum = {}
        tpair_evac = None
        pair_ev = {}
        for c in range(g0, T):
            tA2 = None
            for dc in range(DCH):
                tA2 = p.op("tensor", lambda t, dc=dc, b=b, c=c: t.matmul(
                    pa[:, 0:128], kptTR[c % 4][:, dc * 128:(dc + 1) * 128],
                    qptT2[b][:, dc * 128:(dc + 1) * 128],
                    start=(dc == 0), stop=(dc == DCH - 1)),
                    waits=([tAm, tpair_evac] if dc == 0 else []))
            tpair_evac = p.op("vector", lambda v: v.tensor_copy(
                Apr[:], pa[:, 0:128]), waits=[tA2])
            pair_ev[c] = tpair_evac
            for n in range(2):
                ps_ = [pc, pd][n]
                tnum[n] = p.op("tensor", lambda t, n=n, ps_=ps_, c=c, g0=g0: t.matmul(
                    ps_[:, 0:512], Apr[:], vtR[c % 4][:, n * 512:(n + 1) * 512],
                    start=(c == g0), stop=False),
                    waits=[tpair_evac] + ([guard_num[n]] if c == g0 else []))
        for n in range(2):
            ps_ = [pc, pd][n]
            for dc in range(DCH):
                tnum[n] = p.op("tensor", lambda t, dc=dc, n=n, ps_=ps_, b=b, g0=g0, T=T: t.matmul(
                    ps_[:, 0:512], qptT2[b][:, dc * 128:(dc + 1) * 128],
                    Sb[:, dc * D + n * 512: dc * D + n * 512 + 512],
                    start=(dc == 0 and T == g0), stop=False),
                    waits=(([tqT, guard_num[n]] if T == g0 else [tqT])
                           + tSb_tok if dc == 0 else []))
            tnum[n] = p.op("tensor", lambda t, n=n, ps_=ps_, b=b, r=r: t.matmul(
                ps_[:, 0:512], At2b[b][:], vtR[r][:, n * 512:(n + 1) * 512],
                start=False, stop=True), waits=[tAm, tv])
        tOt = None
        for n in range(2):
            ps_ = [pc, pd][n]
            tOt = p.op("scalar", lambda s, n=n, ps_=ps_, b=b: s.activation(
                Ot2[b][:, n * 512:(n + 1) * 512], ps_[:, 0:512], AF.Copy,
                scale=rden[:]), waits=[tnum[n], trd])
        guard_num[0] = tOt
        guard_num[1] = tOt
        tOT = transposes(Ot2[b], OT2[b], [tOt])
        guard_xadd = None
        for n in range(2):
            ps_ = [pa, pb][n]
            tm = None
            for dc in range(DCH):
                tm = p.op("tensor", lambda t, dc=dc, n=n, ps_=ps_, b=b: t.matmul(
                    ps_[:, 0:512], OT2[b][:, dc * 128:(dc + 1) * 128],
                    wos[:, dc * D + n * 512: dc * D + n * 512 + 512],
                    start=(dc == 0), stop=False),
                    waits=([tOT] if dc == 0 else []))
            tm = p.op("tensor", lambda t, n=n, ps_=ps_: t.matmul(
                ps_[:, 0:512], ones1[:], bos[:, n * 512:(n + 1) * 512],
                start=False, stop=True), waits=[tbo])
            guard_xadd = p.op("vector", lambda v, n=n, ps_=ps_, xt=xt: v.tensor_tensor(
                xt[:, n * 512:(n + 1) * 512], ps_[:, 0:512],
                xt[:, n * 512:(n + 1) * 512], ALU.add),
                waits=[tm, txl])
        xout[b] = p.dma("sync", x5o[:, T:T + 1, :], xt[:],
                        waits=[guard_xadd], slot=2)
        tlnG = emit_ln(p, sc, xt[:], znbG[b][:], deps=[guard_xadd])
        txndma = p.dma("sync", xno[:, T:T + 1, :], znbG[b][:],
                       waits=[tlnG], slot=3)
        tztG = transposes(znbG[b], znTg, [tlnG, tge])
        tgm = None
        for dc in range(DCH):
            tgm = p.op("tensor", lambda t_, dc=dc: t_.matmul(
                pg[0:8, 0:128], wgs[:, dc * E:(dc + 1) * E],
                znTg[:, dc * 128:(dc + 1) * 128],
                start=(dc == 0), stop=(dc == DCH - 1)),
                waits=([tztG, tge, twg, guard_zr] if dc == 0 else []))
        tge = p.op("vector", lambda v, T=T: v.tensor_copy(
            scev[0:8, (T % 4) * 128: (T % 4 + 1) * 128],
            pg[0:8, 0:128]), waits=[tgm] + ([tscdma] if T % 4 == 0 else []))
        if T % 4 == 3:
            tscdma = p.dma("sync", sc2_out[:, (T - 3) * 128: (T + 1) * 128],
                           scev[0:8, :], waits=[tge], slot=1)
        xfree[b] = tlnG
        if T % G == G - 1 and T != NTO - 1:
            tSb_tok = []
            tfa_prev = {0: guard_pe[0], 1: guard_pe[0]}
            tfa_last = None
            for dc in range(DCH):
                for n in range(2):
                    ps_ = [pe_, pf][n]
                    tmf = None
                    for ci in range(G):
                        cc = g0 + ci
                        tmf = p.op("tensor", lambda t, dc=dc, n=n, ps_=ps_, cc=cc, ci=ci: t.matmul(
                            ps_[:, 0:512], kptR[cc % 4][:, dc * 128:(dc + 1) * 128],
                            vtR[cc % 4][:, n * 512:(n + 1) * 512],
                            start=(ci == 0), stop=(ci == G - 1)),
                            waits=([tfa_prev[n], tnum[n]] if ci == 0 else []))
                    tfold_mm = tmf
                    tfa = p.op("vector", lambda v, dc=dc, n=n, ps_=ps_: v.tensor_tensor(
                        Sf[:, dc * D + n * 512: dc * D + n * 512 + 512],
                        ps_[:, 0:512],
                        Sf[:, dc * D + n * 512: dc * D + n * 512 + 512],
                        ALU.add), waits=[tmf])
                    tfa_prev[n] = tfa
                    tfa_last = tfa
                    tfe = p.op("scalar", lambda s, dc=dc, n=n: s.activation(
                        Sb[:, dc * D + n * 512: dc * D + n * 512 + 512],
                        Sf[:, dc * D + n * 512: dc * D + n * 512 + 512],
                        AF.Copy), waits=[tfa])
                    tSb_tok.append(tfe)
            guard_pe[0] = tfa_last

    ends = p.heads()
    for e in ENGINES:
        for tok in ends:
            p.wait(e, tok)
    p.finalize()
    return nc



def build_comb2():
    nc = bass.Bass("TRN2", target_bir_lowering=False)
    inp = lambda n, sh, dt=F32: nc.dram_tensor(n, sh, dt, kind="ExternalInput").ap()
    x5 = inp("x5", [TOK, D])
    y1f = inp("y1f", [NTOTAL, D])
    rkl = inp("rkl", [128, 128])
    wl = inp("wl", [128, 128])
    xf_out = nc.dram_tensor("xf_out", [TOK, D], F32, kind="ExternalOutput").ap()
    xfo = xf_out.rearrange("(t p) d -> p t d", p=128)

    m = Mem(nc)
    Off = [0]
    def alloc(name, shape, dt=F32):
        Off[0] = (Off[0] + 31) // 32 * 32
        t = m.at(name, Off[0], shape, dt)
        Off[0] += int(np.prod(shape[1:])) * mybir.dt.size(dt)
        return t

    xsb = alloc("xsb", [128, NTO * D])
    rks = alloc("rks", [128, 128])
    rki = alloc("rki", [128, 128], I32)
    wls = alloc("wls", [128, 128])
    bufs = [alloc(f"buf{r}", [128, D]) for r in range(8)]
    idxcs = [alloc(f"idxc{r}", [128, 1], I32) for r in range(8)]
    hwc = alloc("hwc", [128, 1])

    p = Prog(nc)
    bcreg = nc.alloc_register(mybir.EngineType.Pool, "bcap")
    tb0 = p.op("gpsimd", lambda g: g.reg_mov(bcreg, CAP - 1))
    tx = p.dma("sync", xsb[:].rearrange("p (t d) -> p t d", d=D),
               x5.rearrange("(t p) d -> p t d", p=128))
    trk = p.dma("sync", rks[:], rkl)
    twl = p.dma("sync", wls[:], wl)
    tri = p.op("vector", lambda v: v.tensor_copy(rki[:], rks[:]), waits=[trk])
    tcomb, ttile = emit_combine(p, nc, xsb, y1f, rki, wls, bufs, idxcs, hwc,
                                bcreg, [tx, tri, twl, tb0], tile_major=True)
    for T in range(NTO):
        p.dma("sync", xfo[:, T:T + 1, :], xsb[:, T * D:(T + 1) * D],
              waits=[ttile[T]], slot=T % 4)
    ends = p.heads()
    for e in ENGINES:
        for tok in ends:
            p.wait(e, tok)
    p.finalize()
    return nc


_NC_CACHE = {}


def _get(name, builder):
    if name not in _NC_CACHE:
        _NC_CACHE[name] = builder()
    return _NC_CACHE[name]


def kernel(**inputs):
    inputs = {k: np.ascontiguousarray(np.asarray(v)) for k, v in inputs.items()}
    cores = list(range(NCORE))

    nc = _get("l0", build_l0)
    res = run_bass_kernel_spmd(nc, _l0_host_inputs(inputs), core_ids=cores)
    x3 = [np.asarray(r["x3_out"]) for r in res.results]
    xn1 = np.concatenate([np.asarray(r["xn_out"]) for r in res.results], 0)
    sc1 = np.concatenate([np.asarray(r["sc_out"]) for r in res.results], 1)

    nc = _get("moe", build_moe)
    res = run_bass_kernel_spmd(nc, _moe_host_inputs(inputs, xn1, sc1, 1),
                               core_ids=cores)
    y1 = np.concatenate([np.asarray(r["y_out"]) for r in res.results], 0)
    rkw1 = np.asarray(res.results[0]["rkw_out"])

    nc = _get("att1a", build_att1a)
    WkT = np.ascontiguousarray(np.asarray(inputs["L1_attn_Wk"], np.float32).T)
    WvT = np.ascontiguousarray(np.asarray(inputs["L1_attn_Wv"], np.float32).T)
    maps = []
    for j in cores:
        rkl, wl = _combine_host(rkw1, j)
        maps.append(dict(
            x3=x3[j], y1f=y1, rkl=rkl, wl=wl,
            wkl=_sbuf_chunks(WkT, DCH), wvl=_sbuf_chunks(WvT, DCH),
            bk=np.asarray(inputs["L1_attn_bk"], np.float32).reshape(1, D),
            bv=np.asarray(inputs["L1_attn_bv"], np.float32).reshape(1, D),
            ident=np.eye(128, dtype=np.float32),
        ))
    res = run_bass_kernel_spmd(nc, maps, core_ids=cores)
    x4 = [np.asarray(r["x4_out"]) for r in res.results]
    st = [np.asarray(r["st_out"]) for r in res.results]

    nc = _get("att1b", build_att1b)
    WT = lambda w: _sbuf_chunks(np.ascontiguousarray(
        np.asarray(inputs[w], np.float32).T), DCH)
    kk, tt = np.meshgrid(np.arange(128), np.arange(128), indexing="ij")
    ue = (kk <= tt).astype(np.float32)
    zst = np.zeros((1025, D), np.float32)
    maps = []
    for j in cores:
        maps.append(dict(
            x4=x4[j], stp=(st[j - 1] if j % 2 == 1 else zst),
            wql=WT("L1_attn_Wq"), wkl=WT("L1_attn_Wk"),
            wvl=WT("L1_attn_Wv"), wol=WT("L1_attn_Wo"),
            bq=np.asarray(inputs["L1_attn_bq"], np.float32).reshape(1, D),
            bk=np.asarray(inputs["L1_attn_bk"], np.float32).reshape(1, D),
            bv=np.asarray(inputs["L1_attn_bv"], np.float32).reshape(1, D),
            bo=np.asarray(inputs["L1_attn_bo"], np.float32).reshape(1, D),
            ue=ue, ident=np.eye(128, dtype=np.float32),
            wgl=_sbuf_chunks(np.ascontiguousarray(
                np.asarray(inputs["L1_ffn_Wg"], np.float32).T), DCH),
        ))
    res = run_bass_kernel_spmd(nc, maps, core_ids=cores)
    x5 = [np.asarray(r["x5_out"]) for r in res.results]
    xn2 = np.concatenate([np.asarray(r["xn2_out"]) for r in res.results], 0)
    sc2 = np.concatenate([np.asarray(r["sc2_out"]) for r in res.results], 1)

    nc = _get("moe", build_moe)
    res = run_bass_kernel_spmd(nc, _moe_host_inputs(inputs, xn2, sc2, 1),
                               core_ids=cores)
    y2 = np.concatenate([np.asarray(r["y_out"]) for r in res.results], 0)
    rkw2 = np.asarray(res.results[0]["rkw_out"])

    nc = _get("comb2", build_comb2)
    maps = []
    for j in cores:
        rkl, wl = _combine_host(rkw2, j)
        maps.append(dict(x5=x5[j], y1f=y2, rkl=rkl, wl=wl))
    res = run_bass_kernel_spmd(nc, maps, core_ids=cores)
    out = np.concatenate([np.asarray(r["xf_out"]) for r in res.results], 0)
    return out.reshape(4, 4096, D)


# version 12
# speedup vs baseline: 1.0127x; 1.0127x over previous
import numpy as np
import ml_dtypes
import concourse.bass as bass
import concourse.mybir as mybir
from concourse.bass_utils import run_bass_kernel_spmd
from concourse.bass_core_rust import IndirectOffsetOnAxis

F32 = mybir.dt.float32
BF16 = mybir.dt.bfloat16
FP16 = mybir.dt.float16
FP8 = mybir.dt.float8e4
I32 = mybir.dt.int32
AF = mybir.ActivationFunctionType
ALU = mybir.AluOpType
AX = mybir.AxisListType

NCORE = 8
TOK = 2048
HALO = 128
TOKH = TOK + 2 * HALO
NT = TOKH // 128
NTO = TOK // 128
D = 1024
DCH = D // 128
HFF = 4096
HCH = HFF // 128
NH = 16
HD = 64
E = 8
CAP = 2048
HMOE = 2048
LN_EPS = 1e-5
NTOTAL = NCORE * TOK

ENGINES = ("sync", "scalar", "tensor", "vector", "gpsimd")
AUTO_SERIAL = {"vector", "scalar", "gpsimd"}


class Prog:

    def __init__(self, nc):
        self.nc = nc
        self.items = {e: [] for e in ENGINES}
        self.sems = {}
        self.cnt = {}
        self.hw = {e: {} for e in ENGINES}

    def _sem(self, name):
        if name not in self.sems:
            self.sems[name] = self.nc.alloc_semaphore(name)
            self.cnt[name] = 0
        return self.sems[name]

    def wait(self, eng, tok):
        if tok is None:
            return
        name, val = tok
        if val <= self.hw[eng].get(name, 0):
            return
        self.hw[eng][name] = val
        self.items[eng].append(("w", self._sem(name), val))

    def op(self, eng, fn, waits=(), dma=False, slot=0):
        sem_name = eng + (f"_d{slot}" if dma else "")
        if dma and self.cnt.get(sem_name, 0) > 0:
            self.wait(eng, (sem_name, self.cnt[sem_name]))
        if eng in AUTO_SERIAL:
            if not dma and self.cnt.get(sem_name, 0) > 0:
                self.wait(eng, (sem_name, self.cnt[sem_name]))
            if dma and self.cnt.get(eng, 0) > 0:
                self.wait(eng, (eng, self.cnt[eng]))
        for t in waits:
            self.wait(eng, t)
        h = self._sem(sem_name)
        amt = 16 if dma else 1
        self.cnt[sem_name] = self.cnt.get(sem_name, 0) + amt
        self.items[eng].append(("e", fn, h, amt))
        return (sem_name, self.cnt[sem_name])

    def dma(self, eng, out, in_, waits=(), slot=0, **kw):
        return self.op(eng, lambda g: g.dma_start(out, in_, **kw), waits=waits,
                       dma=True, slot=slot)

    def heads(self):
        return [(n, c) for n, c in self.cnt.items() if c > 0]

    def finalize(self):
        with self.nc.Block() as block:
            for e in ENGINES:
                items = self.items[e]
                if not items:
                    continue

                def runner(eng, items=items):
                    for it in items:
                        if it[0] == "w":
                            eng.wait_ge(it[1], it[2])
                        else:
                            it[1](eng).then_inc(it[2], it[3])

                getattr(block, e)(runner)


class Mem:

    def __init__(self, nc):
        self.nc = nc
        self.base = (nc.sbuf_base + 31) // 32 * 32
        self.top = nc.sbuf_top

    def at(self, name, off, shape, dt=F32):
        end = off + int(np.prod(shape[1:])) * mybir.dt.size(dt)
        assert self.base + end <= self.top, (name, off, end, self.top - self.base)
        return self.nc.alloc_sbuf_tensor_at(name, list(shape), dt,
                                            offset=self.base + off)


def _finish_early(nc, p, outs, srcs):
    x3_out, xn_out, sc_out = outs
    xsb, znb_g, scev = srcs
    x3o2 = x3_out.rearrange("(t p) d -> p t d", p=128)
    hh = p.heads()
    for t in range(1, NT - 1):
        p.dma("sync", x3o2[:, t - 1:t, :], xsb[:, t * D:(t + 1) * D],
              waits=(hh if t == 1 else []))
    ends = p.heads()
    for e in ENGINES:
        for tok in ends:
            p.wait(e, tok)
    p.finalize()
    return nc


def emit_ln(p, sc, tile_in, tile_out, deps=(), eps=LN_EPS):
    p.op("vector", lambda v: v.bn_stats(sc["bn12"][:, 0:6], tile_in[:, 0:512]),
         waits=deps)
    p.op("vector", lambda v: v.bn_stats(sc["bn12"][:, 6:12], tile_in[:, 512:1024]))
    t3 = p.op("vector", lambda v: v.bn_aggr(sc["st2"][:], sc["bn12"][:]))
    p.op("scalar", lambda s: s.activation(sc["rstd"][:], sc["st2"][:, 1:2],
                                          AF.Ln, bias=sc["eps"]), waits=[t3])
    t5 = p.op("scalar", lambda s: s.activation(sc["rstd"][:], sc["rstd"][:],
                                               AF.Exp, scale=-0.5))
    return p.op("vector", lambda v: v.tensor_scalar(tile_out[:], tile_in[:],
                                                    sc["st2"][:, 0:1],
                                                    sc["rstd"][:],
                                                    ALU.subtract, ALU.mult),
                waits=[t5])



def build_l0(phases=99, dbg=False, dbg_g=0, dbg_hp=0, dbg_lq=0):
    nc = bass.Bass("TRN2", target_bir_lowering=False)

    inp = lambda n, sh, dt=F32: nc.dram_tensor(n, sh, dt, kind="ExternalInput").ap()
    out_ = lambda n, sh, dt=F32: nc.dram_tensor(n, sh, dt, kind="ExternalOutput").ap()

    xh = inp("xh", [TOKH, D])
    w1l = inp("w1l", [128, DCH * HFF])
    b1l = inp("b1l", [128, HCH])
    w2l = inp("w2l", [128, HCH * D])
    b2 = inp("b2", [1, D])
    wqkvl = inp("wqkvl", [128, DCH * 3 * D])
    woutl = inp("woutl", [128, DCH * D])
    gaml = inp("gaml", [128, DCH])
    betl = inp("betl", [128, DCH])
    wgl = inp("wgl", [128, DCH * E])
    ident = inp("ident", [128, 128])
    mprev = inp("mprev", [128, 128])
    mnext = inp("mnext", [128, 128])
    mfirst = inp("mfirst", [128, 128])
    mlast = inp("mlast", [128, 128])

    x3_out = out_("x3_out", [TOK, D])
    xn_out = out_("xn_out", [TOK, D], BF16)
    sc_out = out_("sc_out", [E, TOK])
    x3o = x3_out.rearrange("(t p) d -> p t d", p=128)
    xno = xn_out.rearrange("(t p) d -> p t d", p=128)
    m = Mem(nc)
    xsb = m.at("xsb", 0, [128, NT * D])
    wA = m.at("wA", 73728, [128, 32768], BF16)
    R2, R3, R4 = 139264, 176128, 204288
    wB8 = m.at("wB8", R2, [128, 32768], FP8)
    znT6 = m.at("znT6", R2, [128, DCH * 768], BF16)
    Kf = m.at("Kf", R2 + 12288, [128, 8 * 768], BF16)
    Vt = m.at("Vt", R2 + 24576, [128, 6 * 1024], BF16)
    stag = m.at("stag", R3, [128, 3 * D])
    b2row = m.at("b2row", R3 + 12288, [1, D])
    b2q = m.at("b2q", R3 + 16384, [1, D], FP16)
    hq8 = m.at("hq8", R3, [128, HCH * 512], FP8)
    znTf = m.at("znTf", R3 + 16384, [128, DCH * 512], BF16)
    znb_f = m.at("znb_f", R2 + 32768, [128, D], BF16)
    fout = m.at("fout", R2 + 34816, [128, D], FP16)
    Qf = m.at("Qf", R3, [128, 8 * 256], BF16)
    atf = m.at("atf", R3 + 4096, [128, 2048], BF16)
    Pt = m.at("Pt", R3 + 8192, [128, 4 * 384], BF16)
    zn_a = m.at("zn_a", R3 + 11264, [128, D])
    att = m.at("att", R3 + 11264, [128, 2048], BF16)
    znb_a = m.at("znb_a", R3 + 15360, [128, D], BF16)
    vbrow = m.at("vbrow", R3 + 17408, [1, D], BF16)
    qkrow = m.at("qkrow", R3 + 19456, [1, 2048], BF16)
    onesw = m.at("onesw", R3 + 23552, [1, 512], BF16)
    rcl4 = m.at("rcl4", R3 + 24576, [128, 4])
    znbg2 = m.at("znbg2", R3, [128, D], BF16)
    znTg2 = m.at("znTg2", R3 + 2048, [128, DCH * 128], BF16)
    znb_g = m.at("znb_g", R3 + 4096, [128, D], BF16)
    znTg = m.at("znTg", R3 + 6144, [128, DCH * 128], BF16)
    scev = m.at("scev", R3 + 8192, [8, 512])
    b1s = m.at("b1s", R4, [128, HCH])
    b2tile = m.at("b2tile", R4 + 128, [128, D])
    gams = m.at("gams", R4 + 4224, [128, DCH])
    bets = m.at("bets", R4 + 4256, [128, DCH])
    idn = m.at("idn", R4 + 4288, [128, 128], BF16)
    masks = m.at("masks", R4 + 4544, [128, 4 * 128], BF16)
    wgs = m.at("wgs", R4 + 5568, [128, DCH * E], BF16)
    ones512 = m.at("ones512", R4 + 5696, [1, 512])
    ones1 = m.at("ones1", R4 + 7744, [1, 128])
    bn12 = m.at("bn12", R4 + 8256, [128, 12])
    st2 = m.at("st2", R4 + 8320, [128, 2])
    rstd = m.at("rstd", R4 + 8352, [128, 1])
    nmr = m.at("nmr", R4 + 8384, [128, 1])
    rcl = m.at("rcl", R4 + 8416, [128, 1])
    von = m.at("von", R4 + 8448, [128, 1], BF16)
    epsc = m.at("epsc", R4 + 8480, [128, 1])
    sc = {"bn12": bn12, "st2": st2, "rstd": rstd, "nmr": nmr,
          "eps": epsc[:, 0:1]}

    pt = nc.alloc_psum_tensor("pt", [128, 128], BF16)
    pa = nc.alloc_psum_tensor("pa", [128, 512], F32)
    pb = nc.alloc_psum_tensor("pb", [128, 512], F32)
    pc = nc.alloc_psum_tensor("pc", [128, 512], F32)
    pd = nc.alloc_psum_tensor("pd", [128, 512], F32)
    pe_ = nc.alloc_psum_tensor("pe", [128, 512], F32)
    pf = nc.alloc_psum_tensor("pf", [128, 512], F32)
    pg = nc.alloc_psum_tensor("pg", [128, 512], F32)

    p = Prog(nc)

    p.dma("sync", b1s[:], b1l)
    p.dma("sync", gams[:], gaml)
    p.dma("sync", bets[:], betl)
    p.dma("gpsimd", idn[:], ident)
    p.dma("gpsimd", masks[:, 0:128], mprev)
    p.dma("gpsimd", masks[:, 128:256], mnext)
    p.dma("gpsimd", masks[:, 256:384], mfirst)
    p.dma("gpsimd", masks[:, 384:512], mlast)
    p.dma("gpsimd", wgs[:], wgl)
    tb2 = p.dma("sync", b2row[:], b2)
    p.dma("sync", xsb[:].rearrange("p (t d) -> p t d", d=D),
          xh.rearrange("(t p) d -> p t d", p=128))
    p.dma("gpsimd", wA[:], w1l)
    p.dma("gpsimd", wB8[:], w2l)
    p.op("vector", lambda v: v.memset(ones512[:], 1.0))
    p.op("vector", lambda v: v.memset(epsc[:], LN_EPS))
    p.op("vector", lambda v: v.memset(ones1[:], 1.0))
    p.op("vector", lambda v: v.memset(von[:], 1.0))
    tq = p.op("vector", lambda v: v.tensor_copy(b2q[:], b2row[:]), waits=[tb2])
    tq2 = p.op("vector", lambda v: v.tensor_copy(b2row[:], b2q[:]), waits=[tq])
    tbc = None
    for n in range(2):
        ta = p.op("tensor", lambda t, n=n: t.matmul(
            pg[:, 0:512], ones1[:], b2row[:, n * 512:(n + 1) * 512],
            start=True, stop=True), waits=[tq2, tbc])
        tbc = p.op("vector", lambda v, n=n: v.tensor_copy(
            b2tile[:, n * 512:(n + 1) * 512], pg[:, 0:512]), waits=[ta])
    setup_done = p.heads()
    if phases < 1:
        return _finish_early(nc, p, [x3_out, xn_out, sc_out], [xsb, znb_g, scev])

    def emit_ffn(t_lo, t_hi, start_deps):
        prev_mm1_evac = {0: None, 1: None}
        prev_mm2_add = {0: None, 1: None}
        hq_guard = None
        znT_guard = None
        deps0 = list(start_deps)
        tiles = list(range(t_lo, t_hi))
        for si in range(0, len(tiles), 4):
            slab = tiles[si:si + 4]
            nslab = len(slab) * 128
            tz = None
            for k, t in enumerate(slab):
                xt = xsb[:, t * D:(t + 1) * D]
                tln = emit_ln(p, sc, xt, znb_f[:],
                              deps=deps0 + [znT_guard, tz])
                deps0 = []
                for c in range(DCH):
                    ttr = p.op("tensor", lambda tt, c=c: tt.transpose(
                        pt[:], znb_f[:, c * 128:(c + 1) * 128], idn[:]),
                        waits=[tln, tz])
                    tz = p.op("vector", lambda v, c=c, k=k: v.tensor_copy(
                        znTf[:, c * 512 + k * 128: c * 512 + k * 128 + 128],
                        pt[:]), waits=[ttr])
            last_evac = None
            for hc in range(HCH):
                ps = [pa, pb][hc % 2]
                w0 = [tz, prev_mm1_evac[hc % 2], hq_guard]
                tm = None
                for dc in range(DCH):
                    tm = p.op("tensor", lambda t_, hc=hc, dc=dc, ps=ps, nslab=nslab: t_.matmul(
                        ps[:, 0:nslab],
                        wA[:, dc * HFF + hc * 128: dc * HFF + hc * 128 + 128],
                        znTf[:, dc * 512: dc * 512 + nslab],
                        start=(dc == 0), stop=(dc == DCH - 1)),
                        waits=(w0 if dc == 0 else []))
                tev = p.op("scalar", lambda s, hc=hc, ps=ps, nslab=nslab: s.activation(
                    hq8[:, hc * 512: hc * 512 + nslab], ps[:, 0:nslab],
                    AF.Relu, bias=b1s[:, hc:hc + 1]), waits=[tm])
                prev_mm1_evac[hc % 2] = tev
                last_evac = tev
            znT_guard = last_evac
            for k, t in enumerate(slab):
                pspair = [pc, pd] if k % 2 == 0 else [pe_, pf]
                for n in range(2):
                    ps2 = pspair[n]
                    tm = None
                    for hc in range(HCH):
                        tm = p.op("tensor", lambda t_, hc=hc, n=n, k=k, ps2=ps2: t_.matmul(
                            ps2[:, 0:512],
                            hq8[:, hc * 512 + k * 128: hc * 512 + k * 128 + 128],
                            wB8[:, hc * D + n * 512: hc * D + n * 512 + 512],
                            start=(hc == 0), stop=(hc == HCH - 1)),
                            waits=([last_evac, prev_mm2_add[k % 2]]
                                   if hc == 0 else []))
                    tq_ = p.op("vector", lambda v, n=n, ps2=ps2: v.tensor_tensor(
                        fout[:, n * 512:(n + 1) * 512], ps2[:, 0:512],
                        b2tile[:, n * 512:(n + 1) * 512], ALU.add), waits=[tm])
                    tadd = p.op("vector", lambda v, n=n, t=t: v.scalar_tensor_tensor(
                        xsb[:, t * D + n * 512: t * D + n * 512 + 512],
                        fout[:, n * 512:(n + 1) * 512], 0.5,
                        xsb[:, t * D + n * 512: t * D + n * 512 + 512],
                        ALU.mult, ALU.add), waits=[tq_])
                    prev_mm2_add[k % 2] = tadd
                hq_guard = prev_mm2_add[k % 2]
        return p.heads()

    ffn1_done = emit_ffn(0, NT, setup_done)
    if phases < 2:
        return _finish_early(nc, p, [x3_out, xn_out, sc_out], [xsb, znb_g, scev])

    qb_ps = [pa, pb, pc, pd, pe_, pf]
    tqb = [None] * 6
    tfold = None
    for dc in range(DCH):
        tst = p.dma("sync", stag[:], wqkvl[:, dc * 3 * D:(dc + 1) * 3 * D],
                    waits=(ffn1_done if dc == 0 else [tfold]))
        for n in range(6):
            tqb[n] = p.op("tensor", lambda t, n=n, dc=dc: t.matmul(
                qb_ps[n][0:1, 0:512], bets[:, dc:dc + 1],
                stag[:, n * 512:(n + 1) * 512],
                start=(dc == 0), stop=(dc == DCH - 1)), waits=[tst])
        tfold = p.op("vector", lambda v, dc=dc: v.tensor_scalar(
            wA[:, dc * 3 * D:(dc + 1) * 3 * D], stag[:],
            gams[:, dc:dc + 1], None, ALU.mult), waits=[tst, tqb[5]])
    tqk = None
    for n in range(4):
        tqk = p.op("vector", lambda v, n=n: v.tensor_copy(
            qkrow[0:1, n * 512:(n + 1) * 512], qb_ps[n][0:1, 0:512]),
            waits=[tqb[n], tqk])
    tvb = p.op("vector", lambda v: v.tensor_copy(vbrow[0:1, 0:512],
                                                 pe_[0:1, 0:512]),
               waits=[tqb[4], tqk])
    tvb = p.op("vector", lambda v: v.tensor_copy(vbrow[0:1, 512:1024],
                                                 pf[0:1, 0:512]),
               waits=[tqb[5], tvb])
    tqbc = tqk
    twout = p.dma("gpsimd", wA[:, 24576:32768], woutl, waits=ffn1_done)
    tone = p.op("vector", lambda v: v.memset(onesw[:], 1.0), waits=ffn1_done)
    if phases < 3:
        return _finish_early(nc, p, [x3_out, xn_out, sc_out], [xsb, znb_g, scev])

    prev_op_t = {}
    tevac_reg = [None, None]
    texp_reg = [None, None]
    trc_prev = None
    tact_reg = [None, None]
    tVreg = {0: None, 1: None}
    tpn = None
    ptch = [None]
    setup_tok = [tfold, tqbc, tvb, twout, tone]
    tprojV = {}
    attn_first = True
    for q0 in range(1, NTO, 2):
        newtiles = list(range(q0 + 1, q0 + 3)) if q0 > 1 else [0, 1, 2, 3]
        for t in newtiles:
            xt = xsb[:, t * D:(t + 1) * D]
            t1 = emit_ln(p, sc, xt, zn_a[:],
                         deps=(setup_tok if attn_first else []) + [tpn])
            attn_first = False
            t2 = emit_ln(p, sc, zn_a[:], znb_a[:], deps=[t1])
            for c in range(DCH):
                ttr = p.op("tensor", lambda tt, c=c: tt.transpose(
                    pt[:], znb_a[:, c * 128:(c + 1) * 128], idn[:]),
                    waits=[t2, ptch[0]])
                tpn = p.op("vector", lambda v, c=c, t=t: v.tensor_copy(
                    znT6[:, c * 768 + (t % 6) * 128: c * 768 + (t % 6) * 128 + 128],
                    pt[:]), waits=[ttr])
                ptch[0] = tpn
            if 1 <= t <= NT - 2:
                tpn = p.op("vector", lambda v, xt=xt: v.tensor_add(
                    xt, xt, zn_a[:]), waits=[t2, tpn])
        runs = []
        for t in newtiles:
            if runs and runs[-1][0] + len(runs[-1][1]) == t % 6:
                runs[-1][1].append(t)
            else:
                runs.append([t % 6, [t]])
        qruns = []
        for q in (q0, q0 + 1):
            if qruns and qruns[-1][0] + len(qruns[-1][1]) == q % 6:
                qruns[-1][1].append(q)
            else:
                qruns.append([q % 6, [q]])
        pi = 0
        for hp in range(NH // 2):
            for s0, tl in runs:
                W = 128 * len(tl)
                psK = [pc, pd][pi % 2]
                tm = None
                for dc in range(DCH):
                    tm = p.op("tensor", lambda t_, dc=dc, hp=hp, s0=s0, W=W, psK=psK: t_.matmul(
                        psK[:, 0:W],
                        wA[:, dc * 3 * D + D + hp * 128: dc * 3 * D + D + hp * 128 + 128],
                        znT6[:, dc * 768 + s0 * 128: dc * 768 + s0 * 128 + W],
                        start=(dc == 0), stop=False),
                        waits=([tpn, tevac_reg[pi % 2], texp_reg[pi % 2],
                                prev_op_t.get(id(psK))] if dc == 0 else []))
                tm = p.op("tensor", lambda t_, hp=hp, W=W, psK=psK: t_.matmul(
                    psK[:, 0:W], qkrow[0:1, (8 + hp) * 128: (9 + hp) * 128],
                    onesw[0:1, 0:W], start=False, stop=True), waits=[tqbc, tone])
                tevac_reg[pi % 2] = p.op("vector", lambda v, hp=hp, s0=s0, W=W, psK=psK: v.tensor_copy(
                    Kf[:, hp * 768 + s0 * 128: hp * 768 + s0 * 128 + W],
                    psK[:, 0:W]), waits=[tm])
                pi += 1
            for s0, ql in qruns:
                W = 128 * len(ql)
                psQ = [pc, pd][pi % 2]
                tm = None
                for dc in range(DCH):
                    tm = p.op("tensor", lambda t_, dc=dc, hp=hp, s0=s0, W=W, psQ=psQ: t_.matmul(
                        psQ[:, 0:W],
                        wA[:, dc * 3 * D + hp * 128: dc * 3 * D + hp * 128 + 128],
                        znT6[:, dc * 768 + s0 * 128: dc * 768 + s0 * 128 + W],
                        start=(dc == 0), stop=False),
                        waits=([tpn, tevac_reg[pi % 2], texp_reg[pi % 2],
                                prev_op_t.get(id(psQ))] if dc == 0 else []))
                tm = p.op("tensor", lambda t_, hp=hp, W=W, psQ=psQ: t_.matmul(
                    psQ[:, 0:W], qkrow[0:1, hp * 128: (hp + 1) * 128],
                    onesw[0:1, 0:W], start=False, stop=True), waits=[tqbc, tone])
                tevac_reg[pi % 2] = p.op("vector", lambda v, hp=hp, q0=q0, ql=ql, W=W, psQ=psQ: v.tensor_copy(
                    Qf[:, hp * 256 + (ql[0] - q0) * 128: hp * 256 + (ql[0] - q0) * 128 + W],
                    psQ[:, 0:W]), waits=[tm])
                pi += 1
        for t in newtiles:
            for n in range(2):
                psV = [pe_, pf][n]
                tm = None
                for dc in range(DCH):
                    tm = p.op("tensor", lambda t_, dc=dc, n=n, t=t, psV=psV: t_.matmul(
                        psV[:, 0:512],
                        znT6[:, dc * 768 + (t % 6) * 128: dc * 768 + (t % 6) * 128 + 128],
                        wA[:, dc * 3 * D + 2 * D + n * 512: dc * 3 * D + 2 * D + n * 512 + 512],
                        start=(dc == 0), stop=False),
                        waits=([tpn, prev_op_t.get(id(psV)), tVreg[n]]
                               if dc == 0 else []))
                tm = p.op("tensor", lambda t_, n=n, psV=psV: t_.matmul(
                    psV[:, 0:512], onesw[0:1, 0:128],
                    vbrow[0:1, n * 512:(n + 1) * 512],
                    start=False, stop=True), waits=[tvb, tone])
                tVreg[n] = p.op("vector", lambda v, n=n, t=t, psV=psV: v.tensor_copy(
                    Vt[:, (t % 6) * 1024 + n * 512: (t % 6) * 1024 + n * 512 + 512],
                    psV[:, 0:512]), waits=[tm, tprojV.get(t % 6)])
                tprojV[t % 6] = tVreg[n]
        oi = 0
        tact_last = None
        for hp in range(NH // 2):
            for hh in range(2):
                h = 2 * hp + hh
                pbase = hh * 64
                tPt = {}
                for ki, k in enumerate(range(q0 - 1, q0 + 3)):
                    qlo = max(k - 1, q0)
                    qhi = min(k + 1, q0 + 1)
                    W = (qhi - qlo + 1) * 128
                    psS = [pc, pd][ki % 2]
                    tsc = p.op("tensor", lambda t_, hp=hp, k=k, qlo=qlo, W=W, psS=psS, pbase=pbase, q0=q0: t_.matmul(
                        psS[:, 0:W],
                        Kf[pbase:pbase + 64, hp * 768 + (k % 6) * 128: hp * 768 + (k % 6) * 128 + 128],
                        Qf[pbase:pbase + 64, hp * 256 + (qlo - q0) * 128: hp * 256 + (qlo - q0) * 128 + W],
                        start=True, stop=True),
                        waits=[texp_reg[ki % 2], tevac_reg[ki % 2],
                               tevac_reg[1 - ki % 2]])
                    toff = (qlo - k + 1) * 128
                    texp = p.op("scalar", lambda s, ki=ki, toff=toff, W=W, psS=psS: s.activation(
                        Pt[:, ki * 384 + toff: ki * 384 + toff + W],
                        psS[:, 0:W], AF.Exp, scale=0.125), waits=[tsc])
                    texp_reg[ki % 2] = texp
                    tblk = texp
                    for q in range(qlo, qhi + 1):
                        if q == k:
                            continue
                        if q == k + 1:
                            mi = 2 if k == 0 else 0
                        else:
                            mi = 3 if k == NT - 1 else 1
                        tblk = p.op("vector", lambda v, ki=ki, q=q, k=k, mi=mi: v.tensor_tensor(
                            Pt[:, ki * 384 + (q - k + 1) * 128: ki * 384 + (q - k + 1) * 128 + 128],
                            Pt[:, ki * 384 + (q - k + 1) * 128: ki * 384 + (q - k + 1) * 128 + 128],
                            masks[:, mi * 128:(mi + 1) * 128], ALU.mult),
                            waits=[texp, tblk])
                    tPt[ki] = tblk
                for qi, q in enumerate((q0, q0 + 1)):
                    reg = hh * 2 + qi
                    tlm = None
                    for j, kk in enumerate(range(q - 1, q + 2)):
                        ki = kk - (q0 - 1)
                        tlm = p.op("tensor", lambda t_, ki=ki, q=q, kk=kk, j=j: t_.matmul(
                            pa[:, 0:1],
                            Pt[:, ki * 384 + (q - kk + 1) * 128: ki * 384 + (q - kk + 1) * 128 + 128],
                            von[:],
                            start=(j == 0), stop=(j == 2)),
                            waits=([tPt[ki], trc_prev] if j == 0 else [tPt[ki]]))
                    trc_prev = p.op("vector", lambda v, reg=reg: v.reciprocal(
                        rcl4[:, reg:reg + 1], pa[:, 0:1]), waits=[tlm])
                    trc = trc_prev
                    psO = [pb, pg][oi % 2]
                    tav = None
                    for j, kk in enumerate(range(q - 1, q + 2)):
                        ki = kk - (q0 - 1)
                        tav = p.op("tensor", lambda t_, ki=ki, q=q, kk=kk, j=j, h=h, psO=psO: t_.matmul(
                            psO[:, 0:64],
                            Pt[:, ki * 384 + (q - kk + 1) * 128: ki * 384 + (q - kk + 1) * 128 + 128],
                            Vt[:, (kk % 6) * 1024 + h * 64: (kk % 6) * 1024 + h * 64 + 64],
                            start=(j == 0), stop=(j == 2)),
                            waits=([tPt[ki], tprojV[kk % 6], tact_reg[oi % 2]]
                                   if j == 0 else [tPt[ki]]))
                    tact_last = p.op("scalar", lambda s, reg=reg, qi=qi, h=h, psO=psO: s.activation(
                        att[:, qi * 1024 + h * 64: qi * 1024 + h * 64 + 64],
                        psO[:, 0:64], AF.Copy,
                        scale=rcl4[:, reg:reg + 1]), waits=[tav, trc])
                    tact_reg[oi % 2] = tact_last
                    oi += 1
        tfq = {}
        for qi, q in enumerate((q0, q0 + 1)):
            tfq[qi] = []
            for c in range(DCH):
                ttr = p.op("tensor", lambda t_, qi=qi, c=c: t_.transpose(
                    pt[:], att[:, qi * 1024 + c * 128: qi * 1024 + c * 128 + 128],
                    idn[:]), waits=[tact_last, ptch[0]])
                tev = p.op("vector", lambda v, qi=qi, c=c: v.tensor_copy(
                    atf[:, qi * 1024 + c * 128: qi * 1024 + c * 128 + 128],
                    pt[:]), waits=[ttr])
                ptch[0] = tev
                tfq[qi].append(tev)
        for qi, q in enumerate((q0, q0 + 1)):
            pop = [pc, pd] if qi == 0 else [pe_, pf]
            for n in range(2):
                tm = None
                for hp in range(NH // 2):
                    tm = p.op("tensor", lambda t_, hp=hp, n=n, qi=qi, pop=pop: t_.matmul(
                        pop[n][:, 0:512],
                        atf[:, qi * 1024 + hp * 128: qi * 1024 + hp * 128 + 128],
                        wA[:, 24576 + hp * D + n * 512: 24576 + hp * D + n * 512 + 512],
                        start=(hp == 0), stop=(hp == NH // 2 - 1)),
                        waits=[tfq[qi][hp]] + ([prev_op_t.get(id(pop[n])),
                                                texp_reg[0], texp_reg[1]]
                                               if hp == 0 else []))
                prev_op_t[id(pop[n])] = p.op("vector", lambda v, n=n, q=q, pop=pop: v.tensor_tensor(
                    xsb[:, q * D + n * 512: q * D + n * 512 + 512],
                    pop[n][:, 0:512],
                    xsb[:, q * D + n * 512: q * D + n * 512 + 512],
                    ALU.add), waits=[tm])
    attn_done = p.heads()
    if phases < 5:
        return _finish_early(nc, p, [x3_out, xn_out, sc_out], [xsb, znb_g, scev])

    tw1b = p.dma("gpsimd", wA[:], w1l, waits=attn_done)
    tw2b = p.dma("gpsimd", wB8[:], w2l)
    ffn2_done = emit_ffn(1, NT - 1, [tw1b, tw2b])
    if phases < 6:
        return _finish_early(nc, p, [x3_out, xn_out, sc_out], [xsb, znb_g, scev])

    tge = {0: None, 1: None}
    tztg = None
    tscdma = None
    txndma = {0: None, 1: None}
    for t in range(1, NT - 1):
        b = (t - 1) % 2
        znbg = [znb_g, znbg2][b]
        zntg = [znTg, znTg2][b]
        psg = [pa, pg][b]
        p.dma("sync", x3o[:, t - 1:t, :], xt_ := xsb[:, t * D:(t + 1) * D],
              waits=(ffn2_done if t == 1 else []), slot=2 + b)
        tln = emit_ln(p, sc, xt_, znbg[:],
                      deps=(ffn2_done if t == 1 else [txndma[b]]))
        txndma[b] = p.dma("sync", xno[:, t - 1:t, :], znbg[:], waits=[tln],
                          slot=b)
        for c in range(DCH):
            ttr = p.op("tensor", lambda tt, c=c, znbg=znbg: tt.transpose(
                pt[:], znbg[:, c * 128:(c + 1) * 128], idn[:]),
                waits=[tln, tztg])
            tztg = p.op("vector", lambda v, c=c, zntg=zntg: v.tensor_copy(
                zntg[:, c * 128:(c + 1) * 128], pt[:]), waits=[ttr])
        tg = None
        for dc in range(DCH):
            tg = p.op("tensor", lambda t_, dc=dc, psg=psg, zntg=zntg: t_.matmul(
                psg[0:8, 0:128], wgs[:, dc * E:(dc + 1) * E],
                zntg[:, dc * 128:(dc + 1) * 128],
                start=(dc == 0), stop=(dc == DCH - 1)),
                waits=([tztg, tge[b]] if dc == 0 else []))
        tge[b] = p.op("vector", lambda v, t=t, psg=psg: v.tensor_copy(
            scev[0:8, ((t - 1) % 4) * 128: ((t - 1) % 4 + 1) * 128],
            psg[0:8, 0:128]), waits=[tg] + ([tscdma] if (t - 1) % 4 <= 1 else []))
        if (t - 1) % 4 == 3:
            tscdma = p.dma("sync", sc_out[:, (t - 4) * 128: t * 128],
                           scev[0:8, :], waits=[tge[0], tge[1]], slot=2)
    ends = p.heads()
    for e in ENGINES:
        for tok in ends:
            p.wait(e, tok)
    p.finalize()
    return nc


def _fix_unused():
    pass



def _sbuf_chunks(a, nch):
    F = a.shape[1]
    return np.ascontiguousarray(
        a.reshape(nch, 128, F).transpose(1, 0, 2).reshape(128, nch * F))


def _l0_host_inputs(inputs):
    w1T = np.ascontiguousarray(np.asarray(inputs["L0_ffn_w1"], np.float32).T)
    w2T = np.ascontiguousarray(np.asarray(inputs["L0_ffn_w2"], np.float32).T)
    wqkvT = np.ascontiguousarray(np.asarray(inputs["L0_attn_w_qkv"], np.float32).T)
    woutT = np.ascontiguousarray(np.asarray(inputs["L0_attn_w_out"], np.float32).T)
    wgT = np.ascontiguousarray(np.asarray(inputs["L1_ffn_Wg"], np.float32).T)
    com = dict(
        w1l=_sbuf_chunks(w1T, DCH),
        b1l=np.ascontiguousarray(
            np.asarray(inputs["L0_ffn_b1"], np.float32).reshape(HCH, 128).T),
        w2l=_sbuf_chunks(w2T, HCH),
        b2=np.asarray(inputs["L0_ffn_b2"], np.float32).reshape(1, D),
        wqkvl=_sbuf_chunks(wqkvT, DCH),
        woutl=_sbuf_chunks(woutT, DCH),
        gaml=np.ascontiguousarray(
            np.asarray(inputs["L0_attn_gamma"], np.float32).reshape(DCH, 128).T),
        betl=np.ascontiguousarray(
            np.asarray(inputs["L0_attn_beta"], np.float32).reshape(DCH, 128).T),
        wgl=_sbuf_chunks(wgT, DCH),
        ident=np.eye(128, dtype=np.float32),
    )
    kk, qq = np.meshgrid(np.arange(128), np.arange(128), indexing="ij")
    mprev = (qq <= kk).astype(np.float32)
    mnext = (qq >= kk).astype(np.float32)
    zeros = np.zeros((128, 128), np.float32)
    x = np.ascontiguousarray(np.asarray(inputs["L0_attn_x"], np.float32)
                             .reshape(NTOTAL, D))
    maps = []
    for j in range(NCORE):
        lo = j * TOK - HALO
        hi = j * TOK + TOK + HALO
        xhj = np.zeros((TOKH, D), np.float32)
        s0 = max(lo, (j // 2) * 2 * TOK)
        s1 = min(hi, (j // 2 + 1) * 2 * TOK)
        xhj[s0 - lo: s1 - lo] = x[s0:s1]
        mp = dict(com)
        mp["xh"] = xhj
        mp["mprev"] = mprev
        mp["mnext"] = mnext
        mp["mfirst"] = zeros if j % 2 == 0 else mprev
        mp["mlast"] = zeros if j % 2 == 1 else mnext
        maps.append(mp)
    return maps


def run_l0(inputs, phases=99):
    nc = build_l0(phases)
    maps = _l0_host_inputs(inputs)
    res = run_bass_kernel_spmd(nc, maps, core_ids=list(range(NCORE)))
    x3 = np.concatenate([np.asarray(r["x3_out"]) for r in res.results], 0)
    xn = np.concatenate([np.asarray(r["xn_out"]) for r in res.results], 0)
    scs = np.concatenate([np.asarray(r["sc_out"]) for r in res.results], 1)
    return x3, xn, scs


def kernel(**inputs):
    raise NotImplementedError("under construction")


NBIS = 34
BIGR = 60000.0
HCH_M = HMOE // 128


def build_moe():
    nc = bass.Bass("TRN2", target_bir_lowering=False)
    inp = lambda n, sh, dt=F32: nc.dram_tensor(n, sh, dt, kind="ExternalInput").ap()
    out_ = lambda n, sh, dt=F32: nc.dram_tensor(n, sh, dt, kind="ExternalOutput").ap()

    scfl = inp("scfl", [128, 1024])
    xnf = inp("xnf", [NTOTAL, D], BF16)
    w1e = inp("w1e", [128, DCH * HMOE])
    w3e = inp("w3e", [128, DCH * HMOE])
    w2e = inp("w2e", [128, HCH_M * D])
    grpm = inp("grpm", [128, 8])
    gbt = inp("gbt", [8, 128])
    l16 = inp("l16", [128, 128])
    bneg = inp("bneg", [2, 128])
    bpos = inp("bpos", [2, 128])
    rowsel = inp("rowsel", [128, 1], I32)
    ident = inp("ident", [128, 128])

    y_out = out_("y_out", [CAP, D])
    rkw_out = out_("rkw_out", [2, 128, 1024])
    yo = y_out.rearrange("(t p) d -> p t d", p=128)

    rank_dram = nc.dram_tensor("rank_dram", [1024, 128], F32, kind="Internal").ap()
    xg_dram = nc.dram_tensor("xg_dram", [CAP, D], BF16, kind="Internal").ap()

    m = Mem(nc)
    Off = [0]
    def alloc(name, shape, dt=F32):
        Off[0] = (Off[0] + 31) // 32 * 32
        t = m.at(name, Off[0], shape, dt)
        Off[0] += int(np.prod(shape[1:])) * mybir.dt.size(dt)
        return t

    scs = alloc("scs", [128, 1024])
    w1s = alloc("w1s", [128, DCH * HMOE], BF16)
    w3s = alloc("w3s", [128, DCH * HMOE], BF16)
    w2s = alloc("w2s", [128, HCH_M * D], BF16)
    grs = alloc("grs", [128, 8])
    gbs = alloc("gbs", [8, 128])
    l16s = alloc("l16s", [128, 128])
    bns = alloc("bns", [2, 128])
    bps = alloc("bps", [2, 128])
    idn = alloc("idn", [128, 128], BF16)
    idf = alloc("idf", [128, 128])
    rsel = alloc("rsel", [128, 1], I32)
    lo = alloc("lo", [128, 1])
    hi = alloc("hi", [128, 1])
    mid = alloc("mid", [128, 1])
    cnt = alloc("cnt", [128, 1])
    cnt8 = alloc("cnt8", [8, 1])
    selm = alloc("selm", [128, 1], I32)
    selm2 = alloc("selm2", [128, 1], I32)
    mask = alloc("mask", [128, 1024])
    ztile = alloc("ztile", [128, 1024])
    cums = alloc("cums", [128, 1024])
    rank = alloc("rank", [128, 1024])
    wts = alloc("wts", [128, 1024])
    exm = alloc("exm", [128, 1024])
    tot = alloc("tot", [128, 1])
    offs = alloc("offs", [128, 1])
    nth = alloc("nth", [128, 1])
    rs1 = alloc("rs1", [128, 1])
    mnmx = alloc("mnmx", [128, 2])
    rb2 = alloc("rb2", [2, 128])
    gmx = alloc("gmx", [2, 1])
    rrs2 = alloc("rrs2", [128, 128])
    rrs = alloc("rrs", [128, 128])
    rri = alloc("rri", [128, 128], I32)
    idxcs = [alloc(f"idxc{r}", [128, 1], I32) for r in range(8)]
    xts = [alloc(f"xt{r}", [128, D], BF16) for r in range(8)]
    xgr = alloc("xgr", [128, D], BF16)
    xgt = alloc("xgt", [128, DCH * 512], BF16)
    ha = alloc("ha", [128, 512], BF16)
    hb = alloc("hb", [128, 512], BF16)
    hq = alloc("hq", [128, HCH_M * 512], BF16)
    ev = alloc("ev", [128, D])

    ptf = nc.alloc_psum_tensor("ptf", [128, 128], F32)
    ptb = nc.alloc_psum_tensor("ptb", [128, 128], BF16)
    pa = nc.alloc_psum_tensor("pa", [128, 512], F32)
    pb = nc.alloc_psum_tensor("pb", [128, 512], F32)
    pc = nc.alloc_psum_tensor("pc", [128, 512], F32)
    pd = nc.alloc_psum_tensor("pd", [128, 512], F32)
    pe_ = nc.alloc_psum_tensor("pe", [128, 512], F32)
    pf = nc.alloc_psum_tensor("pf", [128, 512], F32)

    p = Prog(nc)

    bcreg = nc.alloc_register(mybir.EngineType.Pool, "bcap")
    tbc0 = p.op("gpsimd", lambda g: g.reg_mov(bcreg, CAP - 1))

    tsc = p.dma("sync", scs[:], scfl)
    p.dma("sync", grs[:], grpm)
    p.dma("sync", gbs[:], gbt)
    p.dma("sync", l16s[:], l16)
    p.dma("sync", bns[:], bneg)
    p.dma("sync", bps[:], bpos)
    p.dma("sync", rsel[:], rowsel)
    tid = p.dma("sync", idf[:], ident)
    p.dma("gpsimd", idn[:], ident)
    p.dma("gpsimd", w1s[:], w1e)
    p.dma("gpsimd", w3s[:], w3e)
    tw = p.dma("gpsimd", w2s[:], w2e)
    tz = p.op("vector", lambda v: v.memset(ztile[:], 0.0))

    p.op("vector", lambda v: v.tensor_reduce(mnmx[:, 0:1], scs[:], axis=AX.X,
                                             op=ALU.min, negate=True), waits=[tsc])
    t2 = p.op("vector", lambda v: v.reduce_max(mnmx[:, 1:2], scs[:], axis=AX.X))
    ttr = p.op("tensor", lambda t: t.transpose(ptf[0:2, 0:128], mnmx[:], idf[:]),
               waits=[t2, tid])
    t3 = p.op("vector", lambda v: v.tensor_copy(rb2[:], ptf[0:2, 0:128]),
              waits=[ttr])
    t4 = p.op("vector", lambda v: v.reduce_max(gmx[:], rb2[:], axis=AX.X))
    tb1 = p.op("tensor", lambda t: t.matmul(pa[:, 0:1], bns[:], gmx[:],
                                            start=True, stop=True), waits=[t4])
    tb2 = p.op("tensor", lambda t: t.matmul(pb[:, 0:1], bps[:], gmx[:],
                                            start=True, stop=True))
    tl = p.op("vector", lambda v: v.tensor_scalar(lo[:], pa[:, 0:1], 1.0, None,
                                                  ALU.subtract), waits=[tb1])
    th = p.op("vector", lambda v: v.tensor_scalar(hi[:], pb[:, 0:1], 1.0, None,
                                                  ALU.add), waits=[tb2])

    tcn8 = None
    tbc = th
    for it in range(NBIS):
        p.op("vector", lambda v: v.tensor_add(mid[:], lo[:], hi[:]),
             waits=[tl, th, tbc])
        p.op("vector", lambda v: v.tensor_scalar(mid[:], mid[:], 0.5, None,
                                                 ALU.mult))
        tcm = p.op("vector", lambda v: v.tensor_scalar(mask[:], scs[:], mid[:],
                                                       None, ALU.is_ge, ALU.add,
                                                       accum_out=cnt[:]))
        tg = p.op("tensor", lambda t: t.matmul(pa[0:8, 0:1], grs[:], cnt[:],
                                               start=True, stop=True),
                  waits=[tcm, tcn8])
        tcn8 = p.op("vector", lambda v: v.tensor_copy(cnt8[:], pa[0:8, 0:1]),
                    waits=[tg])
        tg2 = p.op("tensor", lambda t: t.matmul(pb[:, 0:1], gbs[:], cnt8[:],
                                                start=True, stop=True),
                   waits=[tcn8, tbc])
        tsl = p.op("vector", lambda v: v.tensor_scalar(selm[:], pb[:, 0:1],
                                                       float(CAP), None,
                                                       ALU.is_ge), waits=[tg2])
        tsl2 = p.op("vector", lambda v: v.tensor_scalar(selm2[:], pb[:, 0:1],
                                                        float(CAP), None,
                                                        ALU.is_lt))
        tl = p.op("vector", lambda v: v.copy_predicated(lo[:], selm[:], mid[:]))
        th = p.op("vector", lambda v: v.copy_predicated(hi[:], selm2[:], mid[:]))
        tbc = tsl2

    tmask = p.op("vector", lambda v: v.tensor_scalar(mask[:], scs[:], lo[:],
                                                     None, ALU.is_ge, ALU.add,
                                                     accum_out=tot[:]),
                 waits=[tl])
    tnth = p.op("vector", lambda v: v.tensor_scalar(nth[:], lo[:], -1.0, None,
                                                    ALU.mult))
    tex = p.op("scalar", lambda s: s.activation(exm[:], scs[:], AF.Exp,
                                                bias=nth[:]), waits=[tnth])
    tex2 = p.op("vector", lambda v: v.tensor_mul(exm[:], exm[:], mask[:]),
                waits=[tex, tmask])
    trs = p.op("vector", lambda v: v.reduce_sum(rs1[:], exm[:], axis=AX.X))
    tg = p.op("tensor", lambda t: t.matmul(pa[0:8, 0:1], grs[:], rs1[:],
                                           start=True, stop=True),
              waits=[trs, tcn8])
    tcn8 = p.op("vector", lambda v: v.tensor_copy(cnt8[:], pa[0:8, 0:1]),
                waits=[tg])
    tg2 = p.op("tensor", lambda t: t.matmul(pb[:, 0:1], gbs[:], cnt8[:],
                                            start=True, stop=True),
               waits=[tcn8, tsl])
    trc = p.op("vector", lambda v: v.reciprocal(rs1[:], pb[:, 0:1]), waits=[tg2])
    twt = p.op("vector", lambda v: v.tensor_scalar(wts[:], exm[:], rs1[:], None,
                                                   ALU.mult), waits=[trc])
    tcu = p.op("vector", lambda v: v.tensor_tensor_scan(cums[:], mask[:],
                                                        ztile[:], 0.0,
                                                        ALU.add, ALU.add),
               waits=[tz])
    tof = p.op("tensor", lambda t: t.matmul(pa[:, 0:1], l16s[:], tot[:],
                                            start=True, stop=True),
               waits=[tmask, tg2])
    tofe = p.op("vector", lambda v: v.tensor_copy(offs[:], pa[:, 0:1]),
                waits=[tof])
    tr1 = p.op("vector", lambda v: v.tensor_sub(rank[:], cums[:], mask[:]),
               waits=[tcu])
    tr2 = p.op("vector", lambda v: v.tensor_scalar(rank[:], rank[:], offs[:],
                                                   None, ALU.add), waits=[tofe])
    tr3 = p.op("vector", lambda v: v.tensor_scalar(exm[:], mask[:], -BIGR, BIGR,
                                                   ALU.mult, ALU.add))
    tr4 = p.op("vector", lambda v: v.tensor_add(rank[:], rank[:], exm[:]),
               waits=[tr3])
    p.op("vector", lambda v: v.tensor_scalar(cums[:], rank[:], -1.0,
                                             float(CAP), ALU.mult, ALU.add),
         waits=[tr4])
    p.op("vector", lambda v: v.tensor_scalar_max(cums[:], cums[:], 0.0))
    tq3 = p.op("vector", lambda v: v.tensor_scalar_min(cums[:], cums[:], 1.0))
    twt = p.op("vector", lambda v: v.tensor_tensor(wts[:], wts[:], cums[:],
                                                   ALU.mult), waits=[twt, tq3])
    p.dma("sync", rkw_out[0:1, :, :], rank[:], waits=[tr4])
    p.dma("sync", rkw_out[1:2, :, :], wts[:], waits=[twt])
    trd = p.dma("sync", rank_dram.rearrange("(a b) f -> a (b f)", a=128),
                rank[:], waits=[tr4])
    trg = p.op("gpsimd", lambda g: g.indirect_dma_start(
        rrs2[:], None, rank_dram, IndirectOffsetOnAxis(rsel[:], 0)),
        waits=[trd], dma=True)
    ttr2 = p.op("tensor", lambda t: t.transpose(ptf[:], rrs2[:], idf[:]),
                waits=[trg, ttr])
    trr = p.op("vector", lambda v: v.tensor_copy(rrs[:], ptf[:]), waits=[ttr2])
    tri = p.op("vector", lambda v: v.tensor_copy(rri[:], rrs[:]))

    tprev = [None] * 8
    tscat = None
    for T in range(NTOTAL // 128):
        r = T % 8
        xt = xts[r]
        idxc = idxcs[r]
        tld = p.dma("sync", xt[:], xnf[T * 128:(T + 1) * 128, :],
                    waits=[tprev[r]], slot=r)
        tcx = p.op("vector", lambda v, T=T, idxc=idxc: v.tensor_copy(
            idxc[:], rri[:, T:T + 1]), waits=[tri, tprev[r]])
        tscat = p.op("gpsimd", lambda g, xt=xt, idxc=idxc: g.indirect_dma_start(
            xg_dram, IndirectOffsetOnAxis(idxc[:], 0), xt[:], None,
            bounds_check=bcreg, oob_is_err=False),
            waits=[tld, tcx, tbc0], dma=True, slot=r)
        tprev[r] = tscat

    tldp = None
    tnext = [t for t in tprev if t is not None]
    prev_ha = None
    prev_hb = None
    prev_ev = {0: None, 1: None}
    hq_guard = None
    xgt_guard = None
    for si in range(CAP // 512):
        tz2 = None
        for k in range(4):
            tile = si * 4 + k
            tldp = p.dma("sync", xgr[:], xg_dram[tile * 128:(tile + 1) * 128, :],
                         waits=tnext + [xgt_guard, tz2], slot=1)
            tnext = []
            for c in range(DCH):
                ttp = p.op("tensor", lambda t, c=c: t.transpose(
                    ptb[:], xgr[:, c * 128:(c + 1) * 128], idn[:]),
                    waits=[tldp, tz2])
                tz2 = p.op("vector", lambda v, c=c, k=k: v.tensor_copy(
                    xgt[:, c * 512 + k * 128: c * 512 + k * 128 + 128], ptb[:]),
                    waits=[ttp])
        thq = None
        for hc in range(HCH_M):
            tma = None
            for dc in range(DCH):
                tma = p.op("tensor", lambda t, hc=hc, dc=dc: t.matmul(
                    pa[:, 0:512],
                    w1s[:, dc * HMOE + hc * 128: dc * HMOE + hc * 128 + 128],
                    xgt[:, dc * 512: dc * 512 + 512],
                    start=(dc == 0), stop=(dc == DCH - 1)),
                    waits=([tz2, tw, prev_ha, hq_guard] if dc == 0 else []))
            tha = p.op("scalar", lambda s: s.activation(ha[:], pa[:, 0:512],
                                                        AF.Silu),
                       waits=[tma, thq])
            prev_ha = tha
            tmb = None
            for dc in range(DCH):
                tmb = p.op("tensor", lambda t, hc=hc, dc=dc: t.matmul(
                    pb[:, 0:512],
                    w3s[:, dc * HMOE + hc * 128: dc * HMOE + hc * 128 + 128],
                    xgt[:, dc * 512: dc * 512 + 512],
                    start=(dc == 0), stop=(dc == DCH - 1)),
                    waits=([prev_hb] if dc == 0 else []))
            thb = p.op("vector", lambda v: v.tensor_copy(hb[:], pb[:, 0:512]),
                       waits=[tmb, thq])
            prev_hb = thb
            thq = p.op("vector", lambda v, hc=hc: v.tensor_tensor(
                hq[:, hc * 512:(hc + 1) * 512], ha[:], hb[:], ALU.mult),
                waits=[tha, thb])
        for k in range(4):
            tile = si * 4 + k
            pspair = [pc, pd] if k % 2 == 0 else [pe_, pf]
            tevs = []
            for n in range(2):
                ps2 = pspair[n]
                tm2 = None
                for hc in range(HCH_M):
                    tm2 = p.op("tensor", lambda t, hc=hc, n=n, k=k, ps2=ps2: t.matmul(
                        ps2[:, 0:512],
                        hq[:, hc * 512 + k * 128: hc * 512 + k * 128 + 128],
                        w2s[:, hc * D + n * 512: hc * D + n * 512 + 512],
                        start=(hc == 0), stop=(hc == HCH_M - 1)),
                        waits=([thq, prev_ev[k % 2]] if hc == 0 else []))
                tev = p.op("vector", lambda v, n=n, ps2=ps2: v.tensor_copy(
                    ev[:, n * 512:(n + 1) * 512], ps2[:, 0:512]),
                    waits=[tm2, prev_ev[1 - k % 2]])
                tevs.append(tev)
            tdy = p.dma("sync", yo[:, tile:tile + 1, :], ev[:], waits=tevs,
                        slot=2 + k % 2)
            prev_ev[k % 2] = tdy
            if k == 3:
                hq_guard = tm2
        xgt_guard = thq

    ends = p.heads()
    for e in ENGINES:
        for tok in ends:
            p.wait(e, tok)
    p.finalize()
    return nc


def _moe_host_inputs(inputs, xn_full, scores_full, layer):
    W1 = np.asarray(inputs["L1_ffn_W1"], np.float32)
    W2 = np.asarray(inputs["L1_ffn_W2"], np.float32)
    W3 = np.asarray(inputs["L1_ffn_W3"], np.float32)
    kk = np.arange(128)
    grpm = np.zeros((128, 8), np.float32)
    for e_ in range(8):
        grpm[e_ * 16:(e_ + 1) * 16, e_] = 1.0
    gbt = np.zeros((8, 128), np.float32)
    for e_ in range(8):
        gbt[e_, e_ * 16:(e_ + 1) * 16] = 1.0
    l16 = ((kk[:, None] // 16 == kk[None, :] // 16) &
           (kk[:, None] < kk[None, :])).astype(np.float32)
    bneg = np.zeros((2, 128), np.float32); bneg[0, :] = -1.0
    bpos = np.zeros((2, 128), np.float32); bpos[1, :] = 1.0
    com = dict(
        scfl=np.ascontiguousarray(scores_full.reshape(128, 1024)),
        xnf=xn_full,
        grpm=grpm, gbt=gbt, l16=l16, bneg=bneg, bpos=bpos,
        ident=np.eye(128, dtype=np.float32),
    )
    maps = []
    for j in range(NCORE):
        mp = dict(com)
        mp["w1e"] = _sbuf_chunks(np.ascontiguousarray(W1[j].T), DCH)
        mp["w3e"] = _sbuf_chunks(np.ascontiguousarray(W3[j].T), DCH)
        mp["w2e"] = _sbuf_chunks(np.ascontiguousarray(W2[j].T), HCH_M)
        mp["rowsel"] = (j * 128 + np.arange(128, dtype=np.int32))[:, None]
        maps.append(mp)
    return maps


def emit_combine(p, nc, xsb, y1f, rki, wls, bufs, idxcs, hwc, bcreg, deps,
                 tile_major=False):
    R = len(bufs)
    tz = [p.op("vector", lambda v, r=r: v.memset(bufs[r][:], 0.0), waits=deps)
          for r in range(R)]
    tadd = None
    tadd_hist = [None] * R
    tgat_hist = [None] * R
    tadd_tile = [None] * NTO
    for i in range(E * NTO):
        if tile_major:
            T, e_ = divmod(i, E)
        else:
            e_, T = divmod(i, NTO)
        r = i % R
        tcx = p.op("gpsimd", lambda g, e_=e_, T=T, r=r: g.tensor_copy(
            idxcs[r][:], rki[:, e_ * 16 + T: e_ * 16 + T + 1]),
            waits=deps + [tgat_hist[r]])
        tgat = p.op("gpsimd", lambda g, e_=e_, r=r: g.indirect_dma_start(
            bufs[r][:], None, y1f, IndirectOffsetOnAxis(idxcs[r][:], 0),
            element_offset=e_ * CAP * D, bounds_check=bcreg,
            oob_is_err=False), waits=[tcx, tz[r], tadd_hist[r]],
            dma=True, slot=r)
        tgat_hist[r] = tgat
        thw = p.op("vector", lambda v, e_=e_, T=T: v.tensor_scalar(
            hwc[:], wls[:, e_ * 16 + T: e_ * 16 + T + 1], 0.5, None,
            ALU.mult), waits=[tgat])
        tadd = p.op("vector", lambda v, T=T, r=r: v.scalar_tensor_tensor(
            xsb[:, T * D:(T + 1) * D], bufs[r][:], hwc[:],
            xsb[:, T * D:(T + 1) * D], ALU.mult, ALU.add), waits=[thw])
        tadd_hist[r] = tadd
        tadd_tile[T] = tadd
    return tadd, tadd_tile


def _combine_host(rkw, j):
    rk = rkw[0].reshape(E, NTOTAL)[:, j * TOK:(j + 1) * TOK]
    wv = rkw[1].reshape(E, NTOTAL)[:, j * TOK:(j + 1) * TOK]
    rkl = np.ascontiguousarray(
        rk.reshape(E, NTO, 128).transpose(2, 0, 1).reshape(128, 128))
    wl = np.ascontiguousarray(
        wv.reshape(E, NTO, 128).transpose(2, 0, 1).reshape(128, 128))
    return rkl.astype(np.float32), wl.astype(np.float32)



def build_att1a():
    nc = bass.Bass("TRN2", target_bir_lowering=False)
    inp = lambda n, sh, dt=F32: nc.dram_tensor(n, sh, dt, kind="ExternalInput").ap()
    out_ = lambda n, sh, dt=F32: nc.dram_tensor(n, sh, dt, kind="ExternalOutput").ap()

    x3 = inp("x3", [TOK, D])
    y1f = inp("y1f", [NTOTAL, D])
    rkl = inp("rkl", [128, 128])
    wl = inp("wl", [128, 128])
    wkl = inp("wkl", [128, DCH * D])
    wvl = inp("wvl", [128, DCH * D])
    bk = inp("bk", [1, D])
    bv = inp("bv", [1, D])
    ident = inp("ident", [128, 128])

    x4_out = out_("x4_out", [TOK, D])
    st_out = out_("st_out", [1025, D])
    x4o = x4_out.rearrange("(t p) d -> p t d", p=128)

    m = Mem(nc)
    Off = [0]
    def alloc(name, shape, dt=F32):
        Off[0] = (Off[0] + 31) // 32 * 32
        t = m.at(name, Off[0], shape, dt)
        Off[0] += int(np.prod(shape[1:])) * mybir.dt.size(dt)
        return t

    xsb = alloc("xsb", [128, NTO * D])
    kps = alloc("kps", [128, NTO * D], BF16)
    vps = alloc("vps", [128, NTO * D], BF16)
    wks = alloc("wks", [128, DCH * D], BF16)
    wvs = alloc("wvs", [128, DCH * D], BF16)
    rks = alloc("rks", [128, 128])
    rki = alloc("rki", [128, 128], I32)
    wls = alloc("wls", [128, 128])
    bufs = [alloc(f"buf{r}", [128, D]) for r in range(5)]
    idxcs = [alloc(f"idxc{r}", [128, 1], I32) for r in range(5)]
    hwc = alloc("hwc", [128, 1])
    znb2 = [alloc(f"znb{r}", [128, D], BF16) for r in range(2)]
    znt2 = [alloc(f"znt{r}", [128, DCH * 128], BF16) for r in range(2)]
    idn = alloc("idn", [128, 128], BF16)
    bks = alloc("bks", [1, D])
    bvs = alloc("bvs", [1, D])
    ones1 = alloc("ones1", [1, 128])
    onesc = alloc("onesc", [128, 1], BF16)
    phi1 = alloc("phi1", [128, 512])
    phi2 = alloc("phi2", [128, 512])
    sev = alloc("sev", [128, 512])
    bn12 = alloc("bn12", [128, 12])
    st2 = alloc("st2", [128, 2])
    rstd = alloc("rstd", [128, 1])
    epsc = alloc("epsc", [128, 1])
    sc = {"bn12": bn12, "st2": st2, "rstd": rstd, "nmr": None, "eps": epsc[:, 0:1]}

    pt = nc.alloc_psum_tensor("pt", [128, 128], BF16)
    pa = nc.alloc_psum_tensor("pa", [128, 512], F32)
    pb = nc.alloc_psum_tensor("pb", [128, 512], F32)
    pc = nc.alloc_psum_tensor("pc", [128, 512], F32)
    pd = nc.alloc_psum_tensor("pd", [128, 512], F32)
    pe_ = nc.alloc_psum_tensor("pe", [128, 512], F32)
    pf = nc.alloc_psum_tensor("pf", [128, 512], F32)

    p = Prog(nc)
    bcreg = nc.alloc_register(mybir.EngineType.Pool, "bcap")
    tb0 = p.op("gpsimd", lambda g: g.reg_mov(bcreg, CAP - 1))
    tx = p.dma("sync", xsb[:].rearrange("p (t d) -> p t d", d=D),
               x3.rearrange("(t p) d -> p t d", p=128))
    trk = p.dma("sync", rks[:], rkl)
    twl = p.dma("sync", wls[:], wl)
    tbk = p.dma("sync", bks[:], bk)
    tbv = p.dma("sync", bvs[:], bv)
    p.dma("gpsimd", idn[:], ident)
    p.dma("gpsimd", wks[:], wkl)
    tw = p.dma("gpsimd", wvs[:], wvl)
    p.op("vector", lambda v: v.memset(ones1[:], 1.0))
    p.op("vector", lambda v: v.memset(onesc[:], 1.0))
    p.op("vector", lambda v: v.memset(epsc[:], LN_EPS))
    tri = p.op("vector", lambda v: v.tensor_copy(rki[:], rks[:]), waits=[trk])

    tcomb, ttile = emit_combine(p, nc, xsb, y1f, rki, wls, bufs, idxcs, hwc,
                                bcreg, [tx, tri, twl, tb0], tile_major=True)

    tdx = None
    for T in range(NTO):
        tdx = p.dma("sync", x4o[:, T:T + 1, :], xsb[:, T * D:(T + 1) * D],
                    waits=[ttile[T]], slot=T % 4)

    tztl = None
    tztp = {0: None, 1: None}
    tphiK = {0: None, 1: None}
    tphiV = None
    for T in range(NTO):
        b = T % 2
        znbT = znb2[b]
        zntT = znt2[b]
        tln = emit_ln(p, sc, xsb[:, T * D:(T + 1) * D], znbT[:],
                      deps=[ttile[T], tztp[b]])
        for c in range(DCH):
            ttr = p.op("tensor", lambda t, c=c, znbT=znbT: t.transpose(
                pt[:], znbT[:, c * 128:(c + 1) * 128], idn[:]),
                waits=[tln, tztl])
            tztl = p.op("vector", lambda v, c=c, zntT=zntT: v.tensor_copy(
                zntT[:, c * 128:(c + 1) * 128], pt[:]), waits=[ttr])
        tztp[b] = tztl
        for n in range(2):
            psK = [[pa, pb], [pe_, pf]][b][n]
            tmk = None
            for dc in range(DCH):
                tmk = p.op("tensor", lambda t, dc=dc, n=n, psK=psK, zntT=zntT: t.matmul(
                    psK[:, 0:512], zntT[:, dc * 128:(dc + 1) * 128],
                    wks[:, dc * D + n * 512: dc * D + n * 512 + 512],
                    start=(dc == 0), stop=False),
                    waits=([tztl, tw, tphiK[b]] if dc == 0 else []))
            tmk = p.op("tensor", lambda t, n=n, psK=psK: t.matmul(
                psK[:, 0:512], ones1[:], bks[:, n * 512:(n + 1) * 512],
                start=False, stop=True), waits=[tbk])
            t1 = p.op("vector", lambda v, psK=psK: v.tensor_scalar_min(
                phi1[:], psK[:, 0:512], 0.0), waits=[tmk])
            t2 = p.op("scalar", lambda s: s.activation(phi1[:], phi1[:], AF.Exp),
                      waits=[t1])
            t3 = p.op("vector", lambda v, psK=psK: v.tensor_scalar_max(
                phi2[:], psK[:, 0:512], 0.0), waits=[t2])
            tphiK[b] = p.op("vector", lambda v, T=T, n=n: v.tensor_tensor(
                kps[:, T * D + n * 512: T * D + n * 512 + 512], phi1[:],
                phi2[:], ALU.add), waits=[t3])
        for n in range(2):
            psV = [pc, pd][n]
            tmv = None
            for dc in range(DCH):
                tmv = p.op("tensor", lambda t, dc=dc, n=n, psV=psV, zntT=zntT: t.matmul(
                    psV[:, 0:512], zntT[:, dc * 128:(dc + 1) * 128],
                    wvs[:, dc * D + n * 512: dc * D + n * 512 + 512],
                    start=(dc == 0), stop=False),
                    waits=([tztl, tbv, tphiV] if dc == 0 else []))
            tmv = p.op("tensor", lambda t, n=n, psV=psV: t.matmul(
                psV[:, 0:512], ones1[:], bvs[:, n * 512:(n + 1) * 512],
                start=False, stop=True), waits=[tbv])
            tphiV = p.op("vector", lambda v, T=T, n=n, psV=psV: v.tensor_copy(
                vps[:, T * D + n * 512: T * D + n * 512 + 512], psV[:, 0:512]),
                waits=[tmv])

    kv_done = tphiV
    tse = None
    for dc in range(DCH):
        for n in range(2):
            ps_ = [pa, pb][(dc * 2 + n) % 2]
            tms = None
            for T in range(NTO):
                tms = p.op("tensor", lambda t, dc=dc, n=n, T=T, ps_=ps_: t.matmul(
                    ps_[:, 0:512],
                    kps[:, T * D + dc * 128: T * D + dc * 128 + 128],
                    vps[:, T * D + n * 512: T * D + n * 512 + 512],
                    start=(T == 0), stop=(T == NTO - 1)),
                    waits=([kv_done, tse] if T == 0 else []))
            tev = p.op("vector", lambda v, ps_=ps_: v.tensor_copy(
                sev[:], ps_[:, 0:512]), waits=[tms])
            tse = p.dma("sync", st_out[dc * 128:(dc + 1) * 128,
                                       n * 512:(n + 1) * 512], sev[:],
                        waits=[tev])
    for n in range(2):
        tmz = None
        for T in range(NTO):
            tmz = p.op("tensor", lambda t, n=n, T=T: t.matmul(
                pc[0:1, 0:512], onesc[:],
                kps[:, T * D + n * 512: T * D + n * 512 + 512],
                start=(T == 0), stop=(T == NTO - 1)),
                waits=([kv_done, tse] if T == 0 else []))
        tev = p.op("vector", lambda v: v.tensor_copy(sev[0:1, 0:512],
                                                     pc[0:1, 0:512]),
                   waits=[tmz])
        tse = p.dma("sync", st_out[1024:1025, n * 512:(n + 1) * 512],
                    sev[0:1, 0:512], waits=[tev])

    ends = p.heads()
    for e in ENGINES:
        for tok in ends:
            p.wait(e, tok)
    p.finalize()
    return nc


def build_att1b():
    nc = bass.Bass("TRN2", target_bir_lowering=False)
    inp = lambda n, sh, dt=F32: nc.dram_tensor(n, sh, dt, kind="ExternalInput").ap()
    out_ = lambda n, sh, dt=F32: nc.dram_tensor(n, sh, dt, kind="ExternalOutput").ap()

    x4 = inp("x4", [TOK, D])
    stp = inp("stp", [1025, D])
    wql = inp("wql", [128, DCH * D])
    wkl = inp("wkl", [128, DCH * D])
    wvl = inp("wvl", [128, DCH * D])
    wol = inp("wol", [128, DCH * D])
    bq = inp("bq", [1, D])
    bk = inp("bk", [1, D])
    bv = inp("bv", [1, D])
    bo = inp("bo", [1, D])
    ue = inp("ue", [128, 128])
    wgl = inp("wgl", [128, DCH * E])
    ident = inp("ident", [128, 128])

    x5_out = out_("x5_out", [TOK, D])
    xn2_out = out_("xn2_out", [TOK, D], BF16)
    sc2_out = out_("sc2_out", [E, TOK])
    x5o = x5_out.rearrange("(t p) d -> p t d", p=128)
    xno = xn2_out.rearrange("(t p) d -> p t d", p=128)

    m = Mem(nc)
    Off = [0]
    def alloc(name, shape, dt=F32):
        Off[0] = (Off[0] + 31) // 32 * 32
        t = m.at(name, Off[0], shape, dt)
        Off[0] += int(np.prod(shape[1:])) * mybir.dt.size(dt)
        return t

    xtl = [alloc(f"xtl{i}", [128, D]) for i in range(2)]
    Sf = alloc("Sf", [128, DCH * D])
    Sb = alloc("Sb", [128, DCH * D], BF16)
    wqs = alloc("wqs", [128, DCH * D], BF16)
    wks = alloc("wks", [128, DCH * D], BF16)
    wvs = alloc("wvs", [128, DCH * D], BF16)
    wos = alloc("wos", [128, DCH * D], BF16)
    wgs = alloc("wgs", [128, DCH * E], BF16)
    Zr = alloc("Zr", [1, D])
    ueb = alloc("ueb", [128, 128], BF16)
    idn = alloc("idn", [128, 128], BF16)
    znt2 = [alloc(f"znt{i}", [128, DCH * 128], BF16) for i in range(2)]
    znb2 = [alloc(f"znb{i}", [128, D], BF16) for i in range(2)]
    qpt2 = [alloc(f"qpt{i}", [128, D], BF16) for i in range(2)]
    qptT2 = [alloc(f"qptT{i}", [128, DCH * 128], BF16) for i in range(2)]
    kptR = [alloc(f"kpt{i}", [128, D], BF16) for i in range(4)]
    kptTR = [alloc(f"kptT{i}", [128, DCH * 128], BF16) for i in range(4)]
    vtR = [alloc(f"vt{i}", [128, D], BF16) for i in range(4)]
    At2b = [alloc(f"At{i}", [128, 128], BF16) for i in range(2)]
    Apr = alloc("Apr", [128, 128], BF16)
    Ot2 = [alloc(f"Ot{i}", [128, D], BF16) for i in range(2)]
    OT2 = [alloc(f"OT{i}", [128, DCH * 128], BF16) for i in range(2)]
    scr = alloc("scr", [128, D])
    phi1 = alloc("phi1", [128, 512])
    phi2 = alloc("phi2", [128, 512])
    den = alloc("den", [128, 1])
    den2 = alloc("den2", [128, 1])
    rden = alloc("rden", [128, 1])
    bqs = alloc("bqs", [1, D])
    bks = alloc("bks", [1, D])
    bvs = alloc("bvs", [1, D])
    bos = alloc("bos", [1, D])
    ones1 = alloc("ones1", [1, 128])
    onesc = alloc("onesc", [128, 1], BF16)
    znbG = [alloc(f"znbG{i}", [128, D], BF16) for i in range(2)]
    znTg = alloc("znTg", [128, DCH * 128], BF16)
    scev = alloc("scev", [8, 512])
    bn12 = alloc("bn12", [128, 12])
    st2 = alloc("st2", [128, 2])
    rstd = alloc("rstd", [128, 1])
    epsc = alloc("epsc", [128, 1])
    sc = {"bn12": bn12, "st2": st2, "rstd": rstd, "eps": epsc[:, 0:1]}

    pt = nc.alloc_psum_tensor("pt", [128, 128], BF16)
    pa = nc.alloc_psum_tensor("pa", [128, 512], F32)
    pb = nc.alloc_psum_tensor("pb", [128, 512], F32)
    pc = nc.alloc_psum_tensor("pc", [128, 512], F32)
    pd = nc.alloc_psum_tensor("pd", [128, 512], F32)
    pe_ = nc.alloc_psum_tensor("pe", [128, 512], F32)
    pf = nc.alloc_psum_tensor("pf", [128, 512], F32)
    pg = nc.alloc_psum_tensor("pg", [128, 512], F32)

    p = Prog(nc)

    x4i = x4.rearrange("(t p) d -> p t d", p=128)
    tS = p.dma("sync", Sf[:].rearrange("p (c e) -> p c e", e=D),
               stp[0:1024, :].rearrange("(c p) e -> p c e", p=128))
    tZ = p.dma("sync", Zr[:], stp[1024:1025, :])
    tbq = p.dma("sync", bqs[:], bq)
    tbk = p.dma("sync", bks[:], bk)
    tbv = p.dma("sync", bvs[:], bv)
    tbo = p.dma("sync", bos[:], bo)
    p.dma("gpsimd", idn[:], ident)
    p.dma("gpsimd", ueb[:], ue)
    p.dma("gpsimd", wqs[:], wql)
    p.dma("gpsimd", wks[:], wkl)
    p.dma("gpsimd", wvs[:], wvl)
    p.dma("gpsimd", wos[:], wol)
    twg = p.dma("gpsimd", wgs[:], wgl)
    p.op("vector", lambda v: v.memset(ones1[:], 1.0))
    p.op("vector", lambda v: v.memset(onesc[:], 1.0))
    p.op("vector", lambda v: v.memset(epsc[:], LN_EPS))
    tSbl = None
    for cset in range(DCH):
        tSbl = p.op("scalar", lambda s, cset=cset: s.activation(
            Sb[:, cset * D:(cset + 1) * D], Sf[:, cset * D:(cset + 1) * D],
            AF.Copy), waits=[tS])
    tloads = p.heads()

    def proj(pspair, ws, brow, tbias, deps, phi, dst, src):
        tout = None
        for n in range(2):
            ps_ = pspair[n]
            tm = None
            for dc in range(DCH):
                tm = p.op("tensor", lambda t, dc=dc, n=n, ps_=ps_, ws=ws, src=src: t.matmul(
                    ps_[:, 0:512], src[:, dc * 128:(dc + 1) * 128],
                    ws[:, dc * D + n * 512: dc * D + n * 512 + 512],
                    start=(dc == 0), stop=False),
                    waits=(list(deps) if dc == 0 else []))
            tm = p.op("tensor", lambda t, n=n, ps_=ps_, brow=brow: t.matmul(
                ps_[:, 0:512], ones1[:], brow[:, n * 512:(n + 1) * 512],
                start=False, stop=True), waits=[tbias])
            if phi:
                t1 = p.op("vector", lambda v, ps_=ps_: v.tensor_scalar_min(
                    phi1[:], ps_[:, 0:512], 0.0), waits=[tm])
                t2 = p.op("scalar", lambda s: s.activation(phi1[:], phi1[:],
                                                           AF.Exp), waits=[t1])
                t3 = p.op("vector", lambda v, ps_=ps_: v.tensor_scalar_max(
                    phi2[:], ps_[:, 0:512], 0.0), waits=[t2])
                tout = p.op("vector", lambda v, n=n, dst=dst: v.tensor_tensor(
                    dst[:, n * 512:(n + 1) * 512], phi1[:], phi2[:], ALU.add),
                    waits=[t3])
            else:
                tout = p.op("vector", lambda v, n=n, ps_=ps_, dst=dst: v.tensor_copy(
                    dst[:, n * 512:(n + 1) * 512], ps_[:, 0:512]), waits=[tm])
        return tout

    pt_chain = [None]
    def transposes(src, dst, deps):
        tzt = pt_chain[0]
        for c in range(DCH):
            ttr = p.op("tensor", lambda t, c=c, src=src: t.transpose(
                pt[:], src[:, c * 128:(c + 1) * 128], idn[:]),
                waits=list(deps) + [tzt])
            tzt = p.op("vector", lambda v, c=c, dst=dst: v.tensor_copy(
                dst[:, c * 128:(c + 1) * 128], pt[:]), waits=[ttr])
        pt_chain[0] = tzt
        return tzt

    G = 4
    guard_num = {0: None, 1: None}
    guard_zr = None
    guard_pe = [None]
    xout = {0: None, 1: None}
    xfree = {0: None, 1: None}
    tSb_tok = [tSbl]
    tfold_mm = None
    tge = None
    tscdma = None
    txndma = None
    for T in range(NTO):
        b = T % 2
        r = T % 4
        g0 = (T // G) * G
        xt = xtl[b]
        txl = p.dma("sync", xt[:], x4i[:, T:T + 1, :],
                    waits=[xout[b], xfree[b]], slot=b)
        tln = emit_ln(p, sc, xt[:], znb2[b][:],
                      deps=[txl] + (tloads if T == 0 else []))
        tzt = transposes(znb2[b], znt2[b], [tln] + (tloads if T == 0 else []))
        tq = proj([pa, pb], wqs, bqs, tbq, [tzt], True, qpt2[b], znt2[b])
        tk = proj([pe_, pf], wks, bks, tbk,
                  [tzt, guard_pe[0]], True, kptR[r], znt2[b])
        tv = proj([pc, pd], wvs, bvs, tbv,
                  [tzt, guard_num[0], guard_num[1]], False, vtR[r], znt2[b])
        tqT = transposes(qpt2[b], qptT2[b], [tq])
        tkT = transposes(kptR[r], kptTR[r], [tk])
        tzc = {}
        for n in range(2):
            ps_ = [pe_, pf][n]
            p.op("tensor", lambda t, n=n, ps_=ps_, r=r: t.matmul(
                ps_[:, 0:512], ueb[:], kptR[r][:, n * 512:(n + 1) * 512],
                start=True, stop=False), waits=[tk, guard_zr])
            tzc[n] = p.op("tensor", lambda t, n=n, ps_=ps_: t.matmul(
                ps_[:, 0:512], ones1[:], Zr[:, n * 512:(n + 1) * 512],
                start=False, stop=True), waits=[tZ])
        td1 = p.op("vector", lambda v, b=b: v.scalar_tensor_tensor(
            scr[:, 0:512], qpt2[b][:, 0:512], 1.0, pe_[:, 0:512], ALU.mult,
            ALU.mult, accum_out=den[:]), waits=[tzc[0], tq])
        td2 = p.op("vector", lambda v, b=b: v.scalar_tensor_tensor(
            scr[:, 512:1024], qpt2[b][:, 512:1024], 1.0, pf[:, 0:512], ALU.mult,
            ALU.mult, accum_out=den2[:]), waits=[tzc[1]])
        p.op("vector", lambda v: v.tensor_add(den[:], den[:], den2[:]),
             waits=[td1, td2])
        p.op("vector", lambda v: v.tensor_scalar(den[:], den[:], 1e-6,
                                                 None, ALU.add))
        trd = p.op("vector", lambda v: v.reciprocal(rden[:], den[:]))
        guard_pe[0] = td2
        for n in range(2):
            tzs = p.op("tensor", lambda t, n=n, r=r: t.matmul(
                pg[0:1, 0:512], onesc[:], kptR[r][:, n * 512:(n + 1) * 512],
                start=True, stop=True), waits=[tk, guard_zr, tge])
            guard_zr = p.op("vector", lambda v, n=n: v.tensor_add(
                Zr[:, n * 512:(n + 1) * 512], Zr[:, n * 512:(n + 1) * 512],
                pg[0:1, 0:512]), waits=[tzs, td1 if n == 0 else td2])
        tA = None
        for dc in range(DCH):
            tA = p.op("tensor", lambda t, dc=dc, b=b, r=r: t.matmul(
                pa[:, 0:128], kptTR[r][:, dc * 128:(dc + 1) * 128],
                qptT2[b][:, dc * 128:(dc + 1) * 128],
                start=(dc == 0), stop=(dc == DCH - 1)),
                waits=([tkT, tqT, tv] if dc == 0 else []))
        tAm = p.op("vector", lambda v, b=b: v.tensor_tensor(
            At2b[b][:], pa[:, 0:128], ueb[:], ALU.mult), waits=[tA])
        tnum = {}
        tpair_evac = None
        pair_ev = {}
        for c in range(g0, T):
            tA2 = None
            for dc in range(DCH):
                tA2 = p.op("tensor", lambda t, dc=dc, b=b, c=c: t.matmul(
                    pa[:, 0:128], kptTR[c % 4][:, dc * 128:(dc + 1) * 128],
                    qptT2[b][:, dc * 128:(dc + 1) * 128],
                    start=(dc == 0), stop=(dc == DCH - 1)),
                    waits=([tAm, tpair_evac] if dc == 0 else []))
            tpair_evac = p.op("vector", lambda v: v.tensor_copy(
                Apr[:], pa[:, 0:128]), waits=[tA2])
            pair_ev[c] = tpair_evac
            for n in range(2):
                ps_ = [pc, pd][n]
                tnum[n] = p.op("tensor", lambda t, n=n, ps_=ps_, c=c, g0=g0: t.matmul(
                    ps_[:, 0:512], Apr[:], vtR[c % 4][:, n * 512:(n + 1) * 512],
                    start=(c == g0), stop=False),
                    waits=[tpair_evac] + ([guard_num[n]] if c == g0 else []))
        for n in range(2):
            ps_ = [pc, pd][n]
            for dc in range(DCH):
                tnum[n] = p.op("tensor", lambda t, dc=dc, n=n, ps_=ps_, b=b, g0=g0, T=T: t.matmul(
                    ps_[:, 0:512], qptT2[b][:, dc * 128:(dc + 1) * 128],
                    Sb[:, dc * D + n * 512: dc * D + n * 512 + 512],
                    start=(dc == 0 and T == g0), stop=False),
                    waits=(([tqT, guard_num[n]] if T == g0 else [tqT])
                           + tSb_tok if dc == 0 else []))
            tnum[n] = p.op("tensor", lambda t, n=n, ps_=ps_, b=b, r=r: t.matmul(
                ps_[:, 0:512], At2b[b][:], vtR[r][:, n * 512:(n + 1) * 512],
                start=False, stop=True), waits=[tAm, tv])
        tOt = None
        for n in range(2):
            ps_ = [pc, pd][n]
            tOt = p.op("scalar", lambda s, n=n, ps_=ps_, b=b: s.activation(
                Ot2[b][:, n * 512:(n + 1) * 512], ps_[:, 0:512], AF.Copy,
                scale=rden[:]), waits=[tnum[n], trd])
        guard_num[0] = tOt
        guard_num[1] = tOt
        tOT = transposes(Ot2[b], OT2[b], [tOt])
        guard_xadd = None
        for n in range(2):
            ps_ = [pa, pb][n]
            tm = None
            for dc in range(DCH):
                tm = p.op("tensor", lambda t, dc=dc, n=n, ps_=ps_, b=b: t.matmul(
                    ps_[:, 0:512], OT2[b][:, dc * 128:(dc + 1) * 128],
                    wos[:, dc * D + n * 512: dc * D + n * 512 + 512],
                    start=(dc == 0), stop=False),
                    waits=([tOT] if dc == 0 else []))
            tm = p.op("tensor", lambda t, n=n, ps_=ps_: t.matmul(
                ps_[:, 0:512], ones1[:], bos[:, n * 512:(n + 1) * 512],
                start=False, stop=True), waits=[tbo])
            guard_xadd = p.op("vector", lambda v, n=n, ps_=ps_, xt=xt: v.tensor_tensor(
                xt[:, n * 512:(n + 1) * 512], ps_[:, 0:512],
                xt[:, n * 512:(n + 1) * 512], ALU.add),
                waits=[tm, txl])
        xout[b] = p.dma("sync", x5o[:, T:T + 1, :], xt[:],
                        waits=[guard_xadd], slot=2)
        tlnG = emit_ln(p, sc, xt[:], znbG[b][:], deps=[guard_xadd])
        txndma = p.dma("sync", xno[:, T:T + 1, :], znbG[b][:],
                       waits=[tlnG], slot=3)
        tztG = transposes(znbG[b], znTg, [tlnG, tge])
        tgm = None
        for dc in range(DCH):
            tgm = p.op("tensor", lambda t_, dc=dc: t_.matmul(
                pg[0:8, 0:128], wgs[:, dc * E:(dc + 1) * E],
                znTg[:, dc * 128:(dc + 1) * 128],
                start=(dc == 0), stop=(dc == DCH - 1)),
                waits=([tztG, tge, twg, guard_zr] if dc == 0 else []))
        tge = p.op("vector", lambda v, T=T: v.tensor_copy(
            scev[0:8, (T % 4) * 128: (T % 4 + 1) * 128],
            pg[0:8, 0:128]), waits=[tgm] + ([tscdma] if T % 4 == 0 else []))
        if T % 4 == 3:
            tscdma = p.dma("sync", sc2_out[:, (T - 3) * 128: (T + 1) * 128],
                           scev[0:8, :], waits=[tge], slot=1)
        xfree[b] = tlnG
        if T % G == G - 1 and T != NTO - 1:
            tSb_tok = []
            tfa_prev = {0: guard_pe[0], 1: guard_pe[0]}
            tfa_last = None
            for dc in range(DCH):
                for n in range(2):
                    ps_ = [pe_, pf][n]
                    tmf = None
                    for ci in range(G):
                        cc = g0 + ci
                        tmf = p.op("tensor", lambda t, dc=dc, n=n, ps_=ps_, cc=cc, ci=ci: t.matmul(
                            ps_[:, 0:512], kptR[cc % 4][:, dc * 128:(dc + 1) * 128],
                            vtR[cc % 4][:, n * 512:(n + 1) * 512],
                            start=(ci == 0), stop=(ci == G - 1)),
                            waits=([tfa_prev[n], tnum[n]] if ci == 0 else []))
                    tfold_mm = tmf
                    tfa = p.op("vector", lambda v, dc=dc, n=n, ps_=ps_: v.tensor_tensor(
                        Sf[:, dc * D + n * 512: dc * D + n * 512 + 512],
                        ps_[:, 0:512],
                        Sf[:, dc * D + n * 512: dc * D + n * 512 + 512],
                        ALU.add), waits=[tmf])
                    tfa_prev[n] = tfa
                    tfa_last = tfa
                    tfe = p.op("scalar", lambda s, dc=dc, n=n: s.activation(
                        Sb[:, dc * D + n * 512: dc * D + n * 512 + 512],
                        Sf[:, dc * D + n * 512: dc * D + n * 512 + 512],
                        AF.Copy), waits=[tfa])
                    tSb_tok.append(tfe)
            guard_pe[0] = tfa_last

    ends = p.heads()
    for e in ENGINES:
        for tok in ends:
            p.wait(e, tok)
    p.finalize()
    return nc



def build_comb2():
    nc = bass.Bass("TRN2", target_bir_lowering=False)
    inp = lambda n, sh, dt=F32: nc.dram_tensor(n, sh, dt, kind="ExternalInput").ap()
    x5 = inp("x5", [TOK, D])
    y1f = inp("y1f", [NTOTAL, D])
    rkl = inp("rkl", [128, 128])
    wl = inp("wl", [128, 128])
    xf_out = nc.dram_tensor("xf_out", [TOK, D], F32, kind="ExternalOutput").ap()
    xfo = xf_out.rearrange("(t p) d -> p t d", p=128)

    m = Mem(nc)
    Off = [0]
    def alloc(name, shape, dt=F32):
        Off[0] = (Off[0] + 31) // 32 * 32
        t = m.at(name, Off[0], shape, dt)
        Off[0] += int(np.prod(shape[1:])) * mybir.dt.size(dt)
        return t

    xsb = alloc("xsb", [128, NTO * D])
    rks = alloc("rks", [128, 128])
    rki = alloc("rki", [128, 128], I32)
    wls = alloc("wls", [128, 128])
    bufs = [alloc(f"buf{r}", [128, D]) for r in range(8)]
    idxcs = [alloc(f"idxc{r}", [128, 1], I32) for r in range(8)]
    hwc = alloc("hwc", [128, 1])

    p = Prog(nc)
    bcreg = nc.alloc_register(mybir.EngineType.Pool, "bcap")
    tb0 = p.op("gpsimd", lambda g: g.reg_mov(bcreg, CAP - 1))
    tx = p.dma("sync", xsb[:].rearrange("p (t d) -> p t d", d=D),
               x5.rearrange("(t p) d -> p t d", p=128))
    trk = p.dma("sync", rks[:], rkl)
    twl = p.dma("sync", wls[:], wl)
    tri = p.op("vector", lambda v: v.tensor_copy(rki[:], rks[:]), waits=[trk])
    tcomb, ttile = emit_combine(p, nc, xsb, y1f, rki, wls, bufs, idxcs, hwc,
                                bcreg, [tx, tri, twl, tb0], tile_major=True)
    for T in range(NTO):
        p.dma("sync", xfo[:, T:T + 1, :], xsb[:, T * D:(T + 1) * D],
              waits=[ttile[T]], slot=T % 4)
    ends = p.heads()
    for e in ENGINES:
        for tok in ends:
            p.wait(e, tok)
    p.finalize()
    return nc


_NC_CACHE = {}


def _get(name, builder):
    if name not in _NC_CACHE:
        _NC_CACHE[name] = builder()
    return _NC_CACHE[name]


def kernel(**inputs):
    inputs = {k: np.ascontiguousarray(np.asarray(v)) for k, v in inputs.items()}
    cores = list(range(NCORE))

    nc = _get("l0", build_l0)
    res = run_bass_kernel_spmd(nc, _l0_host_inputs(inputs), core_ids=cores)
    x3 = [np.asarray(r["x3_out"]) for r in res.results]
    xn1 = np.concatenate([np.asarray(r["xn_out"]) for r in res.results], 0)
    sc1 = np.concatenate([np.asarray(r["sc_out"]) for r in res.results], 1)

    nc = _get("moe", build_moe)
    res = run_bass_kernel_spmd(nc, _moe_host_inputs(inputs, xn1, sc1, 1),
                               core_ids=cores)
    y1 = np.concatenate([np.asarray(r["y_out"]) for r in res.results], 0)
    rkw1 = np.asarray(res.results[0]["rkw_out"])

    nc = _get("att1a", build_att1a)
    WkT = np.ascontiguousarray(np.asarray(inputs["L1_attn_Wk"], np.float32).T)
    WvT = np.ascontiguousarray(np.asarray(inputs["L1_attn_Wv"], np.float32).T)
    maps = []
    for j in cores:
        rkl, wl = _combine_host(rkw1, j)
        maps.append(dict(
            x3=x3[j], y1f=y1, rkl=rkl, wl=wl,
            wkl=_sbuf_chunks(WkT, DCH), wvl=_sbuf_chunks(WvT, DCH),
            bk=np.asarray(inputs["L1_attn_bk"], np.float32).reshape(1, D),
            bv=np.asarray(inputs["L1_attn_bv"], np.float32).reshape(1, D),
            ident=np.eye(128, dtype=np.float32),
        ))
    res = run_bass_kernel_spmd(nc, maps, core_ids=cores)
    x4 = [np.asarray(r["x4_out"]) for r in res.results]
    st = [np.asarray(r["st_out"]) for r in res.results]

    nc = _get("att1b", build_att1b)
    WT = lambda w: _sbuf_chunks(np.ascontiguousarray(
        np.asarray(inputs[w], np.float32).T), DCH)
    kk, tt = np.meshgrid(np.arange(128), np.arange(128), indexing="ij")
    ue = (kk <= tt).astype(np.float32)
    zst = np.zeros((1025, D), np.float32)
    maps = []
    for j in cores:
        maps.append(dict(
            x4=x4[j], stp=(st[j - 1] if j % 2 == 1 else zst),
            wql=WT("L1_attn_Wq"), wkl=WT("L1_attn_Wk"),
            wvl=WT("L1_attn_Wv"), wol=WT("L1_attn_Wo"),
            bq=np.asarray(inputs["L1_attn_bq"], np.float32).reshape(1, D),
            bk=np.asarray(inputs["L1_attn_bk"], np.float32).reshape(1, D),
            bv=np.asarray(inputs["L1_attn_bv"], np.float32).reshape(1, D),
            bo=np.asarray(inputs["L1_attn_bo"], np.float32).reshape(1, D),
            ue=ue, ident=np.eye(128, dtype=np.float32),
            wgl=_sbuf_chunks(np.ascontiguousarray(
                np.asarray(inputs["L1_ffn_Wg"], np.float32).T), DCH),
        ))
    res = run_bass_kernel_spmd(nc, maps, core_ids=cores)
    x5 = [np.asarray(r["x5_out"]) for r in res.results]
    xn2 = np.concatenate([np.asarray(r["xn2_out"]) for r in res.results], 0)
    sc2 = np.concatenate([np.asarray(r["sc2_out"]) for r in res.results], 1)

    nc = _get("moe", build_moe)
    res = run_bass_kernel_spmd(nc, _moe_host_inputs(inputs, xn2, sc2, 1),
                               core_ids=cores)
    y2 = np.concatenate([np.asarray(r["y_out"]) for r in res.results], 0)
    rkw2 = np.asarray(res.results[0]["rkw_out"])

    nc = _get("comb2", build_comb2)
    maps = []
    for j in cores:
        rkl, wl = _combine_host(rkw2, j)
        maps.append(dict(x5=x5[j], y1f=y2, rkl=rkl, wl=wl))
    res = run_bass_kernel_spmd(nc, maps, core_ids=cores)
    out = np.concatenate([np.asarray(r["xf_out"]) for r in res.results], 0)
    return out.reshape(4, 4096, D)


# version 13 (final)
# speedup vs baseline: 1.0191x; 1.0063x over previous
import numpy as np
import ml_dtypes
import concourse.bass as bass
import concourse.mybir as mybir
from concourse.bass_utils import run_bass_kernel_spmd
from concourse.bass_core_rust import IndirectOffsetOnAxis

F32 = mybir.dt.float32
BF16 = mybir.dt.bfloat16
FP16 = mybir.dt.float16
FP8 = mybir.dt.float8e4
I32 = mybir.dt.int32
AF = mybir.ActivationFunctionType
ALU = mybir.AluOpType
AX = mybir.AxisListType

NCORE = 8
TOK = 2048
HALO = 128
TOKH = TOK + 2 * HALO
NT = TOKH // 128
NTO = TOK // 128
D = 1024
DCH = D // 128
HFF = 4096
HCH = HFF // 128
NH = 16
HD = 64
E = 8
CAP = 2048
HMOE = 2048
LN_EPS = 1e-5
NTOTAL = NCORE * TOK

ENGINES = ("sync", "scalar", "tensor", "vector", "gpsimd")
AUTO_SERIAL = {"vector", "scalar", "gpsimd"}


class Prog:

    def __init__(self, nc):
        self.nc = nc
        self.items = {e: [] for e in ENGINES}
        self.sems = {}
        self.cnt = {}
        self.hw = {e: {} for e in ENGINES}

    def _sem(self, name):
        if name not in self.sems:
            self.sems[name] = self.nc.alloc_semaphore(name)
            self.cnt[name] = 0
        return self.sems[name]

    def wait(self, eng, tok):
        if tok is None:
            return
        name, val = tok
        if val <= self.hw[eng].get(name, 0):
            return
        self.hw[eng][name] = val
        self.items[eng].append(("w", self._sem(name), val))

    def op(self, eng, fn, waits=(), dma=False, slot=0):
        sem_name = eng + (f"_d{slot}" if dma else "")
        if dma and self.cnt.get(sem_name, 0) > 0:
            self.wait(eng, (sem_name, self.cnt[sem_name]))
        if eng in AUTO_SERIAL:
            if not dma and self.cnt.get(sem_name, 0) > 0:
                self.wait(eng, (sem_name, self.cnt[sem_name]))
            if dma and self.cnt.get(eng, 0) > 0:
                self.wait(eng, (eng, self.cnt[eng]))
        for t in waits:
            self.wait(eng, t)
        h = self._sem(sem_name)
        amt = 16 if dma else 1
        self.cnt[sem_name] = self.cnt.get(sem_name, 0) + amt
        self.items[eng].append(("e", fn, h, amt))
        return (sem_name, self.cnt[sem_name])

    def dma(self, eng, out, in_, waits=(), slot=0, **kw):
        return self.op(eng, lambda g: g.dma_start(out, in_, **kw), waits=waits,
                       dma=True, slot=slot)

    def heads(self):
        return [(n, c) for n, c in self.cnt.items() if c > 0]

    def finalize(self):
        with self.nc.Block() as block:
            for e in ENGINES:
                items = self.items[e]
                if not items:
                    continue

                def runner(eng, items=items):
                    for it in items:
                        if it[0] == "w":
                            eng.wait_ge(it[1], it[2])
                        else:
                            it[1](eng).then_inc(it[2], it[3])

                getattr(block, e)(runner)


class Mem:

    def __init__(self, nc):
        self.nc = nc
        self.base = (nc.sbuf_base + 31) // 32 * 32
        self.top = nc.sbuf_top

    def at(self, name, off, shape, dt=F32):
        end = off + int(np.prod(shape[1:])) * mybir.dt.size(dt)
        assert self.base + end <= self.top, (name, off, end, self.top - self.base)
        return self.nc.alloc_sbuf_tensor_at(name, list(shape), dt,
                                            offset=self.base + off)


def _finish_early(nc, p, outs, srcs):
    x3_out, xn_out, sc_out = outs
    xsb, znb_g, scev = srcs
    x3o2 = x3_out.rearrange("(t p) d -> p t d", p=128)
    hh = p.heads()
    for t in range(1, NT - 1):
        p.dma("sync", x3o2[:, t - 1:t, :], xsb[:, t * D:(t + 1) * D],
              waits=(hh if t == 1 else []))
    ends = p.heads()
    for e in ENGINES:
        for tok in ends:
            p.wait(e, tok)
    p.finalize()
    return nc


def emit_ln(p, sc, tile_in, tile_out, deps=(), eps=LN_EPS):
    p.op("vector", lambda v: v.bn_stats(sc["bn12"][:, 0:6], tile_in[:, 0:512]),
         waits=deps)
    p.op("vector", lambda v: v.bn_stats(sc["bn12"][:, 6:12], tile_in[:, 512:1024]))
    t3 = p.op("vector", lambda v: v.bn_aggr(sc["st2"][:], sc["bn12"][:]))
    p.op("scalar", lambda s: s.activation(sc["rstd"][:], sc["st2"][:, 1:2],
                                          AF.Ln, bias=sc["eps"]), waits=[t3])
    t5 = p.op("scalar", lambda s: s.activation(sc["rstd"][:], sc["rstd"][:],
                                               AF.Exp, scale=-0.5))
    return p.op("vector", lambda v: v.tensor_scalar(tile_out[:], tile_in[:],
                                                    sc["st2"][:, 0:1],
                                                    sc["rstd"][:],
                                                    ALU.subtract, ALU.mult),
                waits=[t5])



def build_l0(phases=99, dbg=False, dbg_g=0, dbg_hp=0, dbg_lq=0):
    nc = bass.Bass("TRN2", target_bir_lowering=False)

    inp = lambda n, sh, dt=F32: nc.dram_tensor(n, sh, dt, kind="ExternalInput").ap()
    out_ = lambda n, sh, dt=F32: nc.dram_tensor(n, sh, dt, kind="ExternalOutput").ap()

    xh = inp("xh", [TOKH, D])
    w1l = inp("w1l", [128, DCH * HFF])
    b1l = inp("b1l", [128, HCH])
    w2l = inp("w2l", [128, HCH * D])
    b2 = inp("b2", [1, D])
    wqkvl = inp("wqkvl", [128, DCH * 3 * D])
    woutl = inp("woutl", [128, DCH * D])
    gaml = inp("gaml", [128, DCH])
    betl = inp("betl", [128, DCH])
    wgl = inp("wgl", [128, DCH * E])
    ident = inp("ident", [128, 128])
    mprev = inp("mprev", [128, 128])
    mnext = inp("mnext", [128, 128])
    mfirst = inp("mfirst", [128, 128])
    mlast = inp("mlast", [128, 128])

    x3_out = out_("x3_out", [TOK, D])
    xn_out = out_("xn_out", [TOK, D], BF16)
    sc_out = out_("sc_out", [E, TOK])
    x3o = x3_out.rearrange("(t p) d -> p t d", p=128)
    xno = xn_out.rearrange("(t p) d -> p t d", p=128)
    m = Mem(nc)
    xsb = m.at("xsb", 0, [128, NT * D])
    wA = m.at("wA", 73728, [128, 32768], BF16)
    R2, R3, R4 = 139264, 176128, 204288
    wB8 = m.at("wB8", R2, [128, 32768], FP8)
    znT6 = m.at("znT6", R2, [128, DCH * 768], BF16)
    Kf = m.at("Kf", R2 + 12288, [128, 8 * 768], BF16)
    Vt = m.at("Vt", R2 + 24576, [128, 6 * 1024], BF16)
    stag = m.at("stag", R3, [128, 3 * D])
    b2row = m.at("b2row", R3 + 12288, [1, D])
    b2q = m.at("b2q", R3 + 16384, [1, D], FP16)
    hq8 = m.at("hq8", R3, [128, HCH * 512], FP8)
    znTf = m.at("znTf", R3 + 16384, [128, DCH * 512], BF16)
    znb_f = m.at("znb_f", R2 + 32768, [128, D], BF16)
    fout = m.at("fout", R2 + 34816, [128, D], FP16)
    Qf = m.at("Qf", R3, [128, 8 * 256], BF16)
    atf = m.at("atf", R3 + 4096, [128, 2048], BF16)
    Pt = m.at("Pt", R3 + 8192, [128, 4 * 384], BF16)
    zn_a = m.at("zn_a", R3 + 11264, [128, D])
    att = m.at("att", R3 + 11264, [128, 2048], BF16)
    znb_a = m.at("znb_a", R3 + 15360, [128, D], BF16)
    vbrow = m.at("vbrow", R3 + 17408, [1, D], BF16)
    qkrow = m.at("qkrow", R3 + 19456, [1, 2048], BF16)
    onesw = m.at("onesw", R3 + 23552, [1, 512], BF16)
    rcl4 = m.at("rcl4", R3 + 24576, [128, 4])
    znbg2 = m.at("znbg2", R3, [128, D], BF16)
    znTg2 = m.at("znTg2", R3 + 2048, [128, DCH * 128], BF16)
    znb_g = m.at("znb_g", R3 + 4096, [128, D], BF16)
    znTg = m.at("znTg", R3 + 6144, [128, DCH * 128], BF16)
    scev = m.at("scev", R3 + 8192, [8, 512])
    b1s = m.at("b1s", R4, [128, HCH])
    b2tile = m.at("b2tile", R4 + 128, [128, D])
    gams = m.at("gams", R4 + 4224, [128, DCH])
    bets = m.at("bets", R4 + 4256, [128, DCH])
    idn = m.at("idn", R4 + 4288, [128, 128], BF16)
    masks = m.at("masks", R4 + 4544, [128, 4 * 128], BF16)
    wgs = m.at("wgs", R4 + 5568, [128, DCH * E], BF16)
    ones512 = m.at("ones512", R4 + 5696, [1, 512])
    ones1 = m.at("ones1", R4 + 7744, [1, 128])
    bn12 = m.at("bn12", R4 + 8256, [128, 12])
    st2 = m.at("st2", R4 + 8320, [128, 2])
    rstd = m.at("rstd", R4 + 8352, [128, 1])
    nmr = m.at("nmr", R4 + 8384, [128, 1])
    rcl = m.at("rcl", R4 + 8416, [128, 1])
    von = m.at("von", R4 + 8448, [128, 1], BF16)
    epsc = m.at("epsc", R4 + 8480, [128, 1])
    sc = {"bn12": bn12, "st2": st2, "rstd": rstd, "nmr": nmr,
          "eps": epsc[:, 0:1]}

    pt = nc.alloc_psum_tensor("pt", [128, 128], BF16)
    pa = nc.alloc_psum_tensor("pa", [128, 512], F32)
    pb = nc.alloc_psum_tensor("pb", [128, 512], F32)
    pc = nc.alloc_psum_tensor("pc", [128, 512], F32)
    pd = nc.alloc_psum_tensor("pd", [128, 512], F32)
    pe_ = nc.alloc_psum_tensor("pe", [128, 512], F32)
    pf = nc.alloc_psum_tensor("pf", [128, 512], F32)
    pg = nc.alloc_psum_tensor("pg", [128, 512], F32)

    p = Prog(nc)

    p.dma("sync", b1s[:], b1l)
    p.dma("sync", gams[:], gaml)
    p.dma("sync", bets[:], betl)
    p.dma("gpsimd", idn[:], ident)
    p.dma("gpsimd", masks[:, 0:128], mprev)
    p.dma("gpsimd", masks[:, 128:256], mnext)
    p.dma("gpsimd", masks[:, 256:384], mfirst)
    p.dma("gpsimd", masks[:, 384:512], mlast)
    p.dma("gpsimd", wgs[:], wgl)
    tb2 = p.dma("sync", b2row[:], b2)
    p.dma("sync", xsb[:].rearrange("p (t d) -> p t d", d=D),
          xh.rearrange("(t p) d -> p t d", p=128))
    p.dma("gpsimd", wA[:], w1l)
    p.dma("gpsimd", wB8[:], w2l, slot=1)
    p.op("vector", lambda v: v.memset(ones512[:], 1.0))
    p.op("vector", lambda v: v.memset(epsc[:], LN_EPS))
    p.op("vector", lambda v: v.memset(ones1[:], 1.0))
    p.op("vector", lambda v: v.memset(von[:], 1.0))
    tq = p.op("vector", lambda v: v.tensor_copy(b2q[:], b2row[:]), waits=[tb2])
    tq2 = p.op("vector", lambda v: v.tensor_copy(b2row[:], b2q[:]), waits=[tq])
    tbc = None
    for n in range(2):
        ta = p.op("tensor", lambda t, n=n: t.matmul(
            pg[:, 0:512], ones1[:], b2row[:, n * 512:(n + 1) * 512],
            start=True, stop=True), waits=[tq2, tbc])
        tbc = p.op("vector", lambda v, n=n: v.tensor_copy(
            b2tile[:, n * 512:(n + 1) * 512], pg[:, 0:512]), waits=[ta])
    setup_done = p.heads()
    if phases < 1:
        return _finish_early(nc, p, [x3_out, xn_out, sc_out], [xsb, znb_g, scev])

    def emit_ffn(t_lo, t_hi, start_deps):
        prev_mm1_evac = {0: None, 1: None}
        prev_mm2_add = {0: None, 1: None}
        hq_guard = None
        znT_guard = None
        deps0 = list(start_deps)
        tiles = list(range(t_lo, t_hi))
        for si in range(0, len(tiles), 4):
            slab = tiles[si:si + 4]
            nslab = len(slab) * 128
            tz = None
            for k, t in enumerate(slab):
                xt = xsb[:, t * D:(t + 1) * D]
                tln = emit_ln(p, sc, xt, znb_f[:],
                              deps=deps0 + [znT_guard, tz])
                deps0 = []
                for c in range(DCH):
                    ttr = p.op("tensor", lambda tt, c=c: tt.transpose(
                        pt[:], znb_f[:, c * 128:(c + 1) * 128], idn[:]),
                        waits=[tln, tz])
                    tz = p.op("vector", lambda v, c=c, k=k: v.tensor_copy(
                        znTf[:, c * 512 + k * 128: c * 512 + k * 128 + 128],
                        pt[:]), waits=[ttr])
            last_evac = None
            for hc in range(HCH):
                ps = [pa, pb][hc % 2]
                w0 = [tz, prev_mm1_evac[hc % 2], hq_guard]
                tm = None
                for dc in range(DCH):
                    tm = p.op("tensor", lambda t_, hc=hc, dc=dc, ps=ps, nslab=nslab: t_.matmul(
                        ps[:, 0:nslab],
                        wA[:, dc * HFF + hc * 128: dc * HFF + hc * 128 + 128],
                        znTf[:, dc * 512: dc * 512 + nslab],
                        start=(dc == 0), stop=(dc == DCH - 1)),
                        waits=(w0 if dc == 0 else []))
                tev = p.op("scalar", lambda s, hc=hc, ps=ps, nslab=nslab: s.activation(
                    hq8[:, hc * 512: hc * 512 + nslab], ps[:, 0:nslab],
                    AF.Relu, bias=b1s[:, hc:hc + 1]), waits=[tm])
                prev_mm1_evac[hc % 2] = tev
                last_evac = tev
            znT_guard = last_evac
            for k, t in enumerate(slab):
                pspair = [pc, pd] if k % 2 == 0 else [pe_, pf]
                for n in range(2):
                    ps2 = pspair[n]
                    tm = None
                    for hc in range(HCH):
                        tm = p.op("tensor", lambda t_, hc=hc, n=n, k=k, ps2=ps2: t_.matmul(
                            ps2[:, 0:512],
                            hq8[:, hc * 512 + k * 128: hc * 512 + k * 128 + 128],
                            wB8[:, hc * D + n * 512: hc * D + n * 512 + 512],
                            start=(hc == 0), stop=(hc == HCH - 1)),
                            waits=([last_evac, prev_mm2_add[k % 2]]
                                   if hc == 0 else []))
                    tq_ = p.op("vector", lambda v, n=n, ps2=ps2: v.tensor_tensor(
                        fout[:, n * 512:(n + 1) * 512], ps2[:, 0:512],
                        b2tile[:, n * 512:(n + 1) * 512], ALU.add), waits=[tm])
                    tadd = p.op("vector", lambda v, n=n, t=t: v.scalar_tensor_tensor(
                        xsb[:, t * D + n * 512: t * D + n * 512 + 512],
                        fout[:, n * 512:(n + 1) * 512], 0.5,
                        xsb[:, t * D + n * 512: t * D + n * 512 + 512],
                        ALU.mult, ALU.add), waits=[tq_])
                    prev_mm2_add[k % 2] = tadd
                hq_guard = prev_mm2_add[k % 2]
        return p.heads()

    ffn1_done = emit_ffn(0, NT, setup_done)
    if phases < 2:
        return _finish_early(nc, p, [x3_out, xn_out, sc_out], [xsb, znb_g, scev])

    qb_ps = [pa, pb, pc, pd, pe_, pf]
    tqb = [None] * 6
    tfold = None
    for dc in range(DCH):
        tst = p.dma("sync", stag[:], wqkvl[:, dc * 3 * D:(dc + 1) * 3 * D],
                    waits=(ffn1_done if dc == 0 else [tfold]))
        for n in range(6):
            tqb[n] = p.op("tensor", lambda t, n=n, dc=dc: t.matmul(
                qb_ps[n][0:1, 0:512], bets[:, dc:dc + 1],
                stag[:, n * 512:(n + 1) * 512],
                start=(dc == 0), stop=(dc == DCH - 1)), waits=[tst])
        tfold = p.op("vector", lambda v, dc=dc: v.tensor_scalar(
            wA[:, dc * 3 * D:(dc + 1) * 3 * D], stag[:],
            gams[:, dc:dc + 1], None, ALU.mult), waits=[tst, tqb[5]])
    tqk = None
    for n in range(4):
        tqk = p.op("vector", lambda v, n=n: v.tensor_copy(
            qkrow[0:1, n * 512:(n + 1) * 512], qb_ps[n][0:1, 0:512]),
            waits=[tqb[n], tqk])
    tvb = p.op("vector", lambda v: v.tensor_copy(vbrow[0:1, 0:512],
                                                 pe_[0:1, 0:512]),
               waits=[tqb[4], tqk])
    tvb = p.op("vector", lambda v: v.tensor_copy(vbrow[0:1, 512:1024],
                                                 pf[0:1, 0:512]),
               waits=[tqb[5], tvb])
    tqbc = tqk
    twout = p.dma("gpsimd", wA[:, 24576:32768], woutl, waits=ffn1_done)
    tone = p.op("vector", lambda v: v.memset(onesw[:], 1.0), waits=ffn1_done)
    if phases < 3:
        return _finish_early(nc, p, [x3_out, xn_out, sc_out], [xsb, znb_g, scev])

    prev_op_t = {}
    tevac_reg = [None, None]
    texp_reg = [None, None]
    trc_prev = None
    tact_reg = [None, None]
    tVreg = {0: None, 1: None}
    tpn = None
    ptch = [None]
    setup_tok = [tfold, tqbc, tvb, twout, tone]
    tprojV = {}
    attn_first = True
    for q0 in range(1, NTO, 2):
        newtiles = list(range(q0 + 1, q0 + 3)) if q0 > 1 else [0, 1, 2, 3]
        for t in newtiles:
            xt = xsb[:, t * D:(t + 1) * D]
            t1 = emit_ln(p, sc, xt, zn_a[:],
                         deps=(setup_tok if attn_first else []) + [tpn])
            attn_first = False
            t2 = emit_ln(p, sc, zn_a[:], znb_a[:], deps=[t1])
            for c in range(DCH):
                ttr = p.op("tensor", lambda tt, c=c: tt.transpose(
                    pt[:], znb_a[:, c * 128:(c + 1) * 128], idn[:]),
                    waits=[t2, ptch[0]])
                tpn = p.op("vector", lambda v, c=c, t=t: v.tensor_copy(
                    znT6[:, c * 768 + (t % 6) * 128: c * 768 + (t % 6) * 128 + 128],
                    pt[:]), waits=[ttr])
                ptch[0] = tpn
            if 1 <= t <= NT - 2:
                tpn = p.op("vector", lambda v, xt=xt: v.tensor_add(
                    xt, xt, zn_a[:]), waits=[t2, tpn])
        runs = []
        for t in newtiles:
            if runs and runs[-1][0] + len(runs[-1][1]) == t % 6:
                runs[-1][1].append(t)
            else:
                runs.append([t % 6, [t]])
        qruns = []
        for q in (q0, q0 + 1):
            if qruns and qruns[-1][0] + len(qruns[-1][1]) == q % 6:
                qruns[-1][1].append(q)
            else:
                qruns.append([q % 6, [q]])
        pi = 0
        for hp in range(NH // 2):
            for s0, tl in runs:
                W = 128 * len(tl)
                psK = [pc, pd][pi % 2]
                tm = None
                for dc in range(DCH):
                    tm = p.op("tensor", lambda t_, dc=dc, hp=hp, s0=s0, W=W, psK=psK: t_.matmul(
                        psK[:, 0:W],
                        wA[:, dc * 3 * D + D + hp * 128: dc * 3 * D + D + hp * 128 + 128],
                        znT6[:, dc * 768 + s0 * 128: dc * 768 + s0 * 128 + W],
                        start=(dc == 0), stop=False),
                        waits=([tpn, tevac_reg[pi % 2], texp_reg[pi % 2],
                                prev_op_t.get(id(psK))] if dc == 0 else []))
                tm = p.op("tensor", lambda t_, hp=hp, W=W, psK=psK: t_.matmul(
                    psK[:, 0:W], qkrow[0:1, (8 + hp) * 128: (9 + hp) * 128],
                    onesw[0:1, 0:W], start=False, stop=True), waits=[tqbc, tone])
                tevac_reg[pi % 2] = p.op("vector", lambda v, hp=hp, s0=s0, W=W, psK=psK: v.tensor_copy(
                    Kf[:, hp * 768 + s0 * 128: hp * 768 + s0 * 128 + W],
                    psK[:, 0:W]), waits=[tm])
                pi += 1
            for s0, ql in qruns:
                W = 128 * len(ql)
                psQ = [pc, pd][pi % 2]
                tm = None
                for dc in range(DCH):
                    tm = p.op("tensor", lambda t_, dc=dc, hp=hp, s0=s0, W=W, psQ=psQ: t_.matmul(
                        psQ[:, 0:W],
                        wA[:, dc * 3 * D + hp * 128: dc * 3 * D + hp * 128 + 128],
                        znT6[:, dc * 768 + s0 * 128: dc * 768 + s0 * 128 + W],
                        start=(dc == 0), stop=False),
                        waits=([tpn, tevac_reg[pi % 2], texp_reg[pi % 2],
                                prev_op_t.get(id(psQ))] if dc == 0 else []))
                tm = p.op("tensor", lambda t_, hp=hp, W=W, psQ=psQ: t_.matmul(
                    psQ[:, 0:W], qkrow[0:1, hp * 128: (hp + 1) * 128],
                    onesw[0:1, 0:W], start=False, stop=True), waits=[tqbc, tone])
                tevac_reg[pi % 2] = p.op("vector", lambda v, hp=hp, q0=q0, ql=ql, W=W, psQ=psQ: v.tensor_copy(
                    Qf[:, hp * 256 + (ql[0] - q0) * 128: hp * 256 + (ql[0] - q0) * 128 + W],
                    psQ[:, 0:W]), waits=[tm])
                pi += 1
        for t in newtiles:
            for n in range(2):
                psV = [pe_, pf][n]
                tm = None
                for dc in range(DCH):
                    tm = p.op("tensor", lambda t_, dc=dc, n=n, t=t, psV=psV: t_.matmul(
                        psV[:, 0:512],
                        znT6[:, dc * 768 + (t % 6) * 128: dc * 768 + (t % 6) * 128 + 128],
                        wA[:, dc * 3 * D + 2 * D + n * 512: dc * 3 * D + 2 * D + n * 512 + 512],
                        start=(dc == 0), stop=False),
                        waits=([tpn, prev_op_t.get(id(psV)), tVreg[n]]
                               if dc == 0 else []))
                tm = p.op("tensor", lambda t_, n=n, psV=psV: t_.matmul(
                    psV[:, 0:512], onesw[0:1, 0:128],
                    vbrow[0:1, n * 512:(n + 1) * 512],
                    start=False, stop=True), waits=[tvb, tone])
                tVreg[n] = p.op("vector", lambda v, n=n, t=t, psV=psV: v.tensor_copy(
                    Vt[:, (t % 6) * 1024 + n * 512: (t % 6) * 1024 + n * 512 + 512],
                    psV[:, 0:512]), waits=[tm, tprojV.get(t % 6)])
                tprojV[t % 6] = tVreg[n]
        oi = 0
        tact_last = None
        for hp in range(NH // 2):
            for hh in range(2):
                h = 2 * hp + hh
                pbase = hh * 64
                tPt = {}
                for ki, k in enumerate(range(q0 - 1, q0 + 3)):
                    qlo = max(k - 1, q0)
                    qhi = min(k + 1, q0 + 1)
                    W = (qhi - qlo + 1) * 128
                    psS = [pc, pd][ki % 2]
                    tsc = p.op("tensor", lambda t_, hp=hp, k=k, qlo=qlo, W=W, psS=psS, pbase=pbase, q0=q0: t_.matmul(
                        psS[:, 0:W],
                        Kf[pbase:pbase + 64, hp * 768 + (k % 6) * 128: hp * 768 + (k % 6) * 128 + 128],
                        Qf[pbase:pbase + 64, hp * 256 + (qlo - q0) * 128: hp * 256 + (qlo - q0) * 128 + W],
                        start=True, stop=True),
                        waits=[texp_reg[ki % 2], tevac_reg[ki % 2],
                               tevac_reg[1 - ki % 2]])
                    toff = (qlo - k + 1) * 128
                    texp = p.op("scalar", lambda s, ki=ki, toff=toff, W=W, psS=psS: s.activation(
                        Pt[:, ki * 384 + toff: ki * 384 + toff + W],
                        psS[:, 0:W], AF.Exp, scale=0.125), waits=[tsc])
                    texp_reg[ki % 2] = texp
                    tblk = texp
                    for q in range(qlo, qhi + 1):
                        if q == k:
                            continue
                        if q == k + 1:
                            mi = 2 if k == 0 else 0
                        else:
                            mi = 3 if k == NT - 1 else 1
                        tblk = p.op("vector", lambda v, ki=ki, q=q, k=k, mi=mi: v.tensor_tensor(
                            Pt[:, ki * 384 + (q - k + 1) * 128: ki * 384 + (q - k + 1) * 128 + 128],
                            Pt[:, ki * 384 + (q - k + 1) * 128: ki * 384 + (q - k + 1) * 128 + 128],
                            masks[:, mi * 128:(mi + 1) * 128], ALU.mult),
                            waits=[texp, tblk])
                    tPt[ki] = tblk
                for qi, q in enumerate((q0, q0 + 1)):
                    reg = hh * 2 + qi
                    tlm = None
                    for j, kk in enumerate(range(q - 1, q + 2)):
                        ki = kk - (q0 - 1)
                        tlm = p.op("tensor", lambda t_, ki=ki, q=q, kk=kk, j=j: t_.matmul(
                            pa[:, 0:1],
                            Pt[:, ki * 384 + (q - kk + 1) * 128: ki * 384 + (q - kk + 1) * 128 + 128],
                            von[:],
                            start=(j == 0), stop=(j == 2)),
                            waits=([tPt[ki], trc_prev] if j == 0 else [tPt[ki]]))
                    trc_prev = p.op("vector", lambda v, reg=reg: v.reciprocal(
                        rcl4[:, reg:reg + 1], pa[:, 0:1]), waits=[tlm])
                    trc = trc_prev
                    psO = [pb, pg][oi % 2]
                    tav = None
                    for j, kk in enumerate(range(q - 1, q + 2)):
                        ki = kk - (q0 - 1)
                        tav = p.op("tensor", lambda t_, ki=ki, q=q, kk=kk, j=j, h=h, psO=psO: t_.matmul(
                            psO[:, 0:64],
                            Pt[:, ki * 384 + (q - kk + 1) * 128: ki * 384 + (q - kk + 1) * 128 + 128],
                            Vt[:, (kk % 6) * 1024 + h * 64: (kk % 6) * 1024 + h * 64 + 64],
                            start=(j == 0), stop=(j == 2)),
                            waits=([tPt[ki], tprojV[kk % 6], tact_reg[oi % 2]]
                                   if j == 0 else [tPt[ki]]))
                    tact_last = p.op("scalar", lambda s, reg=reg, qi=qi, h=h, psO=psO: s.activation(
                        att[:, qi * 1024 + h * 64: qi * 1024 + h * 64 + 64],
                        psO[:, 0:64], AF.Copy,
                        scale=rcl4[:, reg:reg + 1]), waits=[tav, trc])
                    tact_reg[oi % 2] = tact_last
                    oi += 1
        tfq = {}
        for qi, q in enumerate((q0, q0 + 1)):
            tfq[qi] = []
            for c in range(DCH):
                ttr = p.op("tensor", lambda t_, qi=qi, c=c: t_.transpose(
                    pt[:], att[:, qi * 1024 + c * 128: qi * 1024 + c * 128 + 128],
                    idn[:]), waits=[tact_last, ptch[0]])
                tev = p.op("vector", lambda v, qi=qi, c=c: v.tensor_copy(
                    atf[:, qi * 1024 + c * 128: qi * 1024 + c * 128 + 128],
                    pt[:]), waits=[ttr])
                ptch[0] = tev
                tfq[qi].append(tev)
        for qi, q in enumerate((q0, q0 + 1)):
            pop = [pc, pd] if qi == 0 else [pe_, pf]
            for n in range(2):
                tm = None
                for hp in range(NH // 2):
                    tm = p.op("tensor", lambda t_, hp=hp, n=n, qi=qi, pop=pop: t_.matmul(
                        pop[n][:, 0:512],
                        atf[:, qi * 1024 + hp * 128: qi * 1024 + hp * 128 + 128],
                        wA[:, 24576 + hp * D + n * 512: 24576 + hp * D + n * 512 + 512],
                        start=(hp == 0), stop=(hp == NH // 2 - 1)),
                        waits=[tfq[qi][hp]] + ([prev_op_t.get(id(pop[n])),
                                                texp_reg[0], texp_reg[1]]
                                               if hp == 0 else []))
                prev_op_t[id(pop[n])] = p.op("vector", lambda v, n=n, q=q, pop=pop: v.tensor_tensor(
                    xsb[:, q * D + n * 512: q * D + n * 512 + 512],
                    pop[n][:, 0:512],
                    xsb[:, q * D + n * 512: q * D + n * 512 + 512],
                    ALU.add), waits=[tm])
    attn_done = p.heads()
    if phases < 5:
        return _finish_early(nc, p, [x3_out, xn_out, sc_out], [xsb, znb_g, scev])

    tw1b = p.dma("gpsimd", wA[:], w1l, waits=attn_done)
    tw2b = p.dma("gpsimd", wB8[:], w2l, waits=attn_done, slot=1)
    ffn2_done = emit_ffn(1, NT - 1, [tw1b, tw2b])
    if phases < 6:
        return _finish_early(nc, p, [x3_out, xn_out, sc_out], [xsb, znb_g, scev])

    tge = {0: None, 1: None}
    tztg = None
    tscdma = None
    txndma = {0: None, 1: None}
    for t in range(1, NT - 1):
        b = (t - 1) % 2
        znbg = [znb_g, znbg2][b]
        zntg = [znTg, znTg2][b]
        psg = [pa, pg][b]
        p.dma("sync", x3o[:, t - 1:t, :], xt_ := xsb[:, t * D:(t + 1) * D],
              waits=(ffn2_done if t == 1 else []), slot=2 + b)
        tln = emit_ln(p, sc, xt_, znbg[:],
                      deps=(ffn2_done if t == 1 else [txndma[b]]))
        txndma[b] = p.dma("sync", xno[:, t - 1:t, :], znbg[:], waits=[tln],
                          slot=b)
        for c in range(DCH):
            ttr = p.op("tensor", lambda tt, c=c, znbg=znbg: tt.transpose(
                pt[:], znbg[:, c * 128:(c + 1) * 128], idn[:]),
                waits=[tln, tztg])
            tztg = p.op("vector", lambda v, c=c, zntg=zntg: v.tensor_copy(
                zntg[:, c * 128:(c + 1) * 128], pt[:]), waits=[ttr])
        tg = None
        for dc in range(DCH):
            tg = p.op("tensor", lambda t_, dc=dc, psg=psg, zntg=zntg: t_.matmul(
                psg[0:8, 0:128], wgs[:, dc * E:(dc + 1) * E],
                zntg[:, dc * 128:(dc + 1) * 128],
                start=(dc == 0), stop=(dc == DCH - 1)),
                waits=([tztg, tge[b]] if dc == 0 else []))
        tge[b] = p.op("vector", lambda v, t=t, psg=psg: v.tensor_copy(
            scev[0:8, ((t - 1) % 4) * 128: ((t - 1) % 4 + 1) * 128],
            psg[0:8, 0:128]), waits=[tg] + ([tscdma] if (t - 1) % 4 <= 1 else []))
        if (t - 1) % 4 == 3:
            tscdma = p.dma("sync", sc_out[:, (t - 4) * 128: t * 128],
                           scev[0:8, :], waits=[tge[0], tge[1]], slot=2)
    ends = p.heads()
    for e in ENGINES:
        for tok in ends:
            p.wait(e, tok)
    p.finalize()
    return nc


def _fix_unused():
    pass



def _sbuf_chunks(a, nch):
    F = a.shape[1]
    return np.ascontiguousarray(
        a.reshape(nch, 128, F).transpose(1, 0, 2).reshape(128, nch * F))


def _l0_host_inputs(inputs):
    w1T = np.ascontiguousarray(np.asarray(inputs["L0_ffn_w1"], np.float32).T)
    w2T = np.ascontiguousarray(np.asarray(inputs["L0_ffn_w2"], np.float32).T)
    wqkvT = np.ascontiguousarray(np.asarray(inputs["L0_attn_w_qkv"], np.float32).T)
    woutT = np.ascontiguousarray(np.asarray(inputs["L0_attn_w_out"], np.float32).T)
    wgT = np.ascontiguousarray(np.asarray(inputs["L1_ffn_Wg"], np.float32).T)
    com = dict(
        w1l=_sbuf_chunks(w1T, DCH),
        b1l=np.ascontiguousarray(
            np.asarray(inputs["L0_ffn_b1"], np.float32).reshape(HCH, 128).T),
        w2l=_sbuf_chunks(w2T, HCH),
        b2=np.asarray(inputs["L0_ffn_b2"], np.float32).reshape(1, D),
        wqkvl=_sbuf_chunks(wqkvT, DCH),
        woutl=_sbuf_chunks(woutT, DCH),
        gaml=np.ascontiguousarray(
            np.asarray(inputs["L0_attn_gamma"], np.float32).reshape(DCH, 128).T),
        betl=np.ascontiguousarray(
            np.asarray(inputs["L0_attn_beta"], np.float32).reshape(DCH, 128).T),
        wgl=_sbuf_chunks(wgT, DCH),
        ident=np.eye(128, dtype=np.float32),
    )
    kk, qq = np.meshgrid(np.arange(128), np.arange(128), indexing="ij")
    mprev = (qq <= kk).astype(np.float32)
    mnext = (qq >= kk).astype(np.float32)
    zeros = np.zeros((128, 128), np.float32)
    x = np.ascontiguousarray(np.asarray(inputs["L0_attn_x"], np.float32)
                             .reshape(NTOTAL, D))
    maps = []
    for j in range(NCORE):
        lo = j * TOK - HALO
        hi = j * TOK + TOK + HALO
        xhj = np.zeros((TOKH, D), np.float32)
        s0 = max(lo, (j // 2) * 2 * TOK)
        s1 = min(hi, (j // 2 + 1) * 2 * TOK)
        xhj[s0 - lo: s1 - lo] = x[s0:s1]
        mp = dict(com)
        mp["xh"] = xhj
        mp["mprev"] = mprev
        mp["mnext"] = mnext
        mp["mfirst"] = zeros if j % 2 == 0 else mprev
        mp["mlast"] = zeros if j % 2 == 1 else mnext
        maps.append(mp)
    return maps


def run_l0(inputs, phases=99):
    nc = build_l0(phases)
    maps = _l0_host_inputs(inputs)
    res = run_bass_kernel_spmd(nc, maps, core_ids=list(range(NCORE)))
    x3 = np.concatenate([np.asarray(r["x3_out"]) for r in res.results], 0)
    xn = np.concatenate([np.asarray(r["xn_out"]) for r in res.results], 0)
    scs = np.concatenate([np.asarray(r["sc_out"]) for r in res.results], 1)
    return x3, xn, scs


def kernel(**inputs):
    raise NotImplementedError("under construction")


NBIS = 34
BIGR = 60000.0
HCH_M = HMOE // 128


def build_moe():
    nc = bass.Bass("TRN2", target_bir_lowering=False)
    inp = lambda n, sh, dt=F32: nc.dram_tensor(n, sh, dt, kind="ExternalInput").ap()
    out_ = lambda n, sh, dt=F32: nc.dram_tensor(n, sh, dt, kind="ExternalOutput").ap()

    scfl = inp("scfl", [128, 1024])
    xnf = inp("xnf", [NTOTAL, D], BF16)
    w1e = inp("w1e", [128, DCH * HMOE])
    w3e = inp("w3e", [128, DCH * HMOE])
    w2e = inp("w2e", [128, HCH_M * D])
    grpm = inp("grpm", [128, 8])
    gbt = inp("gbt", [8, 128])
    l16 = inp("l16", [128, 128])
    bneg = inp("bneg", [2, 128])
    bpos = inp("bpos", [2, 128])
    rowsel = inp("rowsel", [128, 1], I32)
    ident = inp("ident", [128, 128])

    y_out = out_("y_out", [CAP, D])
    rkw_out = out_("rkw_out", [2, 128, 1024])
    yo = y_out.rearrange("(t p) d -> p t d", p=128)

    rank_dram = nc.dram_tensor("rank_dram", [1024, 128], F32, kind="Internal").ap()
    xg_dram = nc.dram_tensor("xg_dram", [CAP, D], BF16, kind="Internal").ap()

    m = Mem(nc)
    Off = [0]
    def alloc(name, shape, dt=F32):
        Off[0] = (Off[0] + 31) // 32 * 32
        t = m.at(name, Off[0], shape, dt)
        Off[0] += int(np.prod(shape[1:])) * mybir.dt.size(dt)
        return t

    scs = alloc("scs", [128, 1024])
    w1s = alloc("w1s", [128, DCH * HMOE], BF16)
    w3s = alloc("w3s", [128, DCH * HMOE], BF16)
    w2s = alloc("w2s", [128, HCH_M * D], BF16)
    grs = alloc("grs", [128, 8])
    gbs = alloc("gbs", [8, 128])
    l16s = alloc("l16s", [128, 128])
    bns = alloc("bns", [2, 128])
    bps = alloc("bps", [2, 128])
    idn = alloc("idn", [128, 128], BF16)
    idf = alloc("idf", [128, 128])
    rsel = alloc("rsel", [128, 1], I32)
    lo = alloc("lo", [128, 1])
    hi = alloc("hi", [128, 1])
    mid = alloc("mid", [128, 1])
    cnt = alloc("cnt", [128, 1])
    cnt8 = alloc("cnt8", [8, 1])
    selm = alloc("selm", [128, 1], I32)
    selm2 = alloc("selm2", [128, 1], I32)
    mask = alloc("mask", [128, 1024])
    ztile = alloc("ztile", [128, 1024])
    cums = alloc("cums", [128, 1024])
    rank = alloc("rank", [128, 1024])
    wts = alloc("wts", [128, 1024])
    exm = alloc("exm", [128, 1024])
    tot = alloc("tot", [128, 1])
    offs = alloc("offs", [128, 1])
    nth = alloc("nth", [128, 1])
    rs1 = alloc("rs1", [128, 1])
    mnmx = alloc("mnmx", [128, 2])
    rb2 = alloc("rb2", [2, 128])
    gmx = alloc("gmx", [2, 1])
    rrs2 = alloc("rrs2", [128, 128])
    rrs = alloc("rrs", [128, 128])
    rri = alloc("rri", [128, 128], I32)
    idxcs = [alloc(f"idxc{r}", [128, 1], I32) for r in range(8)]
    xts = [alloc(f"xt{r}", [128, D], BF16) for r in range(8)]
    xgr = alloc("xgr", [128, D], BF16)
    xgt = alloc("xgt", [128, DCH * 512], BF16)
    ha = alloc("ha", [128, 512], BF16)
    hb = alloc("hb", [128, 512], BF16)
    hq = alloc("hq", [128, HCH_M * 512], BF16)
    ev = alloc("ev", [128, D])

    ptf = nc.alloc_psum_tensor("ptf", [128, 128], F32)
    ptb = nc.alloc_psum_tensor("ptb", [128, 128], BF16)
    pa = nc.alloc_psum_tensor("pa", [128, 512], F32)
    pb = nc.alloc_psum_tensor("pb", [128, 512], F32)
    pc = nc.alloc_psum_tensor("pc", [128, 512], F32)
    pd = nc.alloc_psum_tensor("pd", [128, 512], F32)
    pe_ = nc.alloc_psum_tensor("pe", [128, 512], F32)
    pf = nc.alloc_psum_tensor("pf", [128, 512], F32)

    p = Prog(nc)

    bcreg = nc.alloc_register(mybir.EngineType.Pool, "bcap")
    tbc0 = p.op("gpsimd", lambda g: g.reg_mov(bcreg, CAP - 1))

    tsc = p.dma("sync", scs[:], scfl)
    p.dma("sync", grs[:], grpm)
    p.dma("sync", gbs[:], gbt)
    p.dma("sync", l16s[:], l16)
    p.dma("sync", bns[:], bneg)
    p.dma("sync", bps[:], bpos)
    p.dma("sync", rsel[:], rowsel)
    tid = p.dma("sync", idf[:], ident)
    p.dma("gpsimd", idn[:], ident)
    p.dma("gpsimd", w1s[:], w1e)
    p.dma("gpsimd", w3s[:], w3e)
    tw = p.dma("gpsimd", w2s[:], w2e)
    tz = p.op("vector", lambda v: v.memset(ztile[:], 0.0))

    p.op("vector", lambda v: v.tensor_reduce(mnmx[:, 0:1], scs[:], axis=AX.X,
                                             op=ALU.min, negate=True), waits=[tsc])
    t2 = p.op("vector", lambda v: v.reduce_max(mnmx[:, 1:2], scs[:], axis=AX.X))
    ttr = p.op("tensor", lambda t: t.transpose(ptf[0:2, 0:128], mnmx[:], idf[:]),
               waits=[t2, tid])
    t3 = p.op("vector", lambda v: v.tensor_copy(rb2[:], ptf[0:2, 0:128]),
              waits=[ttr])
    t4 = p.op("vector", lambda v: v.reduce_max(gmx[:], rb2[:], axis=AX.X))
    tb1 = p.op("tensor", lambda t: t.matmul(pa[:, 0:1], bns[:], gmx[:],
                                            start=True, stop=True), waits=[t4])
    tb2 = p.op("tensor", lambda t: t.matmul(pb[:, 0:1], bps[:], gmx[:],
                                            start=True, stop=True))
    tl = p.op("vector", lambda v: v.tensor_scalar(lo[:], pa[:, 0:1], 1.0, None,
                                                  ALU.subtract), waits=[tb1])
    th = p.op("vector", lambda v: v.tensor_scalar(hi[:], pb[:, 0:1], 1.0, None,
                                                  ALU.add), waits=[tb2])

    tcn8 = None
    tbc = th
    for it in range(NBIS):
        p.op("vector", lambda v: v.tensor_add(mid[:], lo[:], hi[:]),
             waits=[tl, th, tbc])
        p.op("vector", lambda v: v.tensor_scalar(mid[:], mid[:], 0.5, None,
                                                 ALU.mult))
        tcm = p.op("vector", lambda v: v.tensor_scalar(mask[:], scs[:], mid[:],
                                                       None, ALU.is_ge, ALU.add,
                                                       accum_out=cnt[:]))
        tg = p.op("tensor", lambda t: t.matmul(pa[0:8, 0:1], grs[:], cnt[:],
                                               start=True, stop=True),
                  waits=[tcm, tcn8])
        tcn8 = p.op("vector", lambda v: v.tensor_copy(cnt8[:], pa[0:8, 0:1]),
                    waits=[tg])
        tg2 = p.op("tensor", lambda t: t.matmul(pb[:, 0:1], gbs[:], cnt8[:],
                                                start=True, stop=True),
                   waits=[tcn8, tbc])
        tsl = p.op("vector", lambda v: v.tensor_scalar(selm[:], pb[:, 0:1],
                                                       float(CAP), None,
                                                       ALU.is_ge), waits=[tg2])
        tsl2 = p.op("vector", lambda v: v.tensor_scalar(selm2[:], pb[:, 0:1],
                                                        float(CAP), None,
                                                        ALU.is_lt))
        tl = p.op("vector", lambda v: v.copy_predicated(lo[:], selm[:], mid[:]))
        th = p.op("vector", lambda v: v.copy_predicated(hi[:], selm2[:], mid[:]))
        tbc = tsl2

    tmask = p.op("vector", lambda v: v.tensor_scalar(mask[:], scs[:], lo[:],
                                                     None, ALU.is_ge, ALU.add,
                                                     accum_out=tot[:]),
                 waits=[tl])
    tnth = p.op("vector", lambda v: v.tensor_scalar(nth[:], lo[:], -1.0, None,
                                                    ALU.mult))
    tex = p.op("scalar", lambda s: s.activation(exm[:], scs[:], AF.Exp,
                                                bias=nth[:]), waits=[tnth])
    tex2 = p.op("vector", lambda v: v.tensor_mul(exm[:], exm[:], mask[:]),
                waits=[tex, tmask])
    trs = p.op("vector", lambda v: v.reduce_sum(rs1[:], exm[:], axis=AX.X))
    tg = p.op("tensor", lambda t: t.matmul(pa[0:8, 0:1], grs[:], rs1[:],
                                           start=True, stop=True),
              waits=[trs, tcn8])
    tcn8 = p.op("vector", lambda v: v.tensor_copy(cnt8[:], pa[0:8, 0:1]),
                waits=[tg])
    tg2 = p.op("tensor", lambda t: t.matmul(pb[:, 0:1], gbs[:], cnt8[:],
                                            start=True, stop=True),
               waits=[tcn8, tsl])
    trc = p.op("vector", lambda v: v.reciprocal(rs1[:], pb[:, 0:1]), waits=[tg2])
    twt = p.op("vector", lambda v: v.tensor_scalar(wts[:], exm[:], rs1[:], None,
                                                   ALU.mult), waits=[trc])
    tcu = p.op("vector", lambda v: v.tensor_tensor_scan(cums[:], mask[:],
                                                        ztile[:], 0.0,
                                                        ALU.add, ALU.add),
               waits=[tz])
    tof = p.op("tensor", lambda t: t.matmul(pa[:, 0:1], l16s[:], tot[:],
                                            start=True, stop=True),
               waits=[tmask, tg2])
    tofe = p.op("vector", lambda v: v.tensor_copy(offs[:], pa[:, 0:1]),
                waits=[tof])
    tr1 = p.op("vector", lambda v: v.tensor_sub(rank[:], cums[:], mask[:]),
               waits=[tcu])
    tr2 = p.op("vector", lambda v: v.tensor_scalar(rank[:], rank[:], offs[:],
                                                   None, ALU.add), waits=[tofe])
    tr3 = p.op("vector", lambda v: v.tensor_scalar(exm[:], mask[:], -BIGR, BIGR,
                                                   ALU.mult, ALU.add))
    tr4 = p.op("vector", lambda v: v.tensor_add(rank[:], rank[:], exm[:]),
               waits=[tr3])
    p.op("vector", lambda v: v.tensor_scalar(cums[:], rank[:], -1.0,
                                             float(CAP), ALU.mult, ALU.add),
         waits=[tr4])
    p.op("vector", lambda v: v.tensor_scalar_max(cums[:], cums[:], 0.0))
    tq3 = p.op("vector", lambda v: v.tensor_scalar_min(cums[:], cums[:], 1.0))
    twt = p.op("vector", lambda v: v.tensor_tensor(wts[:], wts[:], cums[:],
                                                   ALU.mult), waits=[twt, tq3])
    p.dma("sync", rkw_out[0:1, :, :], rank[:], waits=[tr4])
    p.dma("sync", rkw_out[1:2, :, :], wts[:], waits=[twt])
    trd = p.dma("sync", rank_dram.rearrange("(a b) f -> a (b f)", a=128),
                rank[:], waits=[tr4])
    trg = p.op("gpsimd", lambda g: g.indirect_dma_start(
        rrs2[:], None, rank_dram, IndirectOffsetOnAxis(rsel[:], 0)),
        waits=[trd], dma=True)
    ttr2 = p.op("tensor", lambda t: t.transpose(ptf[:], rrs2[:], idf[:]),
                waits=[trg, ttr])
    trr = p.op("vector", lambda v: v.tensor_copy(rrs[:], ptf[:]), waits=[ttr2])
    tri = p.op("vector", lambda v: v.tensor_copy(rri[:], rrs[:]))

    tprev = [None] * 8
    tscat = None
    for T in range(NTOTAL // 128):
        r = T % 8
        xt = xts[r]
        idxc = idxcs[r]
        tld = p.dma("sync", xt[:], xnf[T * 128:(T + 1) * 128, :],
                    waits=[tprev[r]], slot=r)
        tcx = p.op("vector", lambda v, T=T, idxc=idxc: v.tensor_copy(
            idxc[:], rri[:, T:T + 1]), waits=[tri, tprev[r]])
        tscat = p.op("gpsimd", lambda g, xt=xt, idxc=idxc: g.indirect_dma_start(
            xg_dram, IndirectOffsetOnAxis(idxc[:], 0), xt[:], None,
            bounds_check=bcreg, oob_is_err=False),
            waits=[tld, tcx, tbc0], dma=True, slot=r)
        tprev[r] = tscat

    tldp = None
    tnext = [t for t in tprev if t is not None]
    prev_ha = None
    prev_hb = None
    prev_ev = {0: None, 1: None}
    hq_guard = None
    xgt_guard = None
    for si in range(CAP // 512):
        tz2 = None
        for k in range(4):
            tile = si * 4 + k
            tldp = p.dma("sync", xgr[:], xg_dram[tile * 128:(tile + 1) * 128, :],
                         waits=tnext + [xgt_guard, tz2], slot=1)
            tnext = []
            for c in range(DCH):
                ttp = p.op("tensor", lambda t, c=c: t.transpose(
                    ptb[:], xgr[:, c * 128:(c + 1) * 128], idn[:]),
                    waits=[tldp, tz2])
                tz2 = p.op("vector", lambda v, c=c, k=k: v.tensor_copy(
                    xgt[:, c * 512 + k * 128: c * 512 + k * 128 + 128], ptb[:]),
                    waits=[ttp])
        thq = None
        for hc in range(HCH_M):
            tma = None
            for dc in range(DCH):
                tma = p.op("tensor", lambda t, hc=hc, dc=dc: t.matmul(
                    pa[:, 0:512],
                    w1s[:, dc * HMOE + hc * 128: dc * HMOE + hc * 128 + 128],
                    xgt[:, dc * 512: dc * 512 + 512],
                    start=(dc == 0), stop=(dc == DCH - 1)),
                    waits=([tz2, tw, prev_ha, hq_guard] if dc == 0 else []))
            tha = p.op("scalar", lambda s: s.activation(ha[:], pa[:, 0:512],
                                                        AF.Silu),
                       waits=[tma, thq])
            prev_ha = tha
            tmb = None
            for dc in range(DCH):
                tmb = p.op("tensor", lambda t, hc=hc, dc=dc: t.matmul(
                    pb[:, 0:512],
                    w3s[:, dc * HMOE + hc * 128: dc * HMOE + hc * 128 + 128],
                    xgt[:, dc * 512: dc * 512 + 512],
                    start=(dc == 0), stop=(dc == DCH - 1)),
                    waits=([prev_hb] if dc == 0 else []))
            thb = p.op("vector", lambda v: v.tensor_copy(hb[:], pb[:, 0:512]),
                       waits=[tmb, thq])
            prev_hb = thb
            thq = p.op("vector", lambda v, hc=hc: v.tensor_tensor(
                hq[:, hc * 512:(hc + 1) * 512], ha[:], hb[:], ALU.mult),
                waits=[tha, thb])
        for k in range(4):
            tile = si * 4 + k
            pspair = [pc, pd] if k % 2 == 0 else [pe_, pf]
            tevs = []
            for n in range(2):
                ps2 = pspair[n]
                tm2 = None
                for hc in range(HCH_M):
                    tm2 = p.op("tensor", lambda t, hc=hc, n=n, k=k, ps2=ps2: t.matmul(
                        ps2[:, 0:512],
                        hq[:, hc * 512 + k * 128: hc * 512 + k * 128 + 128],
                        w2s[:, hc * D + n * 512: hc * D + n * 512 + 512],
                        start=(hc == 0), stop=(hc == HCH_M - 1)),
                        waits=([thq, prev_ev[k % 2]] if hc == 0 else []))
                tev = p.op("vector", lambda v, n=n, ps2=ps2: v.tensor_copy(
                    ev[:, n * 512:(n + 1) * 512], ps2[:, 0:512]),
                    waits=[tm2, prev_ev[1 - k % 2]])
                tevs.append(tev)
            tdy = p.dma("sync", yo[:, tile:tile + 1, :], ev[:], waits=tevs,
                        slot=2 + k % 2)
            prev_ev[k % 2] = tdy
            if k == 3:
                hq_guard = tm2
        xgt_guard = thq

    ends = p.heads()
    for e in ENGINES:
        for tok in ends:
            p.wait(e, tok)
    p.finalize()
    return nc


def _moe_host_inputs(inputs, xn_full, scores_full, layer):
    W1 = np.asarray(inputs["L1_ffn_W1"], np.float32)
    W2 = np.asarray(inputs["L1_ffn_W2"], np.float32)
    W3 = np.asarray(inputs["L1_ffn_W3"], np.float32)
    kk = np.arange(128)
    grpm = np.zeros((128, 8), np.float32)
    for e_ in range(8):
        grpm[e_ * 16:(e_ + 1) * 16, e_] = 1.0
    gbt = np.zeros((8, 128), np.float32)
    for e_ in range(8):
        gbt[e_, e_ * 16:(e_ + 1) * 16] = 1.0
    l16 = ((kk[:, None] // 16 == kk[None, :] // 16) &
           (kk[:, None] < kk[None, :])).astype(np.float32)
    bneg = np.zeros((2, 128), np.float32); bneg[0, :] = -1.0
    bpos = np.zeros((2, 128), np.float32); bpos[1, :] = 1.0
    com = dict(
        scfl=np.ascontiguousarray(scores_full.reshape(128, 1024)),
        xnf=xn_full,
        grpm=grpm, gbt=gbt, l16=l16, bneg=bneg, bpos=bpos,
        ident=np.eye(128, dtype=np.float32),
    )
    maps = []
    for j in range(NCORE):
        mp = dict(com)
        mp["w1e"] = _sbuf_chunks(np.ascontiguousarray(W1[j].T), DCH)
        mp["w3e"] = _sbuf_chunks(np.ascontiguousarray(W3[j].T), DCH)
        mp["w2e"] = _sbuf_chunks(np.ascontiguousarray(W2[j].T), HCH_M)
        mp["rowsel"] = (j * 128 + np.arange(128, dtype=np.int32))[:, None]
        maps.append(mp)
    return maps


def emit_combine(p, nc, xsb, y1f, rki, wls, bufs, idxcs, hwc, bcreg, deps,
                 tile_major=False):
    R = len(bufs)
    tz = [p.op("vector", lambda v, r=r: v.memset(bufs[r][:], 0.0), waits=deps)
          for r in range(R)]
    tadd = None
    tadd_hist = [None] * R
    tgat_hist = [None] * R
    tadd_tile = [None] * NTO
    for i in range(E * NTO):
        if tile_major:
            T, e_ = divmod(i, E)
        else:
            e_, T = divmod(i, NTO)
        r = i % R
        tcx = p.op("gpsimd", lambda g, e_=e_, T=T, r=r: g.tensor_copy(
            idxcs[r][:], rki[:, e_ * 16 + T: e_ * 16 + T + 1]),
            waits=deps + [tgat_hist[r]])
        tgat = p.op("gpsimd", lambda g, e_=e_, r=r: g.indirect_dma_start(
            bufs[r][:], None, y1f, IndirectOffsetOnAxis(idxcs[r][:], 0),
            element_offset=e_ * CAP * D, bounds_check=bcreg,
            oob_is_err=False), waits=[tcx, tz[r], tadd_hist[r]],
            dma=True, slot=r)
        tgat_hist[r] = tgat
        thw = p.op("vector", lambda v, e_=e_, T=T: v.tensor_scalar(
            hwc[:], wls[:, e_ * 16 + T: e_ * 16 + T + 1], 0.5, None,
            ALU.mult), waits=[tgat])
        tadd = p.op("vector", lambda v, T=T, r=r: v.scalar_tensor_tensor(
            xsb[:, T * D:(T + 1) * D], bufs[r][:], hwc[:],
            xsb[:, T * D:(T + 1) * D], ALU.mult, ALU.add), waits=[thw])
        tadd_hist[r] = tadd
        tadd_tile[T] = tadd
    return tadd, tadd_tile


def _combine_host(rkw, j):
    rk = rkw[0].reshape(E, NTOTAL)[:, j * TOK:(j + 1) * TOK]
    wv = rkw[1].reshape(E, NTOTAL)[:, j * TOK:(j + 1) * TOK]
    rkl = np.ascontiguousarray(
        rk.reshape(E, NTO, 128).transpose(2, 0, 1).reshape(128, 128))
    wl = np.ascontiguousarray(
        wv.reshape(E, NTO, 128).transpose(2, 0, 1).reshape(128, 128))
    return rkl.astype(np.float32), wl.astype(np.float32)



def build_att1a():
    nc = bass.Bass("TRN2", target_bir_lowering=False)
    inp = lambda n, sh, dt=F32: nc.dram_tensor(n, sh, dt, kind="ExternalInput").ap()
    out_ = lambda n, sh, dt=F32: nc.dram_tensor(n, sh, dt, kind="ExternalOutput").ap()

    x3 = inp("x3", [TOK, D])
    y1f = inp("y1f", [NTOTAL, D])
    rkl = inp("rkl", [128, 128])
    wl = inp("wl", [128, 128])
    wkl = inp("wkl", [128, DCH * D])
    wvl = inp("wvl", [128, DCH * D])
    bk = inp("bk", [1, D])
    bv = inp("bv", [1, D])
    ident = inp("ident", [128, 128])

    x4_out = out_("x4_out", [TOK, D])
    st_out = out_("st_out", [1025, D])
    x4o = x4_out.rearrange("(t p) d -> p t d", p=128)

    m = Mem(nc)
    Off = [0]
    def alloc(name, shape, dt=F32):
        Off[0] = (Off[0] + 31) // 32 * 32
        t = m.at(name, Off[0], shape, dt)
        Off[0] += int(np.prod(shape[1:])) * mybir.dt.size(dt)
        return t

    xsb = alloc("xsb", [128, NTO * D])
    kps = alloc("kps", [128, NTO * D], BF16)
    vps = alloc("vps", [128, NTO * D], BF16)
    wks = alloc("wks", [128, DCH * D], BF16)
    wvs = alloc("wvs", [128, DCH * D], BF16)
    rks = alloc("rks", [128, 128])
    rki = alloc("rki", [128, 128], I32)
    wls = alloc("wls", [128, 128])
    bufs = [alloc(f"buf{r}", [128, D]) for r in range(5)]
    idxcs = [alloc(f"idxc{r}", [128, 1], I32) for r in range(5)]
    hwc = alloc("hwc", [128, 1])
    znb2 = [alloc(f"znb{r}", [128, D], BF16) for r in range(2)]
    znt2 = [alloc(f"znt{r}", [128, DCH * 128], BF16) for r in range(2)]
    idn = alloc("idn", [128, 128], BF16)
    bks = alloc("bks", [1, D])
    bvs = alloc("bvs", [1, D])
    ones1 = alloc("ones1", [1, 128])
    onesc = alloc("onesc", [128, 1], BF16)
    phi1 = alloc("phi1", [128, 512])
    phi2 = alloc("phi2", [128, 512])
    sev = alloc("sev", [128, 512])
    bn12 = alloc("bn12", [128, 12])
    st2 = alloc("st2", [128, 2])
    rstd = alloc("rstd", [128, 1])
    epsc = alloc("epsc", [128, 1])
    sc = {"bn12": bn12, "st2": st2, "rstd": rstd, "nmr": None, "eps": epsc[:, 0:1]}

    pt = nc.alloc_psum_tensor("pt", [128, 128], BF16)
    pa = nc.alloc_psum_tensor("pa", [128, 512], F32)
    pb = nc.alloc_psum_tensor("pb", [128, 512], F32)
    pc = nc.alloc_psum_tensor("pc", [128, 512], F32)
    pd = nc.alloc_psum_tensor("pd", [128, 512], F32)
    pe_ = nc.alloc_psum_tensor("pe", [128, 512], F32)
    pf = nc.alloc_psum_tensor("pf", [128, 512], F32)

    p = Prog(nc)
    bcreg = nc.alloc_register(mybir.EngineType.Pool, "bcap")
    tb0 = p.op("gpsimd", lambda g: g.reg_mov(bcreg, CAP - 1))
    tx = p.dma("sync", xsb[:].rearrange("p (t d) -> p t d", d=D),
               x3.rearrange("(t p) d -> p t d", p=128))
    trk = p.dma("sync", rks[:], rkl)
    twl = p.dma("sync", wls[:], wl)
    tbk = p.dma("sync", bks[:], bk)
    tbv = p.dma("sync", bvs[:], bv)
    p.dma("gpsimd", idn[:], ident)
    p.dma("gpsimd", wks[:], wkl)
    tw = p.dma("gpsimd", wvs[:], wvl)
    p.op("vector", lambda v: v.memset(ones1[:], 1.0))
    p.op("vector", lambda v: v.memset(onesc[:], 1.0))
    p.op("vector", lambda v: v.memset(epsc[:], LN_EPS))
    tri = p.op("vector", lambda v: v.tensor_copy(rki[:], rks[:]), waits=[trk])

    tcomb, ttile = emit_combine(p, nc, xsb, y1f, rki, wls, bufs, idxcs, hwc,
                                bcreg, [tx, tri, twl, tb0], tile_major=True)

    tdx = None
    for T in range(NTO):
        tdx = p.dma("sync", x4o[:, T:T + 1, :], xsb[:, T * D:(T + 1) * D],
                    waits=[ttile[T]], slot=T % 4)

    tztl = None
    tztp = {0: None, 1: None}
    tphiK = {0: None, 1: None}
    tphiV = None
    for T in range(NTO):
        b = T % 2
        znbT = znb2[b]
        zntT = znt2[b]
        tln = emit_ln(p, sc, xsb[:, T * D:(T + 1) * D], znbT[:],
                      deps=[ttile[T], tztp[b]])
        for c in range(DCH):
            ttr = p.op("tensor", lambda t, c=c, znbT=znbT: t.transpose(
                pt[:], znbT[:, c * 128:(c + 1) * 128], idn[:]),
                waits=[tln, tztl])
            tztl = p.op("vector", lambda v, c=c, zntT=zntT: v.tensor_copy(
                zntT[:, c * 128:(c + 1) * 128], pt[:]), waits=[ttr])
        tztp[b] = tztl
        for n in range(2):
            psK = [[pa, pb], [pe_, pf]][b][n]
            tmk = None
            for dc in range(DCH):
                tmk = p.op("tensor", lambda t, dc=dc, n=n, psK=psK, zntT=zntT: t.matmul(
                    psK[:, 0:512], zntT[:, dc * 128:(dc + 1) * 128],
                    wks[:, dc * D + n * 512: dc * D + n * 512 + 512],
                    start=(dc == 0), stop=False),
                    waits=([tztl, tw, tphiK[b]] if dc == 0 else []))
            tmk = p.op("tensor", lambda t, n=n, psK=psK: t.matmul(
                psK[:, 0:512], ones1[:], bks[:, n * 512:(n + 1) * 512],
                start=False, stop=True), waits=[tbk])
            t1 = p.op("vector", lambda v, psK=psK: v.tensor_scalar_min(
                phi1[:], psK[:, 0:512], 0.0), waits=[tmk])
            t2 = p.op("scalar", lambda s: s.activation(phi1[:], phi1[:], AF.Exp),
                      waits=[t1])
            t3 = p.op("vector", lambda v, psK=psK: v.tensor_scalar_max(
                phi2[:], psK[:, 0:512], 0.0), waits=[t2])
            tphiK[b] = p.op("vector", lambda v, T=T, n=n: v.tensor_tensor(
                kps[:, T * D + n * 512: T * D + n * 512 + 512], phi1[:],
                phi2[:], ALU.add), waits=[t3])
        for n in range(2):
            psV = [pc, pd][n]
            tmv = None
            for dc in range(DCH):
                tmv = p.op("tensor", lambda t, dc=dc, n=n, psV=psV, zntT=zntT: t.matmul(
                    psV[:, 0:512], zntT[:, dc * 128:(dc + 1) * 128],
                    wvs[:, dc * D + n * 512: dc * D + n * 512 + 512],
                    start=(dc == 0), stop=False),
                    waits=([tztl, tbv, tphiV] if dc == 0 else []))
            tmv = p.op("tensor", lambda t, n=n, psV=psV: t.matmul(
                psV[:, 0:512], ones1[:], bvs[:, n * 512:(n + 1) * 512],
                start=False, stop=True), waits=[tbv])
            tphiV = p.op("vector", lambda v, T=T, n=n, psV=psV: v.tensor_copy(
                vps[:, T * D + n * 512: T * D + n * 512 + 512], psV[:, 0:512]),
                waits=[tmv])

    kv_done = tphiV
    tse = None
    for dc in range(DCH):
        for n in range(2):
            ps_ = [pa, pb][(dc * 2 + n) % 2]
            tms = None
            for T in range(NTO):
                tms = p.op("tensor", lambda t, dc=dc, n=n, T=T, ps_=ps_: t.matmul(
                    ps_[:, 0:512],
                    kps[:, T * D + dc * 128: T * D + dc * 128 + 128],
                    vps[:, T * D + n * 512: T * D + n * 512 + 512],
                    start=(T == 0), stop=(T == NTO - 1)),
                    waits=([kv_done, tse] if T == 0 else []))
            tev = p.op("vector", lambda v, ps_=ps_: v.tensor_copy(
                sev[:], ps_[:, 0:512]), waits=[tms])
            tse = p.dma("sync", st_out[dc * 128:(dc + 1) * 128,
                                       n * 512:(n + 1) * 512], sev[:],
                        waits=[tev])
    for n in range(2):
        tmz = None
        for T in range(NTO):
            tmz = p.op("tensor", lambda t, n=n, T=T: t.matmul(
                pc[0:1, 0:512], onesc[:],
                kps[:, T * D + n * 512: T * D + n * 512 + 512],
                start=(T == 0), stop=(T == NTO - 1)),
                waits=([kv_done, tse] if T == 0 else []))
        tev = p.op("vector", lambda v: v.tensor_copy(sev[0:1, 0:512],
                                                     pc[0:1, 0:512]),
                   waits=[tmz])
        tse = p.dma("sync", st_out[1024:1025, n * 512:(n + 1) * 512],
                    sev[0:1, 0:512], waits=[tev])

    ends = p.heads()
    for e in ENGINES:
        for tok in ends:
            p.wait(e, tok)
    p.finalize()
    return nc


def build_att1b():
    nc = bass.Bass("TRN2", target_bir_lowering=False)
    inp = lambda n, sh, dt=F32: nc.dram_tensor(n, sh, dt, kind="ExternalInput").ap()
    out_ = lambda n, sh, dt=F32: nc.dram_tensor(n, sh, dt, kind="ExternalOutput").ap()

    x4 = inp("x4", [TOK, D])
    stp = inp("stp", [1025, D])
    wql = inp("wql", [128, DCH * D])
    wkl = inp("wkl", [128, DCH * D])
    wvl = inp("wvl", [128, DCH * D])
    wol = inp("wol", [128, DCH * D])
    bq = inp("bq", [1, D])
    bk = inp("bk", [1, D])
    bv = inp("bv", [1, D])
    bo = inp("bo", [1, D])
    ue = inp("ue", [128, 128])
    wgl = inp("wgl", [128, DCH * E])
    ident = inp("ident", [128, 128])

    x5_out = out_("x5_out", [TOK, D])
    xn2_out = out_("xn2_out", [TOK, D], BF16)
    sc2_out = out_("sc2_out", [E, TOK])
    x5o = x5_out.rearrange("(t p) d -> p t d", p=128)
    xno = xn2_out.rearrange("(t p) d -> p t d", p=128)

    m = Mem(nc)
    Off = [0]
    def alloc(name, shape, dt=F32):
        Off[0] = (Off[0] + 31) // 32 * 32
        t = m.at(name, Off[0], shape, dt)
        Off[0] += int(np.prod(shape[1:])) * mybir.dt.size(dt)
        return t

    xtl = [alloc(f"xtl{i}", [128, D]) for i in range(2)]
    Sf = alloc("Sf", [128, DCH * D])
    Sb = alloc("Sb", [128, DCH * D], BF16)
    wqs = alloc("wqs", [128, DCH * D], BF16)
    wks = alloc("wks", [128, DCH * D], BF16)
    wvs = alloc("wvs", [128, DCH * D], BF16)
    wos = alloc("wos", [128, DCH * D], BF16)
    wgs = alloc("wgs", [128, DCH * E], BF16)
    Zr = alloc("Zr", [1, D])
    ueb = alloc("ueb", [128, 128], BF16)
    idn = alloc("idn", [128, 128], BF16)
    znt2 = [alloc(f"znt{i}", [128, DCH * 128], BF16) for i in range(2)]
    znb2 = [alloc(f"znb{i}", [128, D], BF16) for i in range(2)]
    qpt2 = [alloc(f"qpt{i}", [128, D], BF16) for i in range(2)]
    qptT2 = [alloc(f"qptT{i}", [128, DCH * 128], BF16) for i in range(2)]
    kptR = [alloc(f"kpt{i}", [128, D], BF16) for i in range(4)]
    kptTR = [alloc(f"kptT{i}", [128, DCH * 128], BF16) for i in range(4)]
    vtR = [alloc(f"vt{i}", [128, D], BF16) for i in range(4)]
    At2b = [alloc(f"At{i}", [128, 128], BF16) for i in range(2)]
    Apr = alloc("Apr", [128, 128], BF16)
    Ot2 = [alloc(f"Ot{i}", [128, D], BF16) for i in range(2)]
    OT2 = [alloc(f"OT{i}", [128, DCH * 128], BF16) for i in range(2)]
    scr = alloc("scr", [128, D])
    phi1 = alloc("phi1", [128, 512])
    phi2 = alloc("phi2", [128, 512])
    den = alloc("den", [128, 1])
    den2 = alloc("den2", [128, 1])
    rden = alloc("rden", [128, 1])
    bqs = alloc("bqs", [1, D])
    bks = alloc("bks", [1, D])
    bvs = alloc("bvs", [1, D])
    bos = alloc("bos", [1, D])
    ones1 = alloc("ones1", [1, 128])
    onesc = alloc("onesc", [128, 1], BF16)
    znbG = [alloc(f"znbG{i}", [128, D], BF16) for i in range(2)]
    znTg = alloc("znTg", [128, DCH * 128], BF16)
    scev = alloc("scev", [8, 512])
    bn12 = alloc("bn12", [128, 12])
    st2 = alloc("st2", [128, 2])
    rstd = alloc("rstd", [128, 1])
    epsc = alloc("epsc", [128, 1])
    sc = {"bn12": bn12, "st2": st2, "rstd": rstd, "eps": epsc[:, 0:1]}

    pt = nc.alloc_psum_tensor("pt", [128, 128], BF16)
    pa = nc.alloc_psum_tensor("pa", [128, 512], F32)
    pb = nc.alloc_psum_tensor("pb", [128, 512], F32)
    pc = nc.alloc_psum_tensor("pc", [128, 512], F32)
    pd = nc.alloc_psum_tensor("pd", [128, 512], F32)
    pe_ = nc.alloc_psum_tensor("pe", [128, 512], F32)
    pf = nc.alloc_psum_tensor("pf", [128, 512], F32)
    pg = nc.alloc_psum_tensor("pg", [128, 512], F32)

    p = Prog(nc)

    x4i = x4.rearrange("(t p) d -> p t d", p=128)
    tS = p.dma("sync", Sf[:].rearrange("p (c e) -> p c e", e=D),
               stp[0:1024, :].rearrange("(c p) e -> p c e", p=128))
    tZ = p.dma("sync", Zr[:], stp[1024:1025, :])
    tbq = p.dma("sync", bqs[:], bq)
    tbk = p.dma("sync", bks[:], bk)
    tbv = p.dma("sync", bvs[:], bv)
    tbo = p.dma("sync", bos[:], bo)
    p.dma("gpsimd", idn[:], ident)
    p.dma("gpsimd", ueb[:], ue)
    p.dma("gpsimd", wqs[:], wql)
    p.dma("gpsimd", wks[:], wkl)
    p.dma("gpsimd", wvs[:], wvl)
    p.dma("gpsimd", wos[:], wol)
    twg = p.dma("gpsimd", wgs[:], wgl)
    p.op("vector", lambda v: v.memset(ones1[:], 1.0))
    p.op("vector", lambda v: v.memset(onesc[:], 1.0))
    p.op("vector", lambda v: v.memset(epsc[:], LN_EPS))
    tSbl = None
    for cset in range(DCH):
        tSbl = p.op("scalar", lambda s, cset=cset: s.activation(
            Sb[:, cset * D:(cset + 1) * D], Sf[:, cset * D:(cset + 1) * D],
            AF.Copy), waits=[tS])
    tloads = p.heads()

    def proj(pspair, ws, brow, tbias, deps, phi, dst, src):
        tout = None
        for n in range(2):
            ps_ = pspair[n]
            tm = None
            for dc in range(DCH):
                tm = p.op("tensor", lambda t, dc=dc, n=n, ps_=ps_, ws=ws, src=src: t.matmul(
                    ps_[:, 0:512], src[:, dc * 128:(dc + 1) * 128],
                    ws[:, dc * D + n * 512: dc * D + n * 512 + 512],
                    start=(dc == 0), stop=False),
                    waits=(list(deps) if dc == 0 else []))
            tm = p.op("tensor", lambda t, n=n, ps_=ps_, brow=brow: t.matmul(
                ps_[:, 0:512], ones1[:], brow[:, n * 512:(n + 1) * 512],
                start=False, stop=True), waits=[tbias])
            if phi:
                t1 = p.op("vector", lambda v, ps_=ps_: v.tensor_scalar_min(
                    phi1[:], ps_[:, 0:512], 0.0), waits=[tm])
                t2 = p.op("scalar", lambda s: s.activation(phi1[:], phi1[:],
                                                           AF.Exp), waits=[t1])
                t3 = p.op("vector", lambda v, ps_=ps_: v.tensor_scalar_max(
                    phi2[:], ps_[:, 0:512], 0.0), waits=[t2])
                tout = p.op("vector", lambda v, n=n, dst=dst: v.tensor_tensor(
                    dst[:, n * 512:(n + 1) * 512], phi1[:], phi2[:], ALU.add),
                    waits=[t3])
            else:
                tout = p.op("vector", lambda v, n=n, ps_=ps_, dst=dst: v.tensor_copy(
                    dst[:, n * 512:(n + 1) * 512], ps_[:, 0:512]), waits=[tm])
        return tout

    pt_chain = [None]
    def transposes(src, dst, deps):
        tzt = pt_chain[0]
        for c in range(DCH):
            ttr = p.op("tensor", lambda t, c=c, src=src: t.transpose(
                pt[:], src[:, c * 128:(c + 1) * 128], idn[:]),
                waits=list(deps) + [tzt])
            tzt = p.op("vector", lambda v, c=c, dst=dst: v.tensor_copy(
                dst[:, c * 128:(c + 1) * 128], pt[:]), waits=[ttr])
        pt_chain[0] = tzt
        return tzt

    G = 4
    guard_num = {0: None, 1: None}
    guard_zr = None
    guard_pe = [None]
    xout = {0: None, 1: None}
    xfree = {0: None, 1: None}
    tSb_tok = [tSbl]
    tfold_mm = None
    tge = None
    tscdma = None
    txndma = None
    for T in range(NTO):
        b = T % 2
        r = T % 4
        g0 = (T // G) * G
        xt = xtl[b]
        txl = p.dma("sync", xt[:], x4i[:, T:T + 1, :],
                    waits=[xout[b], xfree[b]], slot=b)
        tln = emit_ln(p, sc, xt[:], znb2[b][:],
                      deps=[txl] + (tloads if T == 0 else []))
        tzt = transposes(znb2[b], znt2[b], [tln] + (tloads if T == 0 else []))
        tq = proj([pa, pb], wqs, bqs, tbq, [tzt], True, qpt2[b], znt2[b])
        tk = proj([pe_, pf], wks, bks, tbk,
                  [tzt, guard_pe[0]], True, kptR[r], znt2[b])
        tv = proj([pc, pd], wvs, bvs, tbv,
                  [tzt, guard_num[0], guard_num[1]], False, vtR[r], znt2[b])
        tqT = transposes(qpt2[b], qptT2[b], [tq])
        tkT = transposes(kptR[r], kptTR[r], [tk])
        tzc = {}
        for n in range(2):
            ps_ = [pe_, pf][n]
            p.op("tensor", lambda t, n=n, ps_=ps_, r=r: t.matmul(
                ps_[:, 0:512], ueb[:], kptR[r][:, n * 512:(n + 1) * 512],
                start=True, stop=False), waits=[tk, guard_zr])
            tzc[n] = p.op("tensor", lambda t, n=n, ps_=ps_: t.matmul(
                ps_[:, 0:512], ones1[:], Zr[:, n * 512:(n + 1) * 512],
                start=False, stop=True), waits=[tZ])
        td1 = p.op("vector", lambda v, b=b: v.scalar_tensor_tensor(
            scr[:, 0:512], qpt2[b][:, 0:512], 1.0, pe_[:, 0:512], ALU.mult,
            ALU.mult, accum_out=den[:]), waits=[tzc[0], tq])
        td2 = p.op("vector", lambda v, b=b: v.scalar_tensor_tensor(
            scr[:, 512:1024], qpt2[b][:, 512:1024], 1.0, pf[:, 0:512], ALU.mult,
            ALU.mult, accum_out=den2[:]), waits=[tzc[1]])
        p.op("vector", lambda v: v.tensor_add(den[:], den[:], den2[:]),
             waits=[td1, td2])
        p.op("vector", lambda v: v.tensor_scalar(den[:], den[:], 1e-6,
                                                 None, ALU.add))
        trd = p.op("vector", lambda v: v.reciprocal(rden[:], den[:]))
        guard_pe[0] = td2
        for n in range(2):
            tzs = p.op("tensor", lambda t, n=n, r=r: t.matmul(
                pg[0:1, 0:512], onesc[:], kptR[r][:, n * 512:(n + 1) * 512],
                start=True, stop=True), waits=[tk, guard_zr, tge])
            guard_zr = p.op("vector", lambda v, n=n: v.tensor_add(
                Zr[:, n * 512:(n + 1) * 512], Zr[:, n * 512:(n + 1) * 512],
                pg[0:1, 0:512]), waits=[tzs, td1 if n == 0 else td2])
        tA = None
        for dc in range(DCH):
            tA = p.op("tensor", lambda t, dc=dc, b=b, r=r: t.matmul(
                pa[:, 0:128], kptTR[r][:, dc * 128:(dc + 1) * 128],
                qptT2[b][:, dc * 128:(dc + 1) * 128],
                start=(dc == 0), stop=(dc == DCH - 1)),
                waits=([tkT, tqT, tv] if dc == 0 else []))
        tAm = p.op("vector", lambda v, b=b: v.tensor_tensor(
            At2b[b][:], pa[:, 0:128], ueb[:], ALU.mult), waits=[tA])
        tnum = {}
        tpair_evac = None
        pair_ev = {}
        for c in range(g0, T):
            tA2 = None
            for dc in range(DCH):
                tA2 = p.op("tensor", lambda t, dc=dc, b=b, c=c: t.matmul(
                    pa[:, 0:128], kptTR[c % 4][:, dc * 128:(dc + 1) * 128],
                    qptT2[b][:, dc * 128:(dc + 1) * 128],
                    start=(dc == 0), stop=(dc == DCH - 1)),
                    waits=([tAm, tpair_evac] if dc == 0 else []))
            tpair_evac = p.op("vector", lambda v: v.tensor_copy(
                Apr[:], pa[:, 0:128]), waits=[tA2])
            pair_ev[c] = tpair_evac
            for n in range(2):
                ps_ = [pc, pd][n]
                tnum[n] = p.op("tensor", lambda t, n=n, ps_=ps_, c=c, g0=g0: t.matmul(
                    ps_[:, 0:512], Apr[:], vtR[c % 4][:, n * 512:(n + 1) * 512],
                    start=(c == g0), stop=False),
                    waits=[tpair_evac] + ([guard_num[n]] if c == g0 else []))
        for n in range(2):
            ps_ = [pc, pd][n]
            for dc in range(DCH):
                tnum[n] = p.op("tensor", lambda t, dc=dc, n=n, ps_=ps_, b=b, g0=g0, T=T: t.matmul(
                    ps_[:, 0:512], qptT2[b][:, dc * 128:(dc + 1) * 128],
                    Sb[:, dc * D + n * 512: dc * D + n * 512 + 512],
                    start=(dc == 0 and T == g0), stop=False),
                    waits=(([tqT, guard_num[n]] if T == g0 else [tqT])
                           + tSb_tok if dc == 0 else []))
            tnum[n] = p.op("tensor", lambda t, n=n, ps_=ps_, b=b, r=r: t.matmul(
                ps_[:, 0:512], At2b[b][:], vtR[r][:, n * 512:(n + 1) * 512],
                start=False, stop=True), waits=[tAm, tv])
        tOt = None
        for n in range(2):
            ps_ = [pc, pd][n]
            tOt = p.op("scalar", lambda s, n=n, ps_=ps_, b=b: s.activation(
                Ot2[b][:, n * 512:(n + 1) * 512], ps_[:, 0:512], AF.Copy,
                scale=rden[:]), waits=[tnum[n], trd])
        guard_num[0] = tOt
        guard_num[1] = tOt
        tOT = transposes(Ot2[b], OT2[b], [tOt])
        guard_xadd = None
        for n in range(2):
            ps_ = [pa, pb][n]
            tm = None
            for dc in range(DCH):
                tm = p.op("tensor", lambda t, dc=dc, n=n, ps_=ps_, b=b: t.matmul(
                    ps_[:, 0:512], OT2[b][:, dc * 128:(dc + 1) * 128],
                    wos[:, dc * D + n * 512: dc * D + n * 512 + 512],
                    start=(dc == 0), stop=False),
                    waits=([tOT] if dc == 0 else []))
            tm = p.op("tensor", lambda t, n=n, ps_=ps_: t.matmul(
                ps_[:, 0:512], ones1[:], bos[:, n * 512:(n + 1) * 512],
                start=False, stop=True), waits=[tbo])
            guard_xadd = p.op("vector", lambda v, n=n, ps_=ps_, xt=xt: v.tensor_tensor(
                xt[:, n * 512:(n + 1) * 512], ps_[:, 0:512],
                xt[:, n * 512:(n + 1) * 512], ALU.add),
                waits=[tm, txl])
        xout[b] = p.dma("sync", x5o[:, T:T + 1, :], xt[:],
                        waits=[guard_xadd], slot=2)
        tlnG = emit_ln(p, sc, xt[:], znbG[b][:], deps=[guard_xadd])
        txndma = p.dma("sync", xno[:, T:T + 1, :], znbG[b][:],
                       waits=[tlnG], slot=3)
        tztG = transposes(znbG[b], znTg, [tlnG, tge])
        tgm = None
        for dc in range(DCH):
            tgm = p.op("tensor", lambda t_, dc=dc: t_.matmul(
                pg[0:8, 0:128], wgs[:, dc * E:(dc + 1) * E],
                znTg[:, dc * 128:(dc + 1) * 128],
                start=(dc == 0), stop=(dc == DCH - 1)),
                waits=([tztG, tge, twg, guard_zr] if dc == 0 else []))
        tge = p.op("vector", lambda v, T=T: v.tensor_copy(
            scev[0:8, (T % 4) * 128: (T % 4 + 1) * 128],
            pg[0:8, 0:128]), waits=[tgm] + ([tscdma] if T % 4 == 0 else []))
        if T % 4 == 3:
            tscdma = p.dma("sync", sc2_out[:, (T - 3) * 128: (T + 1) * 128],
                           scev[0:8, :], waits=[tge], slot=1)
        xfree[b] = tlnG
        if T % G == G - 1 and T != NTO - 1:
            tSb_tok = []
            tfa_prev = {0: guard_pe[0], 1: guard_pe[0]}
            tfa_last = None
            for dc in range(DCH):
                for n in range(2):
                    ps_ = [pe_, pf][n]
                    tmf = None
                    for ci in range(G):
                        cc = g0 + ci
                        tmf = p.op("tensor", lambda t, dc=dc, n=n, ps_=ps_, cc=cc, ci=ci: t.matmul(
                            ps_[:, 0:512], kptR[cc % 4][:, dc * 128:(dc + 1) * 128],
                            vtR[cc % 4][:, n * 512:(n + 1) * 512],
                            start=(ci == 0), stop=(ci == G - 1)),
                            waits=([tfa_prev[n], tnum[n]] if ci == 0 else []))
                    tfold_mm = tmf
                    tfa = p.op("vector", lambda v, dc=dc, n=n, ps_=ps_: v.tensor_tensor(
                        Sf[:, dc * D + n * 512: dc * D + n * 512 + 512],
                        ps_[:, 0:512],
                        Sf[:, dc * D + n * 512: dc * D + n * 512 + 512],
                        ALU.add), waits=[tmf])
                    tfa_prev[n] = tfa
                    tfa_last = tfa
                    tfe = p.op("scalar", lambda s, dc=dc, n=n: s.activation(
                        Sb[:, dc * D + n * 512: dc * D + n * 512 + 512],
                        Sf[:, dc * D + n * 512: dc * D + n * 512 + 512],
                        AF.Copy), waits=[tfa])
                    tSb_tok.append(tfe)
            guard_pe[0] = tfa_last

    ends = p.heads()
    for e in ENGINES:
        for tok in ends:
            p.wait(e, tok)
    p.finalize()
    return nc



def build_comb2():
    nc = bass.Bass("TRN2", target_bir_lowering=False)
    inp = lambda n, sh, dt=F32: nc.dram_tensor(n, sh, dt, kind="ExternalInput").ap()
    x5 = inp("x5", [TOK, D])
    y1f = inp("y1f", [NTOTAL, D])
    rkl = inp("rkl", [128, 128])
    wl = inp("wl", [128, 128])
    xf_out = nc.dram_tensor("xf_out", [TOK, D], F32, kind="ExternalOutput").ap()
    xfo = xf_out.rearrange("(t p) d -> p t d", p=128)

    m = Mem(nc)
    Off = [0]
    def alloc(name, shape, dt=F32):
        Off[0] = (Off[0] + 31) // 32 * 32
        t = m.at(name, Off[0], shape, dt)
        Off[0] += int(np.prod(shape[1:])) * mybir.dt.size(dt)
        return t

    xsb = alloc("xsb", [128, NTO * D])
    rks = alloc("rks", [128, 128])
    rki = alloc("rki", [128, 128], I32)
    wls = alloc("wls", [128, 128])
    bufs = [alloc(f"buf{r}", [128, D]) for r in range(8)]
    idxcs = [alloc(f"idxc{r}", [128, 1], I32) for r in range(8)]
    hwc = alloc("hwc", [128, 1])

    p = Prog(nc)
    bcreg = nc.alloc_register(mybir.EngineType.Pool, "bcap")
    tb0 = p.op("gpsimd", lambda g: g.reg_mov(bcreg, CAP - 1))
    tx = p.dma("sync", xsb[:].rearrange("p (t d) -> p t d", d=D),
               x5.rearrange("(t p) d -> p t d", p=128))
    trk = p.dma("sync", rks[:], rkl)
    twl = p.dma("sync", wls[:], wl)
    tri = p.op("vector", lambda v: v.tensor_copy(rki[:], rks[:]), waits=[trk])
    tcomb, ttile = emit_combine(p, nc, xsb, y1f, rki, wls, bufs, idxcs, hwc,
                                bcreg, [tx, tri, twl, tb0], tile_major=True)
    for T in range(NTO):
        p.dma("sync", xfo[:, T:T + 1, :], xsb[:, T * D:(T + 1) * D],
              waits=[ttile[T]], slot=T % 4)
    ends = p.heads()
    for e in ENGINES:
        for tok in ends:
            p.wait(e, tok)
    p.finalize()
    return nc


_NC_CACHE = {}


def _get(name, builder):
    if name not in _NC_CACHE:
        _NC_CACHE[name] = builder()
    return _NC_CACHE[name]


def kernel(**inputs):
    inputs = {k: np.ascontiguousarray(np.asarray(v)) for k, v in inputs.items()}
    cores = list(range(NCORE))

    nc = _get("l0", build_l0)
    res = run_bass_kernel_spmd(nc, _l0_host_inputs(inputs), core_ids=cores)
    x3 = [np.asarray(r["x3_out"]) for r in res.results]
    xn1 = np.concatenate([np.asarray(r["xn_out"]) for r in res.results], 0)
    sc1 = np.concatenate([np.asarray(r["sc_out"]) for r in res.results], 1)

    nc = _get("moe", build_moe)
    res = run_bass_kernel_spmd(nc, _moe_host_inputs(inputs, xn1, sc1, 1),
                               core_ids=cores)
    y1 = np.concatenate([np.asarray(r["y_out"]) for r in res.results], 0)
    rkw1 = np.asarray(res.results[0]["rkw_out"])

    nc = _get("att1a", build_att1a)
    WkT = np.ascontiguousarray(np.asarray(inputs["L1_attn_Wk"], np.float32).T)
    WvT = np.ascontiguousarray(np.asarray(inputs["L1_attn_Wv"], np.float32).T)
    maps = []
    for j in cores:
        rkl, wl = _combine_host(rkw1, j)
        maps.append(dict(
            x3=x3[j], y1f=y1, rkl=rkl, wl=wl,
            wkl=_sbuf_chunks(WkT, DCH), wvl=_sbuf_chunks(WvT, DCH),
            bk=np.asarray(inputs["L1_attn_bk"], np.float32).reshape(1, D),
            bv=np.asarray(inputs["L1_attn_bv"], np.float32).reshape(1, D),
            ident=np.eye(128, dtype=np.float32),
        ))
    res = run_bass_kernel_spmd(nc, maps, core_ids=cores)
    x4 = [np.asarray(r["x4_out"]) for r in res.results]
    st = [np.asarray(r["st_out"]) for r in res.results]

    nc = _get("att1b", build_att1b)
    WT = lambda w: _sbuf_chunks(np.ascontiguousarray(
        np.asarray(inputs[w], np.float32).T), DCH)
    kk, tt = np.meshgrid(np.arange(128), np.arange(128), indexing="ij")
    ue = (kk <= tt).astype(np.float32)
    zst = np.zeros((1025, D), np.float32)
    maps = []
    for j in cores:
        maps.append(dict(
            x4=x4[j], stp=(st[j - 1] if j % 2 == 1 else zst),
            wql=WT("L1_attn_Wq"), wkl=WT("L1_attn_Wk"),
            wvl=WT("L1_attn_Wv"), wol=WT("L1_attn_Wo"),
            bq=np.asarray(inputs["L1_attn_bq"], np.float32).reshape(1, D),
            bk=np.asarray(inputs["L1_attn_bk"], np.float32).reshape(1, D),
            bv=np.asarray(inputs["L1_attn_bv"], np.float32).reshape(1, D),
            bo=np.asarray(inputs["L1_attn_bo"], np.float32).reshape(1, D),
            ue=ue, ident=np.eye(128, dtype=np.float32),
            wgl=_sbuf_chunks(np.ascontiguousarray(
                np.asarray(inputs["L1_ffn_Wg"], np.float32).T), DCH),
        ))
    res = run_bass_kernel_spmd(nc, maps, core_ids=cores)
    x5 = [np.asarray(r["x5_out"]) for r in res.results]
    xn2 = np.concatenate([np.asarray(r["xn2_out"]) for r in res.results], 0)
    sc2 = np.concatenate([np.asarray(r["sc2_out"]) for r in res.results], 1)

    nc = _get("moe", build_moe)
    res = run_bass_kernel_spmd(nc, _moe_host_inputs(inputs, xn2, sc2, 1),
                               core_ids=cores)
    y2 = np.concatenate([np.asarray(r["y_out"]) for r in res.results], 0)
    rkw2 = np.asarray(res.results[0]["rkw_out"])

    nc = _get("comb2", build_comb2)
    maps = []
    for j in cores:
        rkl, wl = _combine_host(rkw2, j)
        maps.append(dict(x5=x5[j], y1f=y2, rkl=rkl, wl=wl))
    res = run_bass_kernel_spmd(nc, maps, core_ids=cores)
    out = np.concatenate([np.asarray(r["xf_out"]) for r in res.results], 0)
    return out.reshape(4, 4096, D)
